# version 08
# baseline (speedup 1.0000x reference)
import math
from contextlib import ExitStack
from dataclasses import dataclass

import numpy as np
import ml_dtypes

import concourse.bass as bass
import concourse.mybir as mybir
from concourse.bass_utils import run_bass_kernel_spmd

F32 = mybir.dt.float32
BF16 = mybir.dt.bfloat16
AF = mybir.ActivationFunctionType
ALU = mybir.AluOpType
AX = mybir.AxisListType

P = 128
EPS = 1e-5


@dataclass(frozen=True)
class Cfg:
    D: int = 4096
    T: int = 1024
    I: int = 11008
    KV: int = 1024
    R: int = 4

    @property
    def DC(self):
        return self.D // P

    @property
    def IC(self):
        return self.I // P

    @property
    def NH(self):
        return self.D // P

    @property
    def NKV(self):
        return self.KV // P

    @property
    def TH(self):
        return self.T // 2

    @property
    def KEYS(self):
        return self.R * self.T

    @property
    def KTN(self):
        return self.KEYS // P

    @property
    def GR(self):
        return self.NH // self.NKV

    @property
    def isplits(self):
        n = self.IC
        a = (n + 2) // 3
        b = (n - a + 1) // 2
        return [(0, a), (a, a + b), (a + b, n)]


CFG = Cfg()


class PSem:

    def __init__(self, h, start):
        self.h = h
        self.n = start
        self.base = start

    def inc(self, instr, k=1):
        instr.then_inc(self.h, k)
        self.n += k
        return instr

    def at(self, v):
        return self.base + v


def build_program(cfg: Cfg, debug=False):
    nc = bass.Bass("TRN2", target_bir_lowering=False, detect_race_conditions=False)
    D, T, I, KV = cfg.D, cfg.T, cfg.I, cfg.KV
    DC, IC, NH, NKV, TH, R = cfg.DC, cfg.IC, cfg.NH, cfg.NKV, cfg.TH, cfg.R
    KTN, GR, KEYS = cfg.KTN, cfg.GR, cfg.KEYS
    TB = TH // P
    TN = T // P
    assert T == KV, "kernel assumes tokens-per-core == kv width"

    xT = nc.dram_tensor("xT", [D, T], F32, kind="ExternalInput").ap()
    wg_t = nc.dram_tensor("wg_t", [IC, P, DC, P], F32, kind="ExternalInput").ap()
    wu_t = nc.dram_tensor("wu_t", [IC, P, DC, P], F32, kind="ExternalInput").ap()
    wd_t = nc.dram_tensor("wd_t", [DC, P, IC, P], F32, kind="ExternalInput").ap()
    wq_t = nc.dram_tensor("wq_t", [NH, P, DC, P], F32, kind="ExternalInput").ap()
    wk_t = nc.dram_tensor("wk_t", [NKV, P, DC, P], F32, kind="ExternalInput").ap()
    wv_t = nc.dram_tensor("wv_t", [NKV, P, DC, P], F32, kind="ExternalInput").ap()
    wo_t = nc.dram_tensor("wo_t", [NH, P, DC, P], F32, kind="ExternalInput").ap()
    qsc = nc.dram_tensor("qsc", [P, NH], F32, kind="ExternalInput").ap()
    ksc = nc.dram_tensor("ksc", [P, NKV], F32, kind="ExternalInput").ap()
    vsc = nc.dram_tensor("vsc", [P, NKV], F32, kind="ExternalInput").ap()
    osc = nc.dram_tensor("osc", [P, NH], F32, kind="ExternalInput").ap()
    yT = nc.dram_tensor("yT", [D, T], F32, kind="ExternalOutput").ap()

    xc2 = nc.dram_tensor("xc2", [D, T], F32, kind="Internal").ap()
    xc3 = nc.dram_tensor("xc3", [D, T], F32, kind="Internal").ap()
    part = [nc.dram_tensor(f"part{i}", [D, T], F32, kind="Internal").ap() for i in range(3)]
    cc_in = nc.dram_tensor("cc_in", [2 * T, KV], BF16, kind="Internal")
    cc_out = nc.dram_tensor("cc_out", [2 * R * T, KV], BF16, kind="Internal")

    ident32 = nc.inline_tensor(np.eye(P, dtype=np.float32), name="ident32")
    identbf = nc.inline_tensor(np.eye(P, dtype=ml_dtypes.bfloat16), name="identbf")
    ones_r_bf = nc.inline_tensor(np.ones((1, P), dtype=ml_dtypes.bfloat16), name="ones_r_bf")
    ones_c_bf = nc.inline_tensor(np.ones((P, 1), dtype=ml_dtypes.bfloat16), name="ones_c_bf")
    ones_r_32 = nc.inline_tensor(np.ones((1, P), dtype=np.float32), name="ones_r_32")
    neg1_r_bf = nc.inline_tensor(np.full((1, P), -1.0, dtype=ml_dtypes.bfloat16), name="neg1_r_bf")

    WSL = max(DC * P, KEYS)
    HMX = max(NH, max(b - a for a, b in cfg.isplits))

    es = ExitStack()
    with es:
        xln = es.enter_context(nc.sbuf_tensor("xln", [P, DC * T], BF16))
        big = es.enter_context(nc.sbuf_tensor("big", [P, HMX * T], BF16))
        aw = es.enter_context(nc.sbuf_tensor("aw", [P, 4 * WSL], BF16))
        a32 = es.enter_context(nc.sbuf_tensor("a32", [P, 5 * T], F32))
        AMB = max(DC * P, 8 * TH)
        amisc = es.enter_context(nc.sbuf_tensor("amisc", [P, AMB + 2 * T], BF16))
        rows_mu = es.enter_context(nc.sbuf_tensor("rows_mu", [1, T], F32))
        rowbf = es.enter_context(nc.sbuf_tensor("rowbf", [1, 8 * TH], BF16))
        m4sb = es.enter_context(nc.sbuf_tensor("m4sb", [P, 8], F32))
        seff = es.enter_context(nc.sbuf_tensor("seff", [P, 2], F32))
        scv = es.enter_context(nc.sbuf_tensor("scv", [P, 2 * NH + 2 * NKV], F32))
        idt32 = es.enter_context(nc.sbuf_tensor("idt32", [P, P], F32))
        idtbf = es.enter_context(nc.sbuf_tensor("idtbf", [P, P], BF16))
        o_r_bf = es.enter_context(nc.sbuf_tensor("o_r_bf", [1, P], BF16))
        o_c_bf = es.enter_context(nc.sbuf_tensor("o_c_bf", [P, 1], BF16))
        o_r_32 = es.enter_context(nc.sbuf_tensor("o_r_32", [1, P], F32))
        n1_r_bf = es.enter_context(nc.sbuf_tensor("n1_r_bf", [1, P], BF16))
        cnst = es.enter_context(nc.sbuf_tensor("cnst", [1, 2], F32))

        pb = [es.enter_context(nc.psum_tensor(f"pb{i}", [P, TH], F32)) for i in range(7)]
        pbf = es.enter_context(nc.psum_tensor("pbf", [P, 2 * TH], BF16))

        _pools = {
            "gdma": [es.enter_context(nc.semaphore(f"spg{i}")) for i in range(3)],
            "sdma": [es.enter_context(nc.semaphore(f"sps{i}")) for i in range(6)],
            "eng": [es.enter_context(nc.semaphore(f"spe{i}")) for i in range(16)],
        }
        _pool_ps = {}
        _pool_i = {"gdma": 0, "sdma": 0, "eng": 0}

        def S(name):
            if name.endswith("wd"):
                kind = "gdma"
            elif name.endswith(("ld", "st", "kv")):
                kind = "sdma"
            else:
                kind = "eng"
            pool = _pools[kind]
            i = _pool_i[kind] % len(pool)
            _pool_i[kind] += 1
            h = pool[i]
            prev = _pool_ps.get((kind, i))
            start = prev.n if prev is not None else 0
            ps = PSem(h, start)
            _pool_ps[(kind, i)] = ps
            return ps

        def wslot(i, width=None):
            width = WSL if width is None else width
            return aw[:, i * WSL:i * WSL + width]

        def s32(i, th=None):
            if th is None:
                return a32[:, i * T:(i + 1) * T]
            return a32[:, i * T + th * TH: i * T + (th + 1) * TH]

        rows_sc = a32[0:1, 1 * T:2 * T]
        rows_rs = a32[0:1, 2 * T:3 * T]

        def xlnc(c, th=None):
            if th is None:
                return xln[:, c * T:(c + 1) * T]
            return xln[:, c * T + th * TH: c * T + th * TH + TH]

        def bigc(c, th=None):
            if th is None:
                return big[:, c * T:(c + 1) * T]
            return big[:, c * T + th * TH: c * T + th * TH + TH]

        absb = amisc[:, 0:DC * P]
        kst = amisc[:, AMB:AMB + T]
        vst = amisc[:, AMB + T:AMB + 2 * T]
        vst2 = kst

        def sqv(i):
            return amisc[:, i * T:(i + 1) * T]

        def psb_tile(i):
            return amisc[:, AMB + i * TH:AMB + (i + 1) * TH]

        def ring(u):
            i = u % 8
            return amisc[:, i * TH:(i + 1) * TH]

        def sum_row(th, which):
            bank = {(0, 0): pb[4], (0, 1): pb[5], (1, 0): pb[6], (1, 1): pb[3]}[(th, which)]
            return bank[0:1, :]

        with nc.Block("init") as blk:
            s_ld = S("i_ld")

            @blk.sync
            def _(sync):
                for dst, src in [
                    (idt32[:, :], ident32.ap()), (idtbf[:, :], identbf.ap()),
                    (o_r_bf[:, :], ones_r_bf.ap()), (o_c_bf[:, :], ones_c_bf.ap()),
                    (o_r_32[:, :], ones_r_32.ap()),
                    (n1_r_bf[:, :], neg1_r_bf.ap()),
                    (scv[:, 0:NH], qsc), (scv[:, NH:NH + NKV], ksc),
                    (scv[:, NH + NKV:NH + 2 * NKV], vsc),
                    (scv[:, NH + 2 * NKV:2 * NH + 2 * NKV], osc),
                ]:
                    s_ld.inc(sync.dma_start(out=dst, in_=src), 16)
                sync.wait_ge(s_ld.h, s_ld.at(16 * 9))

            @blk.vector
            def _(vector):
                vector.memset(cnst[0:1, 0:1], EPS)
                vector.memset(cnst[0:1, 1:2], -0.5)

        def ln_phase(name, srcs, write_to):
            ns = len(srcs)
            with nc.Block(name) as blk:
                s_ld, s_st, s_ve, s_ac, s_mm = (S(f"{name}ld"), S(f"{name}st"), S(f"{name}ve"),
                                                S(f"{name}ac"), S(f"{name}mm"))
                vp = max(ns - 1, 0)
                xs = 4 if ns > 1 else 0

                @blk.sync
                def _(sync):
                    for c in range(DC):
                        if c >= 1:
                            sync.wait_ge(s_ac.h, s_ac.at(2 * c))
                            if vp:
                                sync.wait_ge(s_ve.h, s_ve.at(vp * c))
                        for j, src in enumerate(srcs):
                            s_ld.inc(sync.dma_start(out=s32(j), in_=src[c * P:(c + 1) * P, :]), 16)
                        if write_to is not None:
                            sync.wait_ge(s_ve.h, s_ve.at(vp * (c + 1)))
                            s_st.inc(sync.dma_start(out=write_to[c * P:(c + 1) * P, :], in_=s32(xs)), 16)
                    if write_to is not None:
                        sync.wait_ge(s_st.h, s_st.at(16 * DC))

                if vp:
                    @blk.vector
                    def _(vector):
                        for c in range(DC):
                            vector.wait_ge(s_ld.h, s_ld.at(16 * ns * (c + 1)))
                            if c >= 1:
                                vector.wait_ge(s_ac.h, s_ac.at(2 * c))
                                if write_to is not None:
                                    vector.wait_ge(s_st.h, s_st.at(16 * c))
                            s_ve.inc(vector.tensor_add(out=s32(xs), in0=s32(0), in1=s32(1)), 1)
                            for j in range(2, ns):
                                s_ve.inc(vector.tensor_add(out=s32(xs), in0=s32(xs), in1=s32(j)), 1)

                @blk.scalar
                def _(scalar):
                    for c in range(DC):
                        if vp:
                            scalar.wait_ge(s_ve.h, s_ve.at(vp * (c + 1)))
                        else:
                            scalar.wait_ge(s_ld.h, s_ld.at(16 * ns * (c + 1)))
                        if c >= 2:
                            scalar.wait_ge(s_mm.h, s_mm.at(4 * (c - 1)))
                        s_ac.inc(scalar.activation(out=xlnc(c), in_=s32(xs), func=AF.Copy), 1)
                        s_ac.inc(scalar.activation(out=sqv(c % 2), in_=s32(xs), func=AF.Square), 1)

                @blk.tensor
                def _(tensor):
                    for c in range(DC):
                        tensor.wait_ge(s_ac.h, s_ac.at(2 * (c + 1)))
                        for th in range(2):
                            s_mm.inc(tensor.matmul(sum_row(th, 0), o_c_bf[:, :], xlnc(c, th),
                                          start=(c == 0), stop=(c == DC - 1)), 1)
                            s_mm.inc(tensor.matmul(sum_row(th, 1), o_c_bf[:, :],
                                          sqv(c % 2)[:, th * TH:th * TH + TH],
                                          start=(c == 0), stop=(c == DC - 1)), 1)

            ln_fin(name + "f")

        def ln_fin(name):
            with nc.Block(name) as blk:
                s_ve, s_ac, s_mm = S(f"{name}ve"), S(f"{name}ac"), S(f"{name}mm")

                @blk.vector
                def _(vector):
                    for th in range(2):
                        mu = rows_mu[0:1, th * TH:(th + 1) * TH]
                        sc = rows_sc[0:1, th * TH:(th + 1) * TH]
                        rs = rows_rs[0:1, th * TH:(th + 1) * TH]
                        s_ve.inc(vector.tensor_scalar(out=mu, in0=sum_row(th, 0), scalar1=1.0 / D,
                                             scalar2=None, op0=ALU.mult), 1)
                        s_ve.inc(vector.tensor_scalar(out=sc, in0=sum_row(th, 1), scalar1=1.0 / D,
                                             scalar2=None, op0=ALU.mult), 1)
                        s_ve.inc(vector.tensor_mul(out=rs, in0=mu, in1=mu), 1)
                        s_ve.inc(vector.tensor_sub(out=sc, in0=sc, in1=rs), 1)
                    vector.wait_ge(s_mm.h, s_mm.at(4))
                    for c in range(DC):
                        for th in range(2):
                            mb = pb[th]
                            rb = pb[2 + th]
                            t1 = s32(0, th)
                            s_ve.inc(vector.tensor_sub(out=t1, in0=xlnc(c, th), in1=mb[:, :]), 1)
                            s_ve.inc(vector.tensor_mul(out=xlnc(c, th), in0=t1, in1=rb[:, :]), 1)

                @blk.scalar
                def _(scalar):
                    for th in range(2):
                        scalar.wait_ge(s_ve.h, s_ve.at(4 * (th + 1)))
                        sc = rows_sc[0:1, th * TH:(th + 1) * TH]
                        rs = rows_rs[0:1, th * TH:(th + 1) * TH]
                        s_ac.inc(scalar.activation(out=sc, in_=sc, func=AF.Ln, bias=cnst[0:1, 0:1]), 1)
                        s_ac.inc(scalar.activation(out=rs, in_=sc, func=AF.Exp, scale=cnst[0:1, 1:2]), 1)

                @blk.tensor
                def _(tensor):
                    for th in range(2):
                        tensor.wait_ge(s_ac.h, s_ac.at(2 * (th + 1)))
                        mu = rows_mu[0:1, th * TH:(th + 1) * TH]
                        rs = rows_rs[0:1, th * TH:(th + 1) * TH]
                        s_mm.inc(tensor.matmul(pb[th][:, :], o_r_32[:, :], mu, start=True, stop=True), 1)
                        s_mm.inc(tensor.matmul(pb[2 + th][:, :], o_r_32[:, :], rs, start=True, stop=True), 1)

        def ffn_phase(name):
            for t, (c0, c1) in enumerate(cfg.isplits):
                ncnk = c1 - c0
                with nc.Block(f"{name}g{t}") as blk:
                    s_wd, s_mm, s_ac, s_ve = (S(f"{name}g{t}wd"), S(f"{name}g{t}mm"),
                                              S(f"{name}g{t}ac"), S(f"{name}g{t}ve"))

                    @blk.gpsimd
                    def _(gpsimd):
                        for s in range(ncnk):
                            if s >= 2:
                                gpsimd.wait_ge(s_mm.h, s_mm.at(s - 1))
                            gs = c0 + s
                            s_wd.inc(gpsimd.dma_start(
                                out=wslot(s % 2, DC * P).rearrange("p (c j) -> p c j", j=P),
                                in_=wg_t[gs]), 16)
                            s_wd.inc(gpsimd.dma_start(
                                out=wslot(2 + s % 2, DC * P).rearrange("p (c j) -> p c j", j=P),
                                in_=wu_t[gs]), 16)

                    @blk.tensor
                    def _(tensor):
                        for s in range(ncnk):
                            tensor.wait_ge(s_wd.h, s_wd.at(32 * (s + 1)))
                            if s >= 1:
                                tensor.wait_ge(s_ac.h, s_ac.at(s))
                                tensor.wait_ge(s_ve.h, s_ve.at(s))
                            g = wslot(s % 2, DC * P)
                            u = wslot(2 + s % 2, DC * P)
                            for c in range(DC):
                                for th in range(2):
                                    tensor.matmul(pb[th][:, :], g[:, c * P:(c + 1) * P], xlnc(c, th),
                                                  start=(c == 0), stop=(c == DC - 1))
                                    m = tensor.matmul(pb[2 + th][:, :], u[:, c * P:(c + 1) * P], xlnc(c, th),
                                                      start=(c == 0), stop=(c == DC - 1))
                            s_mm.inc(m, 1)

                    @blk.scalar
                    def _(scalar):
                        for s in range(ncnk):
                            scalar.wait_ge(s_mm.h, s_mm.at(s + 1))
                            if s >= 1:
                                scalar.wait_ge(s_ve.h, s_ve.at(s))
                            for th in range(2):
                                a = scalar.activation(out=s32(0, th), in_=pb[th][:, :], func=AF.Silu)
                            s_ac.inc(a, 1)

                    @blk.vector
                    def _(vector):
                        for s in range(ncnk):
                            vector.wait_ge(s_ac.h, s_ac.at(s + 1))
                            for th in range(2):
                                v = vector.tensor_mul(out=bigc(s, th), in0=s32(0, th), in1=pb[2 + th][:, :])
                            s_ve.inc(v, 1)

                with nc.Block(f"{name}d{t}") as blk:
                    s_wd, s_mm, s_ac, s_st = (S(f"{name}d{t}wd"), S(f"{name}d{t}mm"),
                                              S(f"{name}d{t}ac"), S(f"{name}d{t}st"))

                    @blk.gpsimd
                    def _(gpsimd):
                        for o in range(DC):
                            if o >= 2:
                                gpsimd.wait_ge(s_mm.h, s_mm.at(o - 1))
                            s_wd.inc(gpsimd.dma_start(
                                out=wslot(o % 2, ncnk * P).rearrange("p (c j) -> p c j", j=P),
                                in_=wd_t[o][:, c0:c1, :]), 16)

                    @blk.tensor
                    def _(tensor):
                        for o in range(DC):
                            tensor.wait_ge(s_wd.h, s_wd.at(16 * (o + 1)))
                            if o >= 1:
                                tensor.wait_ge(s_ac.h, s_ac.at(o))
                            w = wslot(o % 2, ncnk * P)
                            for c in range(ncnk):
                                for th in range(2):
                                    m = tensor.matmul(pb[4 + th][:, :], w[:, c * P:(c + 1) * P], bigc(c, th),
                                                      start=(c == 0), stop=(c == ncnk - 1))
                            s_mm.inc(m, 1)

                    @blk.scalar
                    def _(scalar):
                        for o in range(DC):
                            scalar.wait_ge(s_mm.h, s_mm.at(o + 1))
                            if o >= 2:
                                scalar.wait_ge(s_st.h, s_st.at(16 * (o - 1)))
                            for th in range(2):
                                a = scalar.activation(out=s32(2 + o % 2, th), in_=pb[4 + th][:, :],
                                                      func=AF.Copy, scale=0.5)
                            s_ac.inc(a, 1)

                    @blk.sync
                    def _(sync):
                        for o in range(DC):
                            sync.wait_ge(s_ac.h, s_ac.at(o + 1))
                            s_st.inc(sync.dma_start(out=part[t][o * P:(o + 1) * P, :],
                                           in_=s32(2 + o % 2)), 16)
                        sync.wait_ge(s_st.h, s_st.at(16 * DC))

        def bitnet_proj(name, w_ap, n_o, scol_off, extra_scale, dest,
                        kv_store=None, v_transpose=False, pre_gpsimd=None):
            with nc.Block(name) as blk:
                s_wd, s_ab, s_sg = S(f"{name}wd"), S(f"{name}ab"), S(f"{name}sg")
                s_sm, s_cl, s_tp = S(f"{name}sm"), S(f"{name}cl"), S(f"{name}tp")
                s_se, s_pm, s_ev = S(f"{name}se"), S(f"{name}pm"), S(f"{name}ev")
                s_st, s_vt, s_vc = S(f"{name}st"), S(f"{name}vt"), S(f"{name}vc")

                @blk.gpsimd
                def _(gpsimd):
                    if pre_gpsimd is not None:
                        pre_gpsimd(gpsimd)
                    for o in range(n_o):
                        if o >= 2:
                            gpsimd.wait_ge(s_pm.h, s_pm.at(o - 1))
                        s_wd.inc(gpsimd.dma_start(
                            out=wslot(o % 2, DC * P).rearrange("p (c j) -> p c j", j=P),
                            in_=w_ap[o]), 16)

                @blk.scalar
                def _(scalar):
                    for o in range(n_o):
                        scalar.wait_ge(s_wd.h, s_wd.at(16 * (o + 1)))
                        if o >= 1:
                            scalar.wait_ge(s_sm.h, s_sm.at(o))
                        w = wslot(o % 2, DC * P)
                        s_ab.inc(scalar.activation(out=absb, in_=w, func=AF.Abs), 1)
                        s_sg.inc(scalar.activation(out=w, in_=w, func=AF.Sign), 1)
                        scalar.wait_ge(s_pm.h, s_pm.at(o + 1))
                        scalar.wait_ge(s_se.h, s_se.at(o + 1))
                        if kv_store is not None and o >= 2:
                            scalar.wait_ge(s_st.h, s_st.at(16 * (o - 1)))
                        sc = seff[:, o % 2:o % 2 + 1]
                        for th in range(2):
                            if kv_store is None:
                                dst = dest(o, th)
                            elif v_transpose:
                                dst = vst[:, th * TH:(th + 1) * TH]
                            else:
                                dst = kst[:, th * TH:(th + 1) * TH]
                            a = scalar.activation(out=dst, in_=pb[th][:, :], func=AF.Copy, scale=sc)
                        s_ev.inc(a, 1)
                        if v_transpose:
                            scalar.wait_ge(s_vt.h, s_vt.at(o + 1))
                            if o >= 1:
                                scalar.wait_ge(s_st.h, s_st.at(16 * o))
                            s_vc.inc(scalar.activation(out=vst2, in_=pbf[:, 0:T], func=AF.Copy), 1)

                @blk.tensor
                def _(tensor):
                    for o in range(n_o):
                        tensor.wait_ge(s_ab.h, s_ab.at(o + 1))
                        if o >= 1:
                            tensor.wait_ge(s_cl.h, s_cl.at(o))
                        for c in range(DC):
                            m = tensor.matmul(pb[6][0:1, 0:P], o_c_bf[:, :], absb[:, c * P:(c + 1) * P],
                                              start=(c == 0), stop=(c == DC - 1))
                        s_sm.inc(m, 1)
                        tensor.wait_ge(s_cl.h, s_cl.at(o + 1))
                        if o >= 1:
                            tensor.wait_ge(s_se.h, s_se.at(o))
                        s_tp.inc(tensor.transpose(pb[6][:, 4:5], rows_sc[0:1, 0:P], idt32[0:1, 0:1]), 1)
                        tensor.wait_ge(s_sg.h, s_sg.at(o + 1))
                        if o >= 1:
                            tensor.wait_ge(s_ev.h, s_ev.at(o))
                        w = wslot(o % 2, DC * P)
                        for c in range(DC):
                            for th in range(2):
                                m = tensor.matmul(pb[th][:, :], w[:, c * P:(c + 1) * P], xlnc(c, th),
                                                  start=(c == 0), stop=(c == DC - 1))
                        s_pm.inc(m, 1)
                        if v_transpose:
                            tensor.wait_ge(s_ev.h, s_ev.at(o + 1))
                            if o >= 1:
                                tensor.wait_ge(s_vc.h, s_vc.at(o))
                            for tt in range(TN):
                                m = tensor.transpose(pbf[:, tt * P:(tt + 1) * P],
                                                     vst[:, tt * P:(tt + 1) * P],
                                                     idtbf[:, :])
                            s_vt.inc(m, 1)

                @blk.vector
                def _(vector):
                    for o in range(n_o):
                        vector.wait_ge(s_sm.h, s_sm.at(o + 1))
                        s_cl.inc(vector.tensor_scalar(out=rows_sc[0:1, 0:P], in0=pb[6][0:1, 0:P],
                                             scalar1=1.0 / D, scalar2=1e-8,
                                             op0=ALU.mult, op1=ALU.max), 1)
                        vector.wait_ge(s_tp.h, s_tp.at(o + 1))
                        if o >= 1:
                            vector.wait_ge(s_ev.h, s_ev.at(o))
                        s_se.inc(vector.scalar_tensor_tensor(out=seff[:, o % 2:o % 2 + 1], in0=pb[6][:, 4:5],
                                                    scalar=extra_scale,
                                                    in1=scv[:, scol_off + o:scol_off + o + 1],
                                                    op0=ALU.mult, op1=ALU.mult), 1)

                if kv_store is not None:
                    @blk.sync
                    def _(sync):
                        for o in range(n_o):
                            if not v_transpose:
                                sync.wait_ge(s_ev.h, s_ev.at(o + 1))
                                s_st.inc(sync.dma_start(out=cc_in.ap()[o * P:(o + 1) * P, :],
                                               in_=kst), 16)
                            else:
                                sync.wait_ge(s_vc.h, s_vc.at(o + 1))
                                s_st.inc(sync.dma_start(
                                    out=cc_in.ap()[T:2 * T, o * P:(o + 1) * P]
                                    .rearrange("(tt p) j -> p tt j", p=P),
                                    in_=vst2.rearrange("p (tt j) -> p tt j", j=P)), 16)
                        sync.wait_ge(s_st.h, s_st.at(16 * n_o))

        def attn_phase(name, s_cc_handle):
            with nc.Block(name) as blk:
                s_kv, s_rk, s_mr = S(f"{name}kv"), S(f"{name}rk"), S(f"{name}mr")
                s_nk, s_sq = S(f"{name}nk"), S(f"{name}sq")
                s_sb, s_ex, s_av = S(f"{name}sb"), S(f"{name}ex"), S(f"{name}av")
                s_tn, s_rc, s_ib = S(f"{name}tn"), S(f"{name}rc"), S(f"{name}ib")
                s_iv, s_ao = S(f"{name}iv"), S(f"{name}ao")
                UPH = GR * 2
                NU = NKV * UPH
                NKB = KEYS // TH
                NPH = NKB + UPH

                def unit_of(h, uh):
                    u = h * UPH + uh
                    return u, h * GR + uh // 2, uh % 2

                def mrow(u):
                    return rowbf[0:1, (u % 8) * TH:(u % 8) * TH + TH]

                def nrow_bank(i):
                    return (pb[5] if i % 2 == 0 else pb[6])[0:1, :]

                sqK = amisc[:, 0:KEYS]
                sqQ = amisc[:, AMB + T:AMB + 2 * T]
                rkp = a32[0:1, 0:8]
                rk2 = a32[0:1, 8:9]
                lnt = a32[0:1, T:T + TH]

                def invb(u):
                    return s32(2, u % 2)

                def mrow_inv(u):
                    return rows_mu[0:1, (u % 2) * TH:(u % 2) * TH + TH]

                @blk.sync
                def _(sync):
                    sync.wait_ge(s_cc_handle.h, s_cc_handle.at(1))
                    for h in range(NKV):
                        if h >= 2:
                            sync.wait_ge(s_tn.h, s_tn.at(UPH * (h - 1)))
                        kslot = wslot(h % 2, KEYS)
                        vslot = wslot(2 + h % 2, KEYS)
                        for r in range(R):
                            s_kv.inc(sync.dma_start(
                                out=kslot[:, r * T:(r + 1) * T],
                                in_=cc_out.ap()[r * 2 * T + h * P: r * 2 * T + h * P + P, :]), 16)
                            s_kv.inc(sync.dma_start(
                                out=vslot.rearrange("p (kt j) -> p kt j", j=P)[:, r * TN:(r + 1) * TN, :],
                                in_=cc_out.ap()[r * 2 * T + T: r * 2 * T + 2 * T, h * P:(h + 1) * P]
                                .rearrange("(kt p) j -> p kt j", p=P)), 16)

                @blk.scalar
                def _(scalar):
                    for h in range(NKV):
                        scalar.wait_ge(s_kv.h, s_kv.at(16 * 2 * R * (h + 1)))
                        if h >= 1:
                            scalar.wait_ge(s_nk.h, s_nk.at(NPH * h - UPH))
                        s_sq.inc(scalar.activation(out=sqK, in_=wslot(h % 2, KEYS), func=AF.Square), 1)
                        for uh in range(UPH):
                            u, g, qt = unit_of(h, uh)
                            if uh % 2 == 0:
                                if u >= 1:
                                    scalar.wait_ge(s_nk.h, s_nk.at(NKB * (h + 1) + UPH * h + uh))
                                s_sq.inc(scalar.activation(out=sqQ[:, 0:T], in_=bigc(g), func=AF.Square), 1)
                            scalar.wait_ge(s_nk.h, s_nk.at(NKB * (h + 1) + UPH * h + uh + 1))
                            scalar.wait_ge(s_rk.h, s_rk.at((NKB + 1) * (h + 1)))
                            scalar.activation(out=lnt, in_=nrow_bank(NKB + uh), func=AF.Ln,
                                              scale=rk2)
                            s_mr.inc(scalar.activation(out=mrow(u), in_=lnt, func=AF.Exp,
                                                       scale=0.5), 1)
                        for uh in range(UPH):
                            u, g, qt = unit_of(h, uh)
                            for kt in range(KTN):
                                gi = u * KTN + kt
                                scalar.wait_ge(s_sb.h, s_sb.at(gi + 1))
                                if gi >= 2:
                                    scalar.wait_ge(s_av.h, s_av.at(gi - 1))
                                s_ex.inc(scalar.activation(out=psb_tile(kt % 2),
                                                           in_=pb[2 + kt % 2][:, :], func=AF.Exp), 1)
                            scalar.wait_ge(s_ib.h, s_ib.at(u + 1))
                            if u >= 2:
                                scalar.wait_ge(s_ao.h, s_ao.at(u - 1))
                            s_iv.inc(scalar.activation(out=invb(u), in_=pb[5][:, :],
                                                       func=AF.Copy), 1)

                @blk.tensor
                def _(tensor):
                    for h in range(NKV):
                        kslot = wslot(h % 2, KEYS)
                        vslot = wslot(2 + h % 2, KEYS)
                        tensor.wait_ge(s_sq.h, s_sq.at((1 + GR) * h + 1))
                        if h >= 1:
                            tensor.wait_ge(s_iv.h, s_iv.at(UPH * h))
                        for kb in range(NKB):
                            i = NPH * h + kb
                            if kb >= 2:
                                tensor.wait_ge(s_rk.h, s_rk.at((NKB + 1) * h + kb - 1))
                            s_nk.inc(tensor.matmul(nrow_bank(kb), o_c_bf[:, :],
                                                   sqK[:, kb * TH:(kb + 1) * TH],
                                                   start=True, stop=True), 1)
                        for uh in range(UPH):
                            u, g, qt = unit_of(h, uh)
                            if uh % 2 == 0:
                                tensor.wait_ge(s_sq.h, s_sq.at((1 + GR) * h + 2 + uh // 2))
                            if uh < 2:
                                tensor.wait_ge(s_rk.h, s_rk.at((NKB + 1) * h + NKB - 1 + uh))
                            else:
                                tensor.wait_ge(s_mr.h, s_mr.at(u - 1))
                            s_nk.inc(tensor.matmul(nrow_bank(NKB + uh), o_c_bf[:, :],
                                                   sqQ[:, qt * TH:(qt + 1) * TH],
                                                   start=True, stop=True), 1)
                        for uh in range(UPH):
                            u, g, qt = unit_of(h, uh)
                            rhs_q = bigc(g, qt)
                            for kt in range(KTN):
                                gi = u * KTN + kt
                                if gi >= 2:
                                    tensor.wait_ge(s_ex.h, s_ex.at(gi - 1))
                                if kt == 0:
                                    tensor.wait_ge(s_mr.h, s_mr.at(UPH * (h + 1)))
                                    tensor.wait_ge(s_ao.h, s_ao.at(u))
                                    tensor.wait_ge(s_iv.h, s_iv.at(u))
                                tensor.matmul(pb[2 + kt % 2][:, :], n1_r_bf[:, :], mrow(u),
                                              start=True, stop=False)
                                s_sb.inc(tensor.matmul(pb[2 + kt % 2][:, :], kslot[:, kt * P:(kt + 1) * P],
                                                       rhs_q, start=False, stop=True), 1)
                                tensor.wait_ge(s_ex.h, s_ex.at(gi + 1))
                                s_av.inc(tensor.matmul(pb[4][:, :], vslot[:, kt * P:(kt + 1) * P],
                                                       psb_tile(kt % 2), start=(kt == 0),
                                                       stop=(kt == KTN - 1)), 1)
                                m = tensor.matmul(pb[5][0:1, :], o_c_bf[:, :], psb_tile(kt % 2),
                                                  start=(kt == 0), stop=(kt == KTN - 1))
                                if kt == KTN - 1:
                                    s_tn.inc(m, 1)
                            tensor.wait_ge(s_rc.h, s_rc.at(u + 1))
                            s_ib.inc(tensor.matmul(pb[5][:, :], o_r_32[:, :],
                                                   mrow_inv(u), start=True, stop=True), 1)

                @blk.vector
                def _(vector):
                    for h in range(NKV):
                        for kb in range(NKB):
                            vector.wait_ge(s_nk.h, s_nk.at(NPH * h + kb + 1))
                            s_rk.inc(vector.reduce_max(out=rkp[0:1, kb:kb + 1],
                                                       in_=nrow_bank(kb), axis=AX.X), 1)
                        s_rk.inc(vector.reduce_max(out=rk2, in_=rkp, axis=AX.X), 1)
                        for uh in range(UPH):
                            u, g, qt = unit_of(h, uh)
                            vector.wait_ge(s_tn.h, s_tn.at(u + 1))
                            if u >= 1:
                                vector.wait_ge(s_ib.h, s_ib.at(u - 1))
                            s_rc.inc(vector.reciprocal(out=mrow_inv(u), in_=pb[5][0:1, :]), 1)
                            vector.wait_ge(s_iv.h, s_iv.at(u + 1))
                            s_ao.inc(vector.tensor_mul(out=bigc(g, qt), in0=pb[4][:, :],
                                                       in1=invb(u)), 1)

        def oproj_phase(name):
            with nc.Block(name) as blk:
                s_wd, s_ab, s_sg = S(f"{name}wd"), S(f"{name}ab"), S(f"{name}sg")
                s_sm, s_cl, s_tp = S(f"{name}sm"), S(f"{name}cl"), S(f"{name}tp")
                s_se, s_pm, s_ev = S(f"{name}se"), S(f"{name}pm"), S(f"{name}ev")
                s_ld, s_ve, s_st = S(f"{name}ld"), S(f"{name}ve"), S(f"{name}st")
                s_ac2, s_mm2 = S(f"{name}ac2"), S(f"{name}mm2")

                @blk.gpsimd
                def _(gpsimd):
                    for o in range(DC):
                        if o >= 2:
                            gpsimd.wait_ge(s_pm.h, s_pm.at(o - 1))
                        s_wd.inc(gpsimd.dma_start(
                            out=wslot(o % 2, DC * P).rearrange("p (c j) -> p c j", j=P),
                            in_=wo_t[o]), 16)

                @blk.scalar
                def _(scalar):
                    for o in range(DC):
                        scalar.wait_ge(s_wd.h, s_wd.at(16 * (o + 1)))
                        if o >= 1:
                            scalar.wait_ge(s_sm.h, s_sm.at(o))
                        w = wslot(o % 2, DC * P)
                        s_ab.inc(scalar.activation(out=absb, in_=w, func=AF.Abs), 1)
                        s_sg.inc(scalar.activation(out=w, in_=w, func=AF.Sign), 1)
                        scalar.wait_ge(s_pm.h, s_pm.at(o + 1))
                        scalar.wait_ge(s_se.h, s_se.at(o + 1))
                        if o >= 1:
                            scalar.wait_ge(s_ve.h, s_ve.at(2 * o))
                        sc = seff[:, o % 2:o % 2 + 1]
                        for th in range(2):
                            a = scalar.activation(out=s32(2, th), in_=pb[th][:, :], func=AF.Copy, scale=sc)
                        s_ev.inc(a, 1)
                        scalar.wait_ge(s_ve.h, s_ve.at(2 * (o + 1)))
                        if o >= 2:
                            scalar.wait_ge(s_mm2.h, s_mm2.at(4 * (o - 1)))
                        s_ac2.inc(scalar.activation(out=xlnc(o), in_=s32(4), func=AF.Copy), 1)
                        s_ac2.inc(scalar.activation(out=(kst if o % 2 == 0 else vst),
                                                    in_=s32(4), func=AF.Square), 1)

                @blk.tensor
                def _(tensor):
                    for o in range(DC):
                        tensor.wait_ge(s_ab.h, s_ab.at(o + 1))
                        if o >= 1:
                            tensor.wait_ge(s_cl.h, s_cl.at(o))
                        for c in range(DC):
                            m = tensor.matmul(pb[2][0:1, 0:P], o_c_bf[:, :], absb[:, c * P:(c + 1) * P],
                                              start=(c == 0), stop=(c == DC - 1))
                        s_sm.inc(m, 1)
                        tensor.wait_ge(s_cl.h, s_cl.at(o + 1))
                        if o >= 1:
                            tensor.wait_ge(s_se.h, s_se.at(o))
                        s_tp.inc(tensor.transpose(pb[2][:, 4:5], rows_sc[0:1, 0:P], idt32[0:1, 0:1]), 1)
                        tensor.wait_ge(s_sg.h, s_sg.at(o + 1))
                        if o >= 1:
                            tensor.wait_ge(s_ev.h, s_ev.at(o))
                        w = wslot(o % 2, DC * P)
                        for c in range(DC):
                            for th in range(2):
                                m = tensor.matmul(pb[th][:, :], w[:, c * P:(c + 1) * P], bigc(c, th),
                                                  start=(c == 0), stop=(c == DC - 1))
                        s_pm.inc(m, 1)
                        tensor.wait_ge(s_ac2.h, s_ac2.at(2 * (o + 1)))
                        sq_t = kst if o % 2 == 0 else vst
                        for th in range(2):
                            s_mm2.inc(tensor.matmul(sum_row(th, 0), o_c_bf[:, :], xlnc(o, th),
                                          start=(o == 0), stop=(o == DC - 1)), 1)
                            s_mm2.inc(tensor.matmul(sum_row(th, 1), o_c_bf[:, :],
                                          sq_t[:, th * TH:th * TH + TH],
                                          start=(o == 0), stop=(o == DC - 1)), 1)

                @blk.vector
                def _(vector):
                    for o in range(DC):
                        vector.wait_ge(s_sm.h, s_sm.at(o + 1))
                        s_cl.inc(vector.tensor_scalar(out=rows_sc[0:1, 0:P], in0=pb[2][0:1, 0:P],
                                             scalar1=1.0 / D, scalar2=1e-8,
                                             op0=ALU.mult, op1=ALU.max), 1)
                        vector.wait_ge(s_tp.h, s_tp.at(o + 1))
                        if o >= 1:
                            vector.wait_ge(s_ev.h, s_ev.at(o))
                        s_se.inc(vector.scalar_tensor_tensor(out=seff[:, o % 2:o % 2 + 1], in0=pb[2][:, 4:5],
                                                    scalar=1.0,
                                                    in1=scv[:, NH + 2 * NKV + o:NH + 2 * NKV + o + 1],
                                                    op0=ALU.mult, op1=ALU.mult), 1)
                        vector.wait_ge(s_ev.h, s_ev.at(o + 1))
                        vector.wait_ge(s_ld.h, s_ld.at(16 * (o + 1)))
                        if o >= 1:
                            vector.wait_ge(s_st.h, s_st.at(16 * o))
                            vector.wait_ge(s_ac2.h, s_ac2.at(2 * o))
                        s_ve.inc(vector.tensor_add(out=s32(4), in0=s32(2), in1=s32(3)), 2)

                @blk.sync
                def _(sync):
                    for o in range(DC):
                        if o >= 1:
                            sync.wait_ge(s_ve.h, s_ve.at(2 * o))
                        s_ld.inc(sync.dma_start(out=s32(3), in_=xc2[o * P:(o + 1) * P, :]), 16)
                        sync.wait_ge(s_ve.h, s_ve.at(2 * (o + 1)))
                        s_st.inc(sync.dma_start(out=xc3[o * P:(o + 1) * P, :], in_=s32(4)), 16)
                    sync.wait_ge(s_st.h, s_st.at(16 * DC))

            ln_fin(name + "f")

        ln_phase("ln1", [xT], None)
        ffn_phase("f1")
        ln_phase("ln2", [xT, part[0], part[1], part[2]], xc2)

        s_cc = PSem(es.enter_context(nc.semaphore("cc")), 0)

        bitnet_proj("kp", wk_t, NKV, NH, 1.0, None, kv_store="k")
        bitnet_proj("vp", wv_t, NKV, NH + NKV, 1.0, None, kv_store="v", v_transpose=True)

        def q_pre(gpsimd):
            s_cc.inc(gpsimd.collective_compute(
                "AllGather", ALU.bypass,
                replica_groups=[[0, 1, 2, 3], [4, 5, 6, 7]],
                ins=[cc_in.ap().opt()],
                outs=[cc_out.ap().opt()],
            ), 1)

        bitnet_proj("qp", wq_t, NH, 0, 1.0 / math.sqrt(P), lambda o, th: bigc(o, th),
                    pre_gpsimd=q_pre)
        attn_phase("at", s_cc)
        oproj_phase("op")
        ffn_phase("f2")

        dbg = {}
        dbg_xln = None
        if debug:
            for nm, src, shp, dt in [
                ("dbg_xc2", xc2, [D, T], F32), ("dbg_xc3", xc3, [D, T], F32),
                ("dbg_p0", part[0], [D, T], F32), ("dbg_p1", part[1], [D, T], F32),
                ("dbg_p2", part[2], [D, T], F32),
                ("dbg_ccin", cc_in.ap(), [2 * T, KV], BF16),
                ("dbg_ccout", cc_out.ap(), [2 * R * T, KV], BF16),
            ]:
                dbg[nm] = (nc.dram_tensor(nm, shp, dt, kind="ExternalOutput").ap(), src)
            dbg_xln = nc.dram_tensor("dbg_xln", [P, DC * T], BF16, kind="ExternalOutput").ap()
            with nc.Block("dbg") as blk:
                s_dst = S("dbgst")

                @blk.sync
                def _(sync):
                    n = 0
                    for nm, (dst, src) in dbg.items():
                        s_dst.inc(sync.dma_start(out=dst, in_=src), 16)
                        n += 1
                    s_dst.inc(sync.dma_start(out=dbg_xln, in_=xln[:, :]), 16)
                    sync.wait_ge(s_dst.h, s_dst.at(16 * (n + 1)))

        with nc.Block("fin") as blk:
            s_ld, s_ve, s_st = S("finld"), S("finve"), S("finst")

            @blk.sync
            def _(sync):
                for c in range(DC):
                    if c >= 1:
                        sync.wait_ge(s_ve.h, s_ve.at(3 * c))
                    for j, src in enumerate([xc3, part[0], part[1], part[2]]):
                        s_ld.inc(sync.dma_start(out=s32(j), in_=src[c * P:(c + 1) * P, :]), 16)
                    sync.wait_ge(s_ve.h, s_ve.at(3 * (c + 1)))
                    s_st.inc(sync.dma_start(out=yT[c * P:(c + 1) * P, :], in_=s32(4)), 16)
                sync.wait_ge(s_st.h, s_st.at(16 * DC))

            @blk.vector
            def _(vector):
                for c in range(DC):
                    vector.wait_ge(s_ld.h, s_ld.at(16 * 4 * (c + 1)))
                    if c >= 1:
                        vector.wait_ge(s_st.h, s_st.at(16 * c))
                    s_ve.inc(vector.tensor_add(out=s32(4), in0=s32(0), in1=s32(1)), 1)
                    s_ve.inc(vector.tensor_add(out=s32(4), in0=s32(4), in1=s32(2)), 1)
                    s_ve.inc(vector.tensor_add(out=s32(4), in0=s32(4), in1=s32(3)), 1)

    return nc


def _tile_T(w):
    O, IC_ = w.shape
    return np.ascontiguousarray(w.T.reshape(IC_ // P, P, O // P, P).transpose(2, 1, 0, 3))


def prep_inputs(cfg: Cfg, inp):
    T, I, R = cfg.T, cfg.I, cfg.R
    x = inp["L0_attn_hidden_states"]
    wgu = inp["L0_ffn_w_gate_up"]

    shared = {
        "wg_t": _tile_T(wgu[:I]),
        "wu_t": _tile_T(wgu[I:]),
        "wd_t": _tile_T(inp["L0_ffn_w_down"]),
        "wq_t": _tile_T(inp["L0_attn_q_weight"]),
        "wk_t": _tile_T(inp["L0_attn_k_weight"]),
        "wv_t": _tile_T(inp["L0_attn_v_weight"]),
        "wo_t": _tile_T(inp["L0_attn_o_weight"]),
        "qsc": np.ascontiguousarray(inp["L0_attn_q_scale"].reshape(-1, P).T),
        "ksc": np.ascontiguousarray(inp["L0_attn_k_scale"].reshape(-1, P).T),
        "vsc": np.ascontiguousarray(inp["L0_attn_v_scale"].reshape(-1, P).T),
        "osc": np.ascontiguousarray(inp["L0_attn_o_scale"].reshape(-1, P).T),
    }
    in_maps = []
    for c in range(8):
        b, q = c // R, c % R
        m = dict(shared)
        m["xT"] = np.ascontiguousarray(x[b, q * T:(q + 1) * T, :].T)
        in_maps.append(m)
    return in_maps


def kernel(**inputs):
    cfg = CFG
    in_maps = prep_inputs(cfg, inputs)
    nc = build_program(cfg)
    res = run_bass_kernel_spmd(nc, in_maps, core_ids=list(range(8)))
    B = 2
    y = np.empty((B, cfg.R * cfg.T, cfg.D), np.float32)
    for c in range(8):
        b, q = c // cfg.R, c % cfg.R
        y[b, q * cfg.T:(q + 1) * cfg.T, :] = res.results[c]["yT"].T
    return y


# version 09
# speedup vs baseline: 1.1621x; 1.1621x over previous
import math
from contextlib import ExitStack
from dataclasses import dataclass

import numpy as np
import ml_dtypes

import concourse.bass as bass
import concourse.mybir as mybir
from concourse.bass_utils import run_bass_kernel_spmd

F32 = mybir.dt.float32
BF16 = mybir.dt.bfloat16
AF = mybir.ActivationFunctionType
ALU = mybir.AluOpType
AX = mybir.AxisListType

P = 128
EPS = 1e-5


@dataclass(frozen=True)
class Cfg:
    D: int = 4096
    T: int = 1024
    I: int = 11008
    KV: int = 1024
    R: int = 4

    @property
    def DC(self):
        return self.D // P

    @property
    def IC(self):
        return self.I // P

    @property
    def NH(self):
        return self.D // P

    @property
    def NKV(self):
        return self.KV // P

    @property
    def TH(self):
        return self.T // 2

    @property
    def KEYS(self):
        return self.R * self.T

    @property
    def KTN(self):
        return self.KEYS // P

    @property
    def GR(self):
        return self.NH // self.NKV

    @property
    def isplits(self):
        n = self.IC
        a = (n + 2) // 3
        b = (n - a + 1) // 2
        return [(0, a), (a, a + b), (a + b, n)]


CFG = Cfg()


class PSem:

    def __init__(self, h, start):
        self.h = h
        self.n = start
        self.base = start

    def inc(self, instr, k=1):
        instr.then_inc(self.h, k)
        self.n += k
        return instr

    def at(self, v):
        return self.base + v


def build_program(cfg: Cfg, debug=False):
    nc = bass.Bass("TRN2", target_bir_lowering=False, detect_race_conditions=False)
    D, T, I, KV = cfg.D, cfg.T, cfg.I, cfg.KV
    DC, IC, NH, NKV, TH, R = cfg.DC, cfg.IC, cfg.NH, cfg.NKV, cfg.TH, cfg.R
    KTN, GR, KEYS = cfg.KTN, cfg.GR, cfg.KEYS
    TB = TH // P
    TN = T // P
    assert T == KV, "kernel assumes tokens-per-core == kv width"

    xT = nc.dram_tensor("xT", [D, T], F32, kind="ExternalInput").ap()
    wg_t = nc.dram_tensor("wg_t", [IC, P, DC, P], F32, kind="ExternalInput").ap()
    wu_t = nc.dram_tensor("wu_t", [IC, P, DC, P], F32, kind="ExternalInput").ap()
    wd_t = nc.dram_tensor("wd_t", [DC, P, IC, P], F32, kind="ExternalInput").ap()
    wq_t = nc.dram_tensor("wq_t", [NH, P, DC, P], F32, kind="ExternalInput").ap()
    wk_t = nc.dram_tensor("wk_t", [NKV, P, DC, P], F32, kind="ExternalInput").ap()
    wv_t = nc.dram_tensor("wv_t", [NKV, P, DC, P], F32, kind="ExternalInput").ap()
    wo_t = nc.dram_tensor("wo_t", [NH, P, DC, P], F32, kind="ExternalInput").ap()
    qsc = nc.dram_tensor("qsc", [P, NH], F32, kind="ExternalInput").ap()
    ksc = nc.dram_tensor("ksc", [P, NKV], F32, kind="ExternalInput").ap()
    vsc = nc.dram_tensor("vsc", [P, NKV], F32, kind="ExternalInput").ap()
    osc = nc.dram_tensor("osc", [P, NH], F32, kind="ExternalInput").ap()
    yT = nc.dram_tensor("yT", [D, T], F32, kind="ExternalOutput").ap()

    xc2 = nc.dram_tensor("xc2", [D, T], F32, kind="Internal").ap()
    xc3 = nc.dram_tensor("xc3", [D, T], F32, kind="Internal").ap()
    part = [nc.dram_tensor(f"part{i}", [D, T], F32, kind="Internal").ap() for i in range(3)]
    cc_in = nc.dram_tensor("cc_in", [2 * T, KV], BF16, kind="Internal")
    cc_out = nc.dram_tensor("cc_out", [2 * R * T, KV], BF16, kind="Internal")

    ident32 = nc.inline_tensor(np.eye(P, dtype=np.float32), name="ident32")
    identbf = nc.inline_tensor(np.eye(P, dtype=ml_dtypes.bfloat16), name="identbf")
    ones_r_bf = nc.inline_tensor(np.ones((1, P), dtype=ml_dtypes.bfloat16), name="ones_r_bf")
    ones_c_bf = nc.inline_tensor(np.ones((P, 1), dtype=ml_dtypes.bfloat16), name="ones_c_bf")
    ones_r_32 = nc.inline_tensor(np.ones((1, P), dtype=np.float32), name="ones_r_32")
    neg1_r_bf = nc.inline_tensor(np.full((1, P), -1.0, dtype=ml_dtypes.bfloat16), name="neg1_r_bf")

    WSL = max(DC * P, KEYS)
    HMX = max(NH, max(b - a for a, b in cfg.isplits))

    es = ExitStack()
    with es:
        xln = es.enter_context(nc.sbuf_tensor("xln", [P, DC * T], BF16))
        big = es.enter_context(nc.sbuf_tensor("big", [P, HMX * T], BF16))
        aw = es.enter_context(nc.sbuf_tensor("aw", [P, 4 * WSL], BF16))
        a32 = es.enter_context(nc.sbuf_tensor("a32", [P, 5 * T], F32))
        AMB = max(DC * P, 8 * TH)
        amisc = es.enter_context(nc.sbuf_tensor("amisc", [P, AMB + 2 * T], BF16))
        rows_mu = es.enter_context(nc.sbuf_tensor("rows_mu", [1, T], F32))
        rowbf = es.enter_context(nc.sbuf_tensor("rowbf", [1, 8 * TH], BF16))
        m4sb = es.enter_context(nc.sbuf_tensor("m4sb", [P, 8], F32))
        seff = es.enter_context(nc.sbuf_tensor("seff", [P, 2], F32))
        scv = es.enter_context(nc.sbuf_tensor("scv", [P, 2 * NH + 2 * NKV], F32))
        idt32 = es.enter_context(nc.sbuf_tensor("idt32", [P, P], F32))
        idtbf = es.enter_context(nc.sbuf_tensor("idtbf", [P, P], BF16))
        o_r_bf = es.enter_context(nc.sbuf_tensor("o_r_bf", [1, P], BF16))
        o_c_bf = es.enter_context(nc.sbuf_tensor("o_c_bf", [P, 1], BF16))
        o_r_32 = es.enter_context(nc.sbuf_tensor("o_r_32", [1, P], F32))
        n1_r_bf = es.enter_context(nc.sbuf_tensor("n1_r_bf", [1, P], BF16))
        cnst = es.enter_context(nc.sbuf_tensor("cnst", [1, 2], F32))

        pb = [es.enter_context(nc.psum_tensor(f"pb{i}", [P, TH], F32)) for i in range(7)]
        pbf = es.enter_context(nc.psum_tensor("pbf", [P, 2 * TH], BF16))

        _pools = {
            "gdma": [es.enter_context(nc.semaphore(f"spg{i}")) for i in range(3)],
            "sdma": [es.enter_context(nc.semaphore(f"sps{i}")) for i in range(6)],
            "eng": [es.enter_context(nc.semaphore(f"spe{i}")) for i in range(16)],
        }
        _pool_ps = {}
        _pool_i = {"gdma": 0, "sdma": 0, "eng": 0}

        def S(name):
            if name.endswith("wd"):
                kind = "gdma"
            elif name.endswith(("ld", "st", "kv")):
                kind = "sdma"
            else:
                kind = "eng"
            pool = _pools[kind]
            i = _pool_i[kind] % len(pool)
            _pool_i[kind] += 1
            h = pool[i]
            prev = _pool_ps.get((kind, i))
            start = prev.n if prev is not None else 0
            ps = PSem(h, start)
            _pool_ps[(kind, i)] = ps
            return ps

        def wslot(i, width=None):
            width = WSL if width is None else width
            return aw[:, i * WSL:i * WSL + width]

        def s32(i, th=None):
            if th is None:
                return a32[:, i * T:(i + 1) * T]
            return a32[:, i * T + th * TH: i * T + (th + 1) * TH]

        rows_sc = a32[0:1, 1 * T:2 * T]
        rows_rs = a32[0:1, 2 * T:3 * T]

        def xlnc(c, th=None):
            if th is None:
                return xln[:, c * T:(c + 1) * T]
            return xln[:, c * T + th * TH: c * T + th * TH + TH]

        def bigc(c, th=None):
            if th is None:
                return big[:, c * T:(c + 1) * T]
            return big[:, c * T + th * TH: c * T + th * TH + TH]

        absb = amisc[:, 0:DC * P]
        kst = amisc[:, AMB:AMB + T]
        vst = amisc[:, AMB + T:AMB + 2 * T]
        vst2 = kst

        def sqv(i):
            return amisc[:, i * T:(i + 1) * T]

        def psb_tile(i):
            return amisc[:, AMB + i * TH:AMB + (i + 1) * TH]

        def ring(u):
            i = u % 8
            return amisc[:, i * TH:(i + 1) * TH]

        def sum_row(th, which):
            bank = {(0, 0): pb[4], (0, 1): pb[5], (1, 0): pb[6], (1, 1): pb[3]}[(th, which)]
            return bank[0:1, :]

        with nc.Block("init") as blk:
            s_ld = S("i_ld")

            @blk.sync
            def _(sync):
                for dst, src in [
                    (idt32[:, :], ident32.ap()), (idtbf[:, :], identbf.ap()),
                    (o_r_bf[:, :], ones_r_bf.ap()), (o_c_bf[:, :], ones_c_bf.ap()),
                    (o_r_32[:, :], ones_r_32.ap()),
                    (n1_r_bf[:, :], neg1_r_bf.ap()),
                    (scv[:, 0:NH], qsc), (scv[:, NH:NH + NKV], ksc),
                    (scv[:, NH + NKV:NH + 2 * NKV], vsc),
                    (scv[:, NH + 2 * NKV:2 * NH + 2 * NKV], osc),
                ]:
                    s_ld.inc(sync.dma_start(out=dst, in_=src), 16)
                sync.wait_ge(s_ld.h, s_ld.at(16 * 9))

            @blk.vector
            def _(vector):
                vector.memset(cnst[0:1, 0:1], EPS)
                vector.memset(cnst[0:1, 1:2], -0.5)

        def ln_phase(name, srcs, write_to):
            ns = len(srcs)
            with nc.Block(name) as blk:
                s_ld, s_st, s_ve, s_ac, s_mm = (S(f"{name}ld"), S(f"{name}st"), S(f"{name}ve"),
                                                S(f"{name}ac"), S(f"{name}mm"))
                vp = max(ns - 1, 0)
                xs = 4 if ns > 1 else 0

                @blk.sync
                def _(sync):
                    for c in range(DC):
                        if c >= 1:
                            sync.wait_ge(s_ac.h, s_ac.at(2 * c))
                            if vp:
                                sync.wait_ge(s_ve.h, s_ve.at(vp * c))
                        for j, src in enumerate(srcs):
                            s_ld.inc(sync.dma_start(out=s32(j), in_=src[c * P:(c + 1) * P, :]), 16)
                        if write_to is not None:
                            sync.wait_ge(s_ve.h, s_ve.at(vp * (c + 1)))
                            s_st.inc(sync.dma_start(out=write_to[c * P:(c + 1) * P, :], in_=s32(xs)), 16)
                    if write_to is not None:
                        sync.wait_ge(s_st.h, s_st.at(16 * DC))

                if vp:
                    @blk.vector
                    def _(vector):
                        for c in range(DC):
                            vector.wait_ge(s_ld.h, s_ld.at(16 * ns * (c + 1)))
                            if c >= 1:
                                vector.wait_ge(s_ac.h, s_ac.at(2 * c))
                                if write_to is not None:
                                    vector.wait_ge(s_st.h, s_st.at(16 * c))
                            s_ve.inc(vector.tensor_add(out=s32(xs), in0=s32(0), in1=s32(1)), 1)
                            for j in range(2, ns):
                                s_ve.inc(vector.tensor_add(out=s32(xs), in0=s32(xs), in1=s32(j)), 1)

                @blk.scalar
                def _(scalar):
                    for c in range(DC):
                        if vp:
                            scalar.wait_ge(s_ve.h, s_ve.at(vp * (c + 1)))
                        else:
                            scalar.wait_ge(s_ld.h, s_ld.at(16 * ns * (c + 1)))
                        if c >= 2:
                            scalar.wait_ge(s_mm.h, s_mm.at(4 * (c - 1)))
                        s_ac.inc(scalar.activation(out=xlnc(c), in_=s32(xs), func=AF.Copy), 1)
                        s_ac.inc(scalar.activation(out=sqv(c % 2), in_=s32(xs), func=AF.Square), 1)

                @blk.tensor
                def _(tensor):
                    for c in range(DC):
                        tensor.wait_ge(s_ac.h, s_ac.at(2 * (c + 1)))
                        for th in range(2):
                            s_mm.inc(tensor.matmul(sum_row(th, 0), o_c_bf[:, :], xlnc(c, th),
                                          start=(c == 0), stop=(c == DC - 1)), 1)
                            s_mm.inc(tensor.matmul(sum_row(th, 1), o_c_bf[:, :],
                                          sqv(c % 2)[:, th * TH:th * TH + TH],
                                          start=(c == 0), stop=(c == DC - 1)), 1)

            ln_fin(name + "f")

        def ln_fin(name):
            with nc.Block(name) as blk:
                s_ve, s_ac, s_mm = S(f"{name}ve"), S(f"{name}ac"), S(f"{name}mm")

                @blk.vector
                def _(vector):
                    for th in range(2):
                        mu = rows_mu[0:1, th * TH:(th + 1) * TH]
                        sc = rows_sc[0:1, th * TH:(th + 1) * TH]
                        rs = rows_rs[0:1, th * TH:(th + 1) * TH]
                        s_ve.inc(vector.tensor_scalar(out=mu, in0=sum_row(th, 0), scalar1=1.0 / D,
                                             scalar2=None, op0=ALU.mult), 1)
                        s_ve.inc(vector.tensor_scalar(out=sc, in0=sum_row(th, 1), scalar1=1.0 / D,
                                             scalar2=None, op0=ALU.mult), 1)
                        s_ve.inc(vector.tensor_mul(out=rs, in0=mu, in1=mu), 1)
                        s_ve.inc(vector.tensor_sub(out=sc, in0=sc, in1=rs), 1)
                    vector.wait_ge(s_mm.h, s_mm.at(4))
                    for c in range(DC):
                        for th in range(2):
                            mb = pb[th]
                            rb = pb[2 + th]
                            t1 = s32(0, th)
                            s_ve.inc(vector.tensor_sub(out=t1, in0=xlnc(c, th), in1=mb[:, :]), 1)
                            s_ve.inc(vector.tensor_mul(out=xlnc(c, th), in0=t1, in1=rb[:, :]), 1)

                @blk.scalar
                def _(scalar):
                    for th in range(2):
                        scalar.wait_ge(s_ve.h, s_ve.at(4 * (th + 1)))
                        sc = rows_sc[0:1, th * TH:(th + 1) * TH]
                        rs = rows_rs[0:1, th * TH:(th + 1) * TH]
                        s_ac.inc(scalar.activation(out=sc, in_=sc, func=AF.Ln, bias=cnst[0:1, 0:1]), 1)
                        s_ac.inc(scalar.activation(out=rs, in_=sc, func=AF.Exp, scale=cnst[0:1, 1:2]), 1)

                @blk.tensor
                def _(tensor):
                    for th in range(2):
                        tensor.wait_ge(s_ac.h, s_ac.at(2 * (th + 1)))
                        mu = rows_mu[0:1, th * TH:(th + 1) * TH]
                        rs = rows_rs[0:1, th * TH:(th + 1) * TH]
                        s_mm.inc(tensor.matmul(pb[th][:, :], o_r_32[:, :], mu, start=True, stop=True), 1)
                        s_mm.inc(tensor.matmul(pb[2 + th][:, :], o_r_32[:, :], rs, start=True, stop=True), 1)

        def ffn_phase(name):
            for t, (c0, c1) in enumerate(cfg.isplits):
                ncnk = c1 - c0
                with nc.Block(f"{name}g{t}") as blk:
                    s_wd, s_mm, s_ac, s_ve = (S(f"{name}g{t}wd"), S(f"{name}g{t}mm"),
                                              S(f"{name}g{t}ac"), S(f"{name}g{t}ve"))

                    @blk.gpsimd
                    def _(gpsimd):
                        for s in range(ncnk):
                            if s >= 2:
                                gpsimd.wait_ge(s_mm.h, s_mm.at(s - 1))
                            gs = c0 + s
                            s_wd.inc(gpsimd.dma_start(
                                out=wslot(s % 2, DC * P).rearrange("p (c j) -> p c j", j=P),
                                in_=wg_t[gs]), 16)
                            s_wd.inc(gpsimd.dma_start(
                                out=wslot(2 + s % 2, DC * P).rearrange("p (c j) -> p c j", j=P),
                                in_=wu_t[gs]), 16)

                    @blk.tensor
                    def _(tensor):
                        for s in range(ncnk):
                            tensor.wait_ge(s_wd.h, s_wd.at(32 * (s + 1)))
                            if s >= 1:
                                tensor.wait_ge(s_ac.h, s_ac.at(s))
                                tensor.wait_ge(s_ve.h, s_ve.at(s))
                            g = wslot(s % 2, DC * P)
                            u = wslot(2 + s % 2, DC * P)
                            for c in range(DC):
                                for th in range(2):
                                    tensor.matmul(pb[th][:, :], g[:, c * P:(c + 1) * P], xlnc(c, th),
                                                  start=(c == 0), stop=(c == DC - 1))
                                    m = tensor.matmul(pb[2 + th][:, :], u[:, c * P:(c + 1) * P], xlnc(c, th),
                                                      start=(c == 0), stop=(c == DC - 1))
                            s_mm.inc(m, 1)

                    @blk.scalar
                    def _(scalar):
                        for s in range(ncnk):
                            scalar.wait_ge(s_mm.h, s_mm.at(s + 1))
                            if s >= 1:
                                scalar.wait_ge(s_ve.h, s_ve.at(s))
                            for th in range(2):
                                a = scalar.activation(out=s32(0, th), in_=pb[th][:, :], func=AF.Silu)
                            s_ac.inc(a, 1)

                    @blk.vector
                    def _(vector):
                        for s in range(ncnk):
                            vector.wait_ge(s_ac.h, s_ac.at(s + 1))
                            for th in range(2):
                                v = vector.tensor_mul(out=bigc(s, th), in0=s32(0, th), in1=pb[2 + th][:, :])
                            s_ve.inc(v, 1)

                with nc.Block(f"{name}d{t}") as blk:
                    s_wd, s_mm, s_ac, s_st = (S(f"{name}d{t}wd"), S(f"{name}d{t}mm"),
                                              S(f"{name}d{t}ac"), S(f"{name}d{t}st"))

                    @blk.gpsimd
                    def _(gpsimd):
                        for o in range(DC):
                            if o >= 2:
                                gpsimd.wait_ge(s_mm.h, s_mm.at(o - 1))
                            s_wd.inc(gpsimd.dma_start(
                                out=wslot(o % 2, ncnk * P).rearrange("p (c j) -> p c j", j=P),
                                in_=wd_t[o][:, c0:c1, :]), 16)

                    @blk.tensor
                    def _(tensor):
                        for o in range(DC):
                            tensor.wait_ge(s_wd.h, s_wd.at(16 * (o + 1)))
                            if o >= 1:
                                tensor.wait_ge(s_ac.h, s_ac.at(o))
                            w = wslot(o % 2, ncnk * P)
                            for c in range(ncnk):
                                for th in range(2):
                                    m = tensor.matmul(pb[4 + th][:, :], w[:, c * P:(c + 1) * P], bigc(c, th),
                                                      start=(c == 0), stop=(c == ncnk - 1))
                            s_mm.inc(m, 1)

                    @blk.scalar
                    def _(scalar):
                        for o in range(DC):
                            scalar.wait_ge(s_mm.h, s_mm.at(o + 1))
                            if o >= 2:
                                scalar.wait_ge(s_st.h, s_st.at(16 * (o - 1)))
                            for th in range(2):
                                a = scalar.activation(out=s32(2 + o % 2, th), in_=pb[4 + th][:, :],
                                                      func=AF.Copy, scale=0.5)
                            s_ac.inc(a, 1)

                    @blk.sync
                    def _(sync):
                        for o in range(DC):
                            sync.wait_ge(s_ac.h, s_ac.at(o + 1))
                            s_st.inc(sync.dma_start(out=part[t][o * P:(o + 1) * P, :],
                                           in_=s32(2 + o % 2)), 16)
                        sync.wait_ge(s_st.h, s_st.at(16 * DC))

        def bitnet_proj(name, w_ap, n_o, scol_off, extra_scale, dest,
                        kv_store=None, v_transpose=False, pre_gpsimd=None):
            with nc.Block(name) as blk:
                s_wd, s_ab, s_sg = S(f"{name}wd"), S(f"{name}ab"), S(f"{name}sg")
                s_sm, s_cl, s_tp = S(f"{name}sm"), S(f"{name}cl"), S(f"{name}tp")
                s_se, s_pm, s_ev = S(f"{name}se"), S(f"{name}pm"), S(f"{name}ev")
                s_st, s_vt, s_vc = S(f"{name}st"), S(f"{name}vt"), S(f"{name}vc")

                @blk.gpsimd
                def _(gpsimd):
                    if pre_gpsimd is not None:
                        pre_gpsimd(gpsimd)
                    for o in range(n_o):
                        if o >= 2:
                            gpsimd.wait_ge(s_pm.h, s_pm.at(o - 1))
                        s_wd.inc(gpsimd.dma_start(
                            out=wslot(o % 2, DC * P).rearrange("p (c j) -> p c j", j=P),
                            in_=w_ap[o]), 16)

                @blk.scalar
                def _(scalar):
                    for o in range(n_o):
                        scalar.wait_ge(s_wd.h, s_wd.at(16 * (o + 1)))
                        if o >= 1:
                            scalar.wait_ge(s_sm.h, s_sm.at(o))
                        w = wslot(o % 2, DC * P)
                        s_ab.inc(scalar.activation(out=absb, in_=w, func=AF.Abs), 1)
                        s_sg.inc(scalar.activation(out=w, in_=w, func=AF.Sign), 1)
                        scalar.wait_ge(s_pm.h, s_pm.at(o + 1))
                        scalar.wait_ge(s_se.h, s_se.at(o + 1))
                        if kv_store is not None and o >= 2:
                            scalar.wait_ge(s_st.h, s_st.at(16 * (o - 1)))
                        sc = seff[:, o % 2:o % 2 + 1]
                        for th in range(2):
                            if kv_store is None:
                                dst = dest(o, th)
                            elif v_transpose:
                                dst = vst[:, th * TH:(th + 1) * TH]
                            else:
                                dst = kst[:, th * TH:(th + 1) * TH]
                            a = scalar.activation(out=dst, in_=pb[th][:, :], func=AF.Copy, scale=sc)
                        s_ev.inc(a, 1)
                        if v_transpose:
                            scalar.wait_ge(s_vt.h, s_vt.at(o + 1))
                            if o >= 1:
                                scalar.wait_ge(s_st.h, s_st.at(16 * o))
                            s_vc.inc(scalar.activation(out=vst2, in_=pbf[:, 0:T], func=AF.Copy), 1)

                @blk.tensor
                def _(tensor):
                    for o in range(n_o):
                        tensor.wait_ge(s_ab.h, s_ab.at(o + 1))
                        if o >= 1:
                            tensor.wait_ge(s_cl.h, s_cl.at(o))
                        for c in range(DC):
                            m = tensor.matmul(pb[6][0:1, 0:P], o_c_bf[:, :], absb[:, c * P:(c + 1) * P],
                                              start=(c == 0), stop=(c == DC - 1))
                        s_sm.inc(m, 1)
                        tensor.wait_ge(s_cl.h, s_cl.at(o + 1))
                        if o >= 1:
                            tensor.wait_ge(s_se.h, s_se.at(o))
                        s_tp.inc(tensor.transpose(pb[6][:, 4:5], rows_sc[0:1, 0:P], idt32[0:1, 0:1]), 1)
                        tensor.wait_ge(s_sg.h, s_sg.at(o + 1))
                        if o >= 1:
                            tensor.wait_ge(s_ev.h, s_ev.at(o))
                        w = wslot(o % 2, DC * P)
                        for c in range(DC):
                            for th in range(2):
                                m = tensor.matmul(pb[th][:, :], w[:, c * P:(c + 1) * P], xlnc(c, th),
                                                  start=(c == 0), stop=(c == DC - 1))
                        s_pm.inc(m, 1)
                        if v_transpose:
                            tensor.wait_ge(s_ev.h, s_ev.at(o + 1))
                            if o >= 1:
                                tensor.wait_ge(s_vc.h, s_vc.at(o))
                            for tt in range(TN):
                                m = tensor.transpose(pbf[:, tt * P:(tt + 1) * P],
                                                     vst[:, tt * P:(tt + 1) * P],
                                                     idtbf[:, :])
                            s_vt.inc(m, 1)

                @blk.vector
                def _(vector):
                    for o in range(n_o):
                        vector.wait_ge(s_sm.h, s_sm.at(o + 1))
                        s_cl.inc(vector.tensor_scalar(out=rows_sc[0:1, 0:P], in0=pb[6][0:1, 0:P],
                                             scalar1=1.0 / D, scalar2=1e-8,
                                             op0=ALU.mult, op1=ALU.max), 1)
                        vector.wait_ge(s_tp.h, s_tp.at(o + 1))
                        if o >= 1:
                            vector.wait_ge(s_ev.h, s_ev.at(o))
                        s_se.inc(vector.scalar_tensor_tensor(out=seff[:, o % 2:o % 2 + 1], in0=pb[6][:, 4:5],
                                                    scalar=extra_scale,
                                                    in1=scv[:, scol_off + o:scol_off + o + 1],
                                                    op0=ALU.mult, op1=ALU.mult), 1)

                if kv_store is not None:
                    @blk.sync
                    def _(sync):
                        for o in range(n_o):
                            if not v_transpose:
                                sync.wait_ge(s_ev.h, s_ev.at(o + 1))
                                s_st.inc(sync.dma_start(out=cc_in.ap()[o * P:(o + 1) * P, :],
                                               in_=kst), 16)
                            else:
                                sync.wait_ge(s_vc.h, s_vc.at(o + 1))
                                s_st.inc(sync.dma_start(
                                    out=cc_in.ap()[T:2 * T, o * P:(o + 1) * P]
                                    .rearrange("(tt p) j -> p tt j", p=P),
                                    in_=vst2.rearrange("p (tt j) -> p tt j", j=P)), 16)
                        sync.wait_ge(s_st.h, s_st.at(16 * n_o))

        def attn_phase(name, s_cc_handle):
            with nc.Block(name) as blk:
                s_kv, s_rk, s_mr = S(f"{name}kv"), S(f"{name}rk"), S(f"{name}mr")
                s_nk, s_sq = S(f"{name}nk"), S(f"{name}sq")
                s_sb, s_ex, s_av = S(f"{name}sb"), S(f"{name}ex"), S(f"{name}av")
                s_tn, s_rc, s_ib = S(f"{name}tn"), S(f"{name}rc"), S(f"{name}ib")
                s_iv, s_ao = S(f"{name}iv"), S(f"{name}ao")
                UPH = GR * 2
                NU = NKV * UPH
                NKB = KEYS // TH
                NPH = NKB + UPH

                def unit_of(h, uh):
                    u = h * UPH + uh
                    return u, h * GR + uh // 2, uh % 2

                def mrow(u):
                    return rowbf[0:1, (u % 8) * TH:(u % 8) * TH + TH]

                def nrow_bank(i):
                    return (pb[5] if i % 2 == 0 else pb[6])[0:1, :]

                psB = [pb[2], pb[3], pb[6]]
                sqK = amisc[:, 0:KEYS]
                sqQ = amisc[:, AMB + T:AMB + 2 * T]
                rkp = a32[0:1, 0:8]
                rk2 = a32[0:1, 8:9]
                lnt = a32[0:1, T:T + TH]

                def invb(u):
                    return s32(2, u % 2)

                def mrow_inv(u):
                    return rows_mu[0:1, (u % 2) * TH:(u % 2) * TH + TH]

                @blk.sync
                def _(sync):
                    sync.wait_ge(s_cc_handle.h, s_cc_handle.at(1))
                    for h in range(NKV):
                        if h >= 2:
                            sync.wait_ge(s_tn.h, s_tn.at(UPH * (h - 1)))
                        kslot = wslot(h % 2, KEYS)
                        vslot = wslot(2 + h % 2, KEYS)
                        for r in range(R):
                            s_kv.inc(sync.dma_start(
                                out=kslot[:, r * T:(r + 1) * T],
                                in_=cc_out.ap()[r * 2 * T + h * P: r * 2 * T + h * P + P, :]), 16)
                            s_kv.inc(sync.dma_start(
                                out=vslot.rearrange("p (kt j) -> p kt j", j=P)[:, r * TN:(r + 1) * TN, :],
                                in_=cc_out.ap()[r * 2 * T + T: r * 2 * T + 2 * T, h * P:(h + 1) * P]
                                .rearrange("(kt p) j -> p kt j", p=P)), 16)

                @blk.scalar
                def _(scalar):
                    for h in range(NKV):
                        scalar.wait_ge(s_kv.h, s_kv.at(16 * 2 * R * (h + 1)))
                        if h >= 1:
                            scalar.wait_ge(s_nk.h, s_nk.at(NPH * h - UPH))
                        s_sq.inc(scalar.activation(out=sqK, in_=wslot(h % 2, KEYS), func=AF.Square), 1)
                        for uh in range(UPH):
                            u, g, qt = unit_of(h, uh)
                            if uh % 2 == 0:
                                if u >= 1:
                                    scalar.wait_ge(s_nk.h, s_nk.at(NKB * (h + 1) + UPH * h + uh))
                                s_sq.inc(scalar.activation(out=sqQ[:, 0:T], in_=bigc(g), func=AF.Square), 1)
                            scalar.wait_ge(s_nk.h, s_nk.at(NKB * (h + 1) + UPH * h + uh + 1))
                            scalar.wait_ge(s_rk.h, s_rk.at((NKB + 1) * (h + 1)))
                            scalar.activation(out=lnt, in_=nrow_bank(NKB + uh), func=AF.Ln,
                                              scale=rk2)
                            s_mr.inc(scalar.activation(out=mrow(u), in_=lnt, func=AF.Exp,
                                                       scale=0.5), 1)
                        for uh in range(UPH):
                            u, g, qt = unit_of(h, uh)
                            for kt in range(KTN):
                                gi = u * KTN + kt
                                scalar.wait_ge(s_sb.h, s_sb.at(gi + 1))
                                if gi >= 3:
                                    scalar.wait_ge(s_av.h, s_av.at(gi - 2))
                                s_ex.inc(scalar.activation(out=psb_tile(kt % 3),
                                                           in_=psB[kt % 3][:, :], func=AF.Exp), 1)
                            scalar.wait_ge(s_ib.h, s_ib.at(u + 1))
                            if u >= 2:
                                scalar.wait_ge(s_ao.h, s_ao.at(u - 1))
                            s_iv.inc(scalar.activation(out=invb(u), in_=pb[5][:, :],
                                                       func=AF.Copy), 1)

                @blk.tensor
                def _(tensor):
                    for h in range(NKV):
                        kslot = wslot(h % 2, KEYS)
                        vslot = wslot(2 + h % 2, KEYS)
                        tensor.wait_ge(s_sq.h, s_sq.at((1 + GR) * h + 1))
                        if h >= 1:
                            tensor.wait_ge(s_iv.h, s_iv.at(UPH * h))
                        for kb in range(NKB):
                            i = NPH * h + kb
                            if kb >= 2:
                                tensor.wait_ge(s_rk.h, s_rk.at((NKB + 1) * h + kb - 1))
                            s_nk.inc(tensor.matmul(nrow_bank(kb), o_c_bf[:, :],
                                                   sqK[:, kb * TH:(kb + 1) * TH],
                                                   start=True, stop=True), 1)
                        for uh in range(UPH):
                            u, g, qt = unit_of(h, uh)
                            if uh % 2 == 0:
                                tensor.wait_ge(s_sq.h, s_sq.at((1 + GR) * h + 2 + uh // 2))
                            if uh < 2:
                                tensor.wait_ge(s_rk.h, s_rk.at((NKB + 1) * h + NKB - 1 + uh))
                            else:
                                tensor.wait_ge(s_mr.h, s_mr.at(u - 1))
                            s_nk.inc(tensor.matmul(nrow_bank(NKB + uh), o_c_bf[:, :],
                                                   sqQ[:, qt * TH:(qt + 1) * TH],
                                                   start=True, stop=True), 1)
                        for uh in range(UPH):
                            u, g, qt = unit_of(h, uh)
                            rhs_q = bigc(g, qt)
                            for kt in range(KTN):
                                gi = u * KTN + kt
                                if gi >= 3:
                                    tensor.wait_ge(s_ex.h, s_ex.at(gi - 2))
                                if kt == 0:
                                    tensor.wait_ge(s_mr.h, s_mr.at(UPH * (h + 1)))
                                tensor.matmul(psB[kt % 3][:, :], n1_r_bf[:, :], mrow(u),
                                              start=True, stop=False)
                                s_sb.inc(tensor.matmul(psB[kt % 3][:, :], kslot[:, kt * P:(kt + 1) * P],
                                                       rhs_q, start=False, stop=True), 1)
                                if kt >= 1:
                                    tensor.wait_ge(s_ex.h, s_ex.at(gi))
                                    if kt == 1:
                                        tensor.wait_ge(s_ao.h, s_ao.at(u))
                                        tensor.wait_ge(s_iv.h, s_iv.at(u))
                                    s_av.inc(tensor.matmul(pb[4][:, :], vslot[:, (kt - 1) * P:kt * P],
                                                           psb_tile((kt - 1) % 3), start=(kt == 1),
                                                           stop=False), 1)
                                    tensor.matmul(pb[5][0:1, :], o_c_bf[:, :], psb_tile((kt - 1) % 3),
                                                  start=(kt == 1), stop=False)
                            tensor.wait_ge(s_ex.h, s_ex.at(u * KTN + KTN))
                            s_av.inc(tensor.matmul(pb[4][:, :], vslot[:, (KTN - 1) * P:KTN * P],
                                                   psb_tile((KTN - 1) % 3), start=False, stop=True), 1)
                            m = tensor.matmul(pb[5][0:1, :], o_c_bf[:, :], psb_tile((KTN - 1) % 3),
                                              start=False, stop=True)
                            s_tn.inc(m, 1)
                            tensor.wait_ge(s_rc.h, s_rc.at(u + 1))
                            s_ib.inc(tensor.matmul(pb[5][:, :], o_r_32[:, :],
                                                   mrow_inv(u), start=True, stop=True), 1)

                @blk.vector
                def _(vector):
                    for h in range(NKV):
                        for kb in range(NKB):
                            vector.wait_ge(s_nk.h, s_nk.at(NPH * h + kb + 1))
                            s_rk.inc(vector.reduce_max(out=rkp[0:1, kb:kb + 1],
                                                       in_=nrow_bank(kb), axis=AX.X), 1)
                        s_rk.inc(vector.reduce_max(out=rk2, in_=rkp, axis=AX.X), 1)
                        for uh in range(UPH):
                            u, g, qt = unit_of(h, uh)
                            vector.wait_ge(s_tn.h, s_tn.at(u + 1))
                            if u >= 1:
                                vector.wait_ge(s_ib.h, s_ib.at(u - 1))
                            s_rc.inc(vector.reciprocal(out=mrow_inv(u), in_=pb[5][0:1, :]), 1)
                            vector.wait_ge(s_iv.h, s_iv.at(u + 1))
                            s_ao.inc(vector.tensor_mul(out=bigc(g, qt), in0=pb[4][:, :],
                                                       in1=invb(u)), 1)

        def oproj_phase(name):
            with nc.Block(name) as blk:
                s_wd, s_ab, s_sg = S(f"{name}wd"), S(f"{name}ab"), S(f"{name}sg")
                s_sm, s_cl, s_tp = S(f"{name}sm"), S(f"{name}cl"), S(f"{name}tp")
                s_se, s_pm, s_ev = S(f"{name}se"), S(f"{name}pm"), S(f"{name}ev")
                s_ld, s_ve, s_st = S(f"{name}ld"), S(f"{name}ve"), S(f"{name}st")
                s_ac2, s_mm2 = S(f"{name}ac2"), S(f"{name}mm2")

                @blk.gpsimd
                def _(gpsimd):
                    for o in range(DC):
                        if o >= 2:
                            gpsimd.wait_ge(s_pm.h, s_pm.at(o - 1))
                        s_wd.inc(gpsimd.dma_start(
                            out=wslot(o % 2, DC * P).rearrange("p (c j) -> p c j", j=P),
                            in_=wo_t[o]), 16)

                @blk.scalar
                def _(scalar):
                    for o in range(DC):
                        scalar.wait_ge(s_wd.h, s_wd.at(16 * (o + 1)))
                        if o >= 1:
                            scalar.wait_ge(s_sm.h, s_sm.at(o))
                        w = wslot(o % 2, DC * P)
                        s_ab.inc(scalar.activation(out=absb, in_=w, func=AF.Abs), 1)
                        s_sg.inc(scalar.activation(out=w, in_=w, func=AF.Sign), 1)
                        scalar.wait_ge(s_pm.h, s_pm.at(o + 1))
                        scalar.wait_ge(s_se.h, s_se.at(o + 1))
                        if o >= 1:
                            scalar.wait_ge(s_ve.h, s_ve.at(2 * o))
                        sc = seff[:, o % 2:o % 2 + 1]
                        for th in range(2):
                            a = scalar.activation(out=s32(2, th), in_=pb[th][:, :], func=AF.Copy, scale=sc)
                        s_ev.inc(a, 1)
                        scalar.wait_ge(s_ve.h, s_ve.at(2 * (o + 1)))
                        if o >= 2:
                            scalar.wait_ge(s_mm2.h, s_mm2.at(4 * (o - 1)))
                        s_ac2.inc(scalar.activation(out=xlnc(o), in_=s32(4), func=AF.Copy), 1)
                        s_ac2.inc(scalar.activation(out=(kst if o % 2 == 0 else vst),
                                                    in_=s32(4), func=AF.Square), 1)

                @blk.tensor
                def _(tensor):
                    for o in range(DC):
                        tensor.wait_ge(s_ab.h, s_ab.at(o + 1))
                        if o >= 1:
                            tensor.wait_ge(s_cl.h, s_cl.at(o))
                        for c in range(DC):
                            m = tensor.matmul(pb[2][0:1, 0:P], o_c_bf[:, :], absb[:, c * P:(c + 1) * P],
                                              start=(c == 0), stop=(c == DC - 1))
                        s_sm.inc(m, 1)
                        tensor.wait_ge(s_cl.h, s_cl.at(o + 1))
                        if o >= 1:
                            tensor.wait_ge(s_se.h, s_se.at(o))
                        s_tp.inc(tensor.transpose(pb[2][:, 4:5], rows_sc[0:1, 0:P], idt32[0:1, 0:1]), 1)
                        tensor.wait_ge(s_sg.h, s_sg.at(o + 1))
                        if o >= 1:
                            tensor.wait_ge(s_ev.h, s_ev.at(o))
                        w = wslot(o % 2, DC * P)
                        for c in range(DC):
                            for th in range(2):
                                m = tensor.matmul(pb[th][:, :], w[:, c * P:(c + 1) * P], bigc(c, th),
                                                  start=(c == 0), stop=(c == DC - 1))
                        s_pm.inc(m, 1)
                        tensor.wait_ge(s_ac2.h, s_ac2.at(2 * (o + 1)))
                        sq_t = kst if o % 2 == 0 else vst
                        for th in range(2):
                            s_mm2.inc(tensor.matmul(sum_row(th, 0), o_c_bf[:, :], xlnc(o, th),
                                          start=(o == 0), stop=(o == DC - 1)), 1)
                            s_mm2.inc(tensor.matmul(sum_row(th, 1), o_c_bf[:, :],
                                          sq_t[:, th * TH:th * TH + TH],
                                          start=(o == 0), stop=(o == DC - 1)), 1)

                @blk.vector
                def _(vector):
                    for o in range(DC):
                        vector.wait_ge(s_sm.h, s_sm.at(o + 1))
                        s_cl.inc(vector.tensor_scalar(out=rows_sc[0:1, 0:P], in0=pb[2][0:1, 0:P],
                                             scalar1=1.0 / D, scalar2=1e-8,
                                             op0=ALU.mult, op1=ALU.max), 1)
                        vector.wait_ge(s_tp.h, s_tp.at(o + 1))
                        if o >= 1:
                            vector.wait_ge(s_ev.h, s_ev.at(o))
                        s_se.inc(vector.scalar_tensor_tensor(out=seff[:, o % 2:o % 2 + 1], in0=pb[2][:, 4:5],
                                                    scalar=1.0,
                                                    in1=scv[:, NH + 2 * NKV + o:NH + 2 * NKV + o + 1],
                                                    op0=ALU.mult, op1=ALU.mult), 1)
                        vector.wait_ge(s_ev.h, s_ev.at(o + 1))
                        vector.wait_ge(s_ld.h, s_ld.at(16 * (o + 1)))
                        if o >= 1:
                            vector.wait_ge(s_st.h, s_st.at(16 * o))
                            vector.wait_ge(s_ac2.h, s_ac2.at(2 * o))
                        s_ve.inc(vector.tensor_add(out=s32(4), in0=s32(2), in1=s32(3)), 2)

                @blk.sync
                def _(sync):
                    for o in range(DC):
                        if o >= 1:
                            sync.wait_ge(s_ve.h, s_ve.at(2 * o))
                        s_ld.inc(sync.dma_start(out=s32(3), in_=xc2[o * P:(o + 1) * P, :]), 16)
                        sync.wait_ge(s_ve.h, s_ve.at(2 * (o + 1)))
                        s_st.inc(sync.dma_start(out=xc3[o * P:(o + 1) * P, :], in_=s32(4)), 16)
                    sync.wait_ge(s_st.h, s_st.at(16 * DC))

            ln_fin(name + "f")

        ln_phase("ln1", [xT], None)
        ffn_phase("f1")
        ln_phase("ln2", [xT, part[0], part[1], part[2]], xc2)

        s_cc = PSem(es.enter_context(nc.semaphore("cc")), 0)

        bitnet_proj("kp", wk_t, NKV, NH, 1.0, None, kv_store="k")
        bitnet_proj("vp", wv_t, NKV, NH + NKV, 1.0, None, kv_store="v", v_transpose=True)

        def q_pre(gpsimd):
            s_cc.inc(gpsimd.collective_compute(
                "AllGather", ALU.bypass,
                replica_groups=[[0, 1, 2, 3], [4, 5, 6, 7]],
                ins=[cc_in.ap().opt()],
                outs=[cc_out.ap().opt()],
            ), 1)

        bitnet_proj("qp", wq_t, NH, 0, 1.0 / math.sqrt(P), lambda o, th: bigc(o, th),
                    pre_gpsimd=q_pre)
        attn_phase("at", s_cc)
        oproj_phase("op")
        ffn_phase("f2")

        dbg = {}
        dbg_xln = None
        if debug:
            for nm, src, shp, dt in [
                ("dbg_xc2", xc2, [D, T], F32), ("dbg_xc3", xc3, [D, T], F32),
                ("dbg_p0", part[0], [D, T], F32), ("dbg_p1", part[1], [D, T], F32),
                ("dbg_p2", part[2], [D, T], F32),
                ("dbg_ccin", cc_in.ap(), [2 * T, KV], BF16),
                ("dbg_ccout", cc_out.ap(), [2 * R * T, KV], BF16),
            ]:
                dbg[nm] = (nc.dram_tensor(nm, shp, dt, kind="ExternalOutput").ap(), src)
            dbg_xln = nc.dram_tensor("dbg_xln", [P, DC * T], BF16, kind="ExternalOutput").ap()
            with nc.Block("dbg") as blk:
                s_dst = S("dbgst")

                @blk.sync
                def _(sync):
                    n = 0
                    for nm, (dst, src) in dbg.items():
                        s_dst.inc(sync.dma_start(out=dst, in_=src), 16)
                        n += 1
                    s_dst.inc(sync.dma_start(out=dbg_xln, in_=xln[:, :]), 16)
                    sync.wait_ge(s_dst.h, s_dst.at(16 * (n + 1)))

        with nc.Block("fin") as blk:
            s_ld, s_ve, s_st = S("finld"), S("finve"), S("finst")

            @blk.sync
            def _(sync):
                for c in range(DC):
                    if c >= 1:
                        sync.wait_ge(s_ve.h, s_ve.at(3 * c))
                    for j, src in enumerate([xc3, part[0], part[1], part[2]]):
                        s_ld.inc(sync.dma_start(out=s32(j), in_=src[c * P:(c + 1) * P, :]), 16)
                    sync.wait_ge(s_ve.h, s_ve.at(3 * (c + 1)))
                    s_st.inc(sync.dma_start(out=yT[c * P:(c + 1) * P, :], in_=s32(4)), 16)
                sync.wait_ge(s_st.h, s_st.at(16 * DC))

            @blk.vector
            def _(vector):
                for c in range(DC):
                    vector.wait_ge(s_ld.h, s_ld.at(16 * 4 * (c + 1)))
                    if c >= 1:
                        vector.wait_ge(s_st.h, s_st.at(16 * c))
                    s_ve.inc(vector.tensor_add(out=s32(4), in0=s32(0), in1=s32(1)), 1)
                    s_ve.inc(vector.tensor_add(out=s32(4), in0=s32(4), in1=s32(2)), 1)
                    s_ve.inc(vector.tensor_add(out=s32(4), in0=s32(4), in1=s32(3)), 1)

    return nc


def _tile_T(w):
    O, IC_ = w.shape
    return np.ascontiguousarray(w.T.reshape(IC_ // P, P, O // P, P).transpose(2, 1, 0, 3))


def prep_inputs(cfg: Cfg, inp):
    T, I, R = cfg.T, cfg.I, cfg.R
    x = inp["L0_attn_hidden_states"]
    wgu = inp["L0_ffn_w_gate_up"]

    shared = {
        "wg_t": _tile_T(wgu[:I]),
        "wu_t": _tile_T(wgu[I:]),
        "wd_t": _tile_T(inp["L0_ffn_w_down"]),
        "wq_t": _tile_T(inp["L0_attn_q_weight"]),
        "wk_t": _tile_T(inp["L0_attn_k_weight"]),
        "wv_t": _tile_T(inp["L0_attn_v_weight"]),
        "wo_t": _tile_T(inp["L0_attn_o_weight"]),
        "qsc": np.ascontiguousarray(inp["L0_attn_q_scale"].reshape(-1, P).T),
        "ksc": np.ascontiguousarray(inp["L0_attn_k_scale"].reshape(-1, P).T),
        "vsc": np.ascontiguousarray(inp["L0_attn_v_scale"].reshape(-1, P).T),
        "osc": np.ascontiguousarray(inp["L0_attn_o_scale"].reshape(-1, P).T),
    }
    in_maps = []
    for c in range(8):
        b, q = c // R, c % R
        m = dict(shared)
        m["xT"] = np.ascontiguousarray(x[b, q * T:(q + 1) * T, :].T)
        in_maps.append(m)
    return in_maps


def kernel(**inputs):
    cfg = CFG
    in_maps = prep_inputs(cfg, inputs)
    nc = build_program(cfg)
    res = run_bass_kernel_spmd(nc, in_maps, core_ids=list(range(8)))
    B = 2
    y = np.empty((B, cfg.R * cfg.T, cfg.D), np.float32)
    for c in range(8):
        b, q = c // cfg.R, c % cfg.R
        y[b, q * cfg.T:(q + 1) * cfg.T, :] = res.results[c]["yT"].T
    return y


# version 10
# speedup vs baseline: 1.1654x; 1.0028x over previous
import math
from contextlib import ExitStack
from dataclasses import dataclass

import numpy as np
import ml_dtypes

import concourse.bass as bass
import concourse.mybir as mybir
from concourse.bass_utils import run_bass_kernel_spmd

F32 = mybir.dt.float32
BF16 = mybir.dt.bfloat16
AF = mybir.ActivationFunctionType
ALU = mybir.AluOpType
AX = mybir.AxisListType

P = 128
EPS = 1e-5


@dataclass(frozen=True)
class Cfg:
    D: int = 4096
    T: int = 1024
    I: int = 11008
    KV: int = 1024
    R: int = 4

    @property
    def DC(self):
        return self.D // P

    @property
    def IC(self):
        return self.I // P

    @property
    def NH(self):
        return self.D // P

    @property
    def NKV(self):
        return self.KV // P

    @property
    def TH(self):
        return self.T // 2

    @property
    def KEYS(self):
        return self.R * self.T

    @property
    def KTN(self):
        return self.KEYS // P

    @property
    def GR(self):
        return self.NH // self.NKV

    @property
    def isplits(self):
        n = self.IC
        a = (n + 2) // 3
        b = (n - a + 1) // 2
        return [(0, a), (a, a + b), (a + b, n)]


CFG = Cfg()


class PSem:

    def __init__(self, h, start):
        self.h = h
        self.n = start
        self.base = start

    def inc(self, instr, k=1):
        instr.then_inc(self.h, k)
        self.n += k
        return instr

    def at(self, v):
        return self.base + v


def build_program(cfg: Cfg, debug=False):
    nc = bass.Bass("TRN2", target_bir_lowering=False, detect_race_conditions=False)
    D, T, I, KV = cfg.D, cfg.T, cfg.I, cfg.KV
    DC, IC, NH, NKV, TH, R = cfg.DC, cfg.IC, cfg.NH, cfg.NKV, cfg.TH, cfg.R
    KTN, GR, KEYS = cfg.KTN, cfg.GR, cfg.KEYS
    TB = TH // P
    TN = T // P
    assert T == KV, "kernel assumes tokens-per-core == kv width"

    xT = nc.dram_tensor("xT", [D, T], F32, kind="ExternalInput").ap()
    wg_t = nc.dram_tensor("wg_t", [IC, P, DC, P], F32, kind="ExternalInput").ap()
    wu_t = nc.dram_tensor("wu_t", [IC, P, DC, P], F32, kind="ExternalInput").ap()
    wd_t = nc.dram_tensor("wd_t", [DC, P, IC, P], F32, kind="ExternalInput").ap()
    wq_t = nc.dram_tensor("wq_t", [NH, P, DC, P], F32, kind="ExternalInput").ap()
    wk_t = nc.dram_tensor("wk_t", [NKV, P, DC, P], F32, kind="ExternalInput").ap()
    wv_t = nc.dram_tensor("wv_t", [NKV, P, DC, P], F32, kind="ExternalInput").ap()
    wo_t = nc.dram_tensor("wo_t", [NH, P, DC, P], F32, kind="ExternalInput").ap()
    qsc = nc.dram_tensor("qsc", [P, NH], F32, kind="ExternalInput").ap()
    ksc = nc.dram_tensor("ksc", [P, NKV], F32, kind="ExternalInput").ap()
    vsc = nc.dram_tensor("vsc", [P, NKV], F32, kind="ExternalInput").ap()
    osc = nc.dram_tensor("osc", [P, NH], F32, kind="ExternalInput").ap()
    yT = nc.dram_tensor("yT", [D, T], F32, kind="ExternalOutput").ap()

    xc2 = nc.dram_tensor("xc2", [D, T], F32, kind="Internal").ap()
    xc3 = nc.dram_tensor("xc3", [D, T], F32, kind="Internal").ap()
    part = [nc.dram_tensor(f"part{i}", [D, T], F32, kind="Internal").ap() for i in range(3)]
    cc_in = nc.dram_tensor("cc_in", [2 * T, KV], BF16, kind="Internal")
    cc_out = nc.dram_tensor("cc_out", [2 * R * T, KV], BF16, kind="Internal")

    ident32 = nc.inline_tensor(np.eye(P, dtype=np.float32), name="ident32")
    identbf = nc.inline_tensor(np.eye(P, dtype=ml_dtypes.bfloat16), name="identbf")
    ones_r_bf = nc.inline_tensor(np.ones((1, P), dtype=ml_dtypes.bfloat16), name="ones_r_bf")
    ones_c_bf = nc.inline_tensor(np.ones((P, 1), dtype=ml_dtypes.bfloat16), name="ones_c_bf")
    ones_r_32 = nc.inline_tensor(np.ones((1, P), dtype=np.float32), name="ones_r_32")
    neg1_r_bf = nc.inline_tensor(np.full((1, P), -1.0, dtype=ml_dtypes.bfloat16), name="neg1_r_bf")

    WSL = max(DC * P, KEYS)
    HMX = max(NH, max(b - a for a, b in cfg.isplits))

    es = ExitStack()
    with es:
        xln = es.enter_context(nc.sbuf_tensor("xln", [P, DC * T], BF16))
        big = es.enter_context(nc.sbuf_tensor("big", [P, HMX * T], BF16))
        aw = es.enter_context(nc.sbuf_tensor("aw", [P, 4 * WSL], BF16))
        a32 = es.enter_context(nc.sbuf_tensor("a32", [P, 5 * T], F32))
        AMB = max(DC * P, 8 * TH)
        amisc = es.enter_context(nc.sbuf_tensor("amisc", [P, AMB + 2 * T], BF16))
        rows_mu = es.enter_context(nc.sbuf_tensor("rows_mu", [1, T], F32))
        rowbf = es.enter_context(nc.sbuf_tensor("rowbf", [1, 8 * TH], BF16))
        m4sb = es.enter_context(nc.sbuf_tensor("m4sb", [P, 8], F32))
        seff = es.enter_context(nc.sbuf_tensor("seff", [P, 2], F32))
        scv = es.enter_context(nc.sbuf_tensor("scv", [P, 2 * NH + 2 * NKV], F32))
        idt32 = es.enter_context(nc.sbuf_tensor("idt32", [P, P], F32))
        idtbf = es.enter_context(nc.sbuf_tensor("idtbf", [P, P], BF16))
        o_r_bf = es.enter_context(nc.sbuf_tensor("o_r_bf", [1, P], BF16))
        o_c_bf = es.enter_context(nc.sbuf_tensor("o_c_bf", [P, 1], BF16))
        o_r_32 = es.enter_context(nc.sbuf_tensor("o_r_32", [1, P], F32))
        n1_r_bf = es.enter_context(nc.sbuf_tensor("n1_r_bf", [1, P], BF16))
        cnst = es.enter_context(nc.sbuf_tensor("cnst", [1, 2], F32))

        pb = [es.enter_context(nc.psum_tensor(f"pb{i}", [P, TH], F32)) for i in range(7)]
        pbf = es.enter_context(nc.psum_tensor("pbf", [P, 2 * TH], BF16))

        _pools = {
            "gdma": [es.enter_context(nc.semaphore(f"spg{i}")) for i in range(3)],
            "sdma": [es.enter_context(nc.semaphore(f"sps{i}")) for i in range(6)],
            "eng": [es.enter_context(nc.semaphore(f"spe{i}")) for i in range(16)],
        }
        _pool_ps = {}
        _pool_i = {"gdma": 0, "sdma": 0, "eng": 0}

        def S(name):
            if name.endswith("wd"):
                kind = "gdma"
            elif name.endswith(("ld", "st", "kv")):
                kind = "sdma"
            else:
                kind = "eng"
            pool = _pools[kind]
            i = _pool_i[kind] % len(pool)
            _pool_i[kind] += 1
            h = pool[i]
            prev = _pool_ps.get((kind, i))
            start = prev.n if prev is not None else 0
            ps = PSem(h, start)
            _pool_ps[(kind, i)] = ps
            return ps

        def wslot(i, width=None):
            width = WSL if width is None else width
            return aw[:, i * WSL:i * WSL + width]

        def s32(i, th=None):
            if th is None:
                return a32[:, i * T:(i + 1) * T]
            return a32[:, i * T + th * TH: i * T + (th + 1) * TH]

        rows_sc = a32[0:1, 1 * T:2 * T]
        rows_rs = a32[0:1, 2 * T:3 * T]

        def xlnc(c, th=None):
            if th is None:
                return xln[:, c * T:(c + 1) * T]
            return xln[:, c * T + th * TH: c * T + th * TH + TH]

        def bigc(c, th=None):
            if th is None:
                return big[:, c * T:(c + 1) * T]
            return big[:, c * T + th * TH: c * T + th * TH + TH]

        absb = amisc[:, 0:DC * P]
        kst = amisc[:, AMB:AMB + T]
        vst = amisc[:, AMB + T:AMB + 2 * T]
        vst2 = kst

        def sqv(i):
            return amisc[:, i * T:(i + 1) * T]

        def psb_tile(i):
            return amisc[:, AMB + i * TH:AMB + (i + 1) * TH]

        def ring(u):
            i = u % 8
            return amisc[:, i * TH:(i + 1) * TH]

        def sum_row(th, which):
            bank = {(0, 0): pb[4], (0, 1): pb[5], (1, 0): pb[6], (1, 1): pb[3]}[(th, which)]
            return bank[0:1, :]

        with nc.Block("init") as blk:
            s_ld = S("i_ld")

            @blk.sync
            def _(sync):
                for dst, src in [
                    (idt32[:, :], ident32.ap()), (idtbf[:, :], identbf.ap()),
                    (o_r_bf[:, :], ones_r_bf.ap()), (o_c_bf[:, :], ones_c_bf.ap()),
                    (o_r_32[:, :], ones_r_32.ap()),
                    (n1_r_bf[:, :], neg1_r_bf.ap()),
                    (scv[:, 0:NH], qsc), (scv[:, NH:NH + NKV], ksc),
                    (scv[:, NH + NKV:NH + 2 * NKV], vsc),
                    (scv[:, NH + 2 * NKV:2 * NH + 2 * NKV], osc),
                ]:
                    s_ld.inc(sync.dma_start(out=dst, in_=src), 16)
                sync.wait_ge(s_ld.h, s_ld.at(16 * 9))

            @blk.vector
            def _(vector):
                vector.memset(cnst[0:1, 0:1], EPS)
                vector.memset(cnst[0:1, 1:2], -0.5)

        def ln_phase(name, srcs, write_to):
            ns = len(srcs)
            with nc.Block(name) as blk:
                s_ld, s_st, s_ve, s_ac, s_mm = (S(f"{name}ld"), S(f"{name}st"), S(f"{name}ve"),
                                                S(f"{name}ac"), S(f"{name}mm"))
                vp = max(ns - 1, 0)
                xs = 4 if ns > 1 else 0

                @blk.sync
                def _(sync):
                    for c in range(DC):
                        if c >= 1:
                            sync.wait_ge(s_ac.h, s_ac.at(2 * c))
                            if vp:
                                sync.wait_ge(s_ve.h, s_ve.at(vp * c))
                        for j, src in enumerate(srcs):
                            s_ld.inc(sync.dma_start(out=s32(j), in_=src[c * P:(c + 1) * P, :]), 16)
                        if write_to is not None:
                            sync.wait_ge(s_ve.h, s_ve.at(vp * (c + 1)))
                            s_st.inc(sync.dma_start(out=write_to[c * P:(c + 1) * P, :], in_=s32(xs)), 16)
                    if write_to is not None:
                        sync.wait_ge(s_st.h, s_st.at(16 * DC))

                if vp:
                    @blk.vector
                    def _(vector):
                        for c in range(DC):
                            vector.wait_ge(s_ld.h, s_ld.at(16 * ns * (c + 1)))
                            if c >= 1:
                                vector.wait_ge(s_ac.h, s_ac.at(2 * c))
                                if write_to is not None:
                                    vector.wait_ge(s_st.h, s_st.at(16 * c))
                            s_ve.inc(vector.tensor_add(out=s32(xs), in0=s32(0), in1=s32(1)), 1)
                            for j in range(2, ns):
                                s_ve.inc(vector.tensor_add(out=s32(xs), in0=s32(xs), in1=s32(j)), 1)

                @blk.scalar
                def _(scalar):
                    for c in range(DC):
                        if vp:
                            scalar.wait_ge(s_ve.h, s_ve.at(vp * (c + 1)))
                        else:
                            scalar.wait_ge(s_ld.h, s_ld.at(16 * ns * (c + 1)))
                        if c >= 2:
                            scalar.wait_ge(s_mm.h, s_mm.at(4 * (c - 1)))
                        s_ac.inc(scalar.activation(out=xlnc(c), in_=s32(xs), func=AF.Copy), 1)
                        s_ac.inc(scalar.activation(out=sqv(c % 2), in_=s32(xs), func=AF.Square), 1)

                @blk.tensor
                def _(tensor):
                    for c in range(DC):
                        tensor.wait_ge(s_ac.h, s_ac.at(2 * (c + 1)))
                        for th in range(2):
                            s_mm.inc(tensor.matmul(sum_row(th, 0), o_c_bf[:, :], xlnc(c, th),
                                          start=(c == 0), stop=(c == DC - 1)), 1)
                            s_mm.inc(tensor.matmul(sum_row(th, 1), o_c_bf[:, :],
                                          sqv(c % 2)[:, th * TH:th * TH + TH],
                                          start=(c == 0), stop=(c == DC - 1)), 1)

            ln_fin(name + "f")

        def ln_fin(name):
            with nc.Block(name) as blk:
                s_ve, s_ac, s_mm = S(f"{name}ve"), S(f"{name}ac"), S(f"{name}mm")

                @blk.vector
                def _(vector):
                    for th in range(2):
                        mu = rows_mu[0:1, th * TH:(th + 1) * TH]
                        sc = rows_sc[0:1, th * TH:(th + 1) * TH]
                        rs = rows_rs[0:1, th * TH:(th + 1) * TH]
                        s_ve.inc(vector.tensor_scalar(out=mu, in0=sum_row(th, 0), scalar1=1.0 / D,
                                             scalar2=None, op0=ALU.mult), 1)
                        s_ve.inc(vector.tensor_scalar(out=sc, in0=sum_row(th, 1), scalar1=1.0 / D,
                                             scalar2=None, op0=ALU.mult), 1)
                        s_ve.inc(vector.tensor_mul(out=rs, in0=mu, in1=mu), 1)
                        s_ve.inc(vector.tensor_sub(out=sc, in0=sc, in1=rs), 1)
                    vector.wait_ge(s_mm.h, s_mm.at(4))
                    for c in range(DC):
                        for th in range(2):
                            mb = pb[th]
                            rb = pb[2 + th]
                            t1 = s32(0, th)
                            s_ve.inc(vector.tensor_sub(out=t1, in0=xlnc(c, th), in1=mb[:, :]), 1)
                            s_ve.inc(vector.tensor_mul(out=xlnc(c, th), in0=t1, in1=rb[:, :]), 1)

                @blk.scalar
                def _(scalar):
                    for th in range(2):
                        scalar.wait_ge(s_ve.h, s_ve.at(4 * (th + 1)))
                        sc = rows_sc[0:1, th * TH:(th + 1) * TH]
                        rs = rows_rs[0:1, th * TH:(th + 1) * TH]
                        s_ac.inc(scalar.activation(out=sc, in_=sc, func=AF.Ln, bias=cnst[0:1, 0:1]), 1)
                        s_ac.inc(scalar.activation(out=rs, in_=sc, func=AF.Exp, scale=cnst[0:1, 1:2]), 1)

                @blk.tensor
                def _(tensor):
                    for th in range(2):
                        tensor.wait_ge(s_ac.h, s_ac.at(2 * (th + 1)))
                        mu = rows_mu[0:1, th * TH:(th + 1) * TH]
                        rs = rows_rs[0:1, th * TH:(th + 1) * TH]
                        s_mm.inc(tensor.matmul(pb[th][:, :], o_r_32[:, :], mu, start=True, stop=True), 1)
                        s_mm.inc(tensor.matmul(pb[2 + th][:, :], o_r_32[:, :], rs, start=True, stop=True), 1)

        def ffn_phase(name):
            for t, (c0, c1) in enumerate(cfg.isplits):
                ncnk = c1 - c0
                with nc.Block(f"{name}g{t}") as blk:
                    s_wd, s_mm, s_ac, s_ve = (S(f"{name}g{t}wd"), S(f"{name}g{t}mm"),
                                              S(f"{name}g{t}ac"), S(f"{name}g{t}ve"))

                    @blk.gpsimd
                    def _(gpsimd):
                        for s in range(ncnk):
                            if s >= 2:
                                gpsimd.wait_ge(s_mm.h, s_mm.at(s - 1))
                            gs = c0 + s
                            s_wd.inc(gpsimd.dma_start(
                                out=wslot(s % 2, DC * P).rearrange("p (c j) -> p c j", j=P),
                                in_=wg_t[gs]), 16)
                            s_wd.inc(gpsimd.dma_start(
                                out=wslot(2 + s % 2, DC * P).rearrange("p (c j) -> p c j", j=P),
                                in_=wu_t[gs]), 16)

                    @blk.tensor
                    def _(tensor):
                        for s in range(ncnk):
                            tensor.wait_ge(s_wd.h, s_wd.at(32 * (s + 1)))
                            if s >= 1:
                                tensor.wait_ge(s_ac.h, s_ac.at(s))
                                tensor.wait_ge(s_ve.h, s_ve.at(s))
                            g = wslot(s % 2, DC * P)
                            u = wslot(2 + s % 2, DC * P)
                            for c in range(DC):
                                for th in range(2):
                                    tensor.matmul(pb[th][:, :], g[:, c * P:(c + 1) * P], xlnc(c, th),
                                                  start=(c == 0), stop=(c == DC - 1))
                                    m = tensor.matmul(pb[2 + th][:, :], u[:, c * P:(c + 1) * P], xlnc(c, th),
                                                      start=(c == 0), stop=(c == DC - 1))
                            s_mm.inc(m, 1)

                    @blk.scalar
                    def _(scalar):
                        for s in range(ncnk):
                            scalar.wait_ge(s_mm.h, s_mm.at(s + 1))
                            if s >= 1:
                                scalar.wait_ge(s_ve.h, s_ve.at(s))
                            for th in range(2):
                                a = scalar.activation(out=s32(0, th), in_=pb[th][:, :], func=AF.Silu)
                            s_ac.inc(a, 1)

                    @blk.vector
                    def _(vector):
                        for s in range(ncnk):
                            vector.wait_ge(s_ac.h, s_ac.at(s + 1))
                            for th in range(2):
                                v = vector.tensor_mul(out=bigc(s, th), in0=s32(0, th), in1=pb[2 + th][:, :])
                            s_ve.inc(v, 1)

                with nc.Block(f"{name}d{t}") as blk:
                    s_wd, s_mm, s_ac, s_st = (S(f"{name}d{t}wd"), S(f"{name}d{t}mm"),
                                              S(f"{name}d{t}ac"), S(f"{name}d{t}st"))

                    @blk.gpsimd
                    def _(gpsimd):
                        for o in range(DC):
                            if o >= 2:
                                gpsimd.wait_ge(s_mm.h, s_mm.at(o - 1))
                            s_wd.inc(gpsimd.dma_start(
                                out=wslot(o % 2, ncnk * P).rearrange("p (c j) -> p c j", j=P),
                                in_=wd_t[o][:, c0:c1, :]), 16)

                    @blk.tensor
                    def _(tensor):
                        for o in range(DC):
                            tensor.wait_ge(s_wd.h, s_wd.at(16 * (o + 1)))
                            if o >= 1:
                                tensor.wait_ge(s_ac.h, s_ac.at(o))
                            w = wslot(o % 2, ncnk * P)
                            for c in range(ncnk):
                                for th in range(2):
                                    m = tensor.matmul(pb[4 + th][:, :], w[:, c * P:(c + 1) * P], bigc(c, th),
                                                      start=(c == 0), stop=(c == ncnk - 1))
                            s_mm.inc(m, 1)

                    @blk.scalar
                    def _(scalar):
                        for o in range(DC):
                            scalar.wait_ge(s_mm.h, s_mm.at(o + 1))
                            if o >= 2:
                                scalar.wait_ge(s_st.h, s_st.at(16 * (o - 1)))
                            for th in range(2):
                                a = scalar.activation(out=s32(2 + o % 2, th), in_=pb[4 + th][:, :],
                                                      func=AF.Copy, scale=0.5)
                            s_ac.inc(a, 1)

                    @blk.sync
                    def _(sync):
                        for o in range(DC):
                            sync.wait_ge(s_ac.h, s_ac.at(o + 1))
                            s_st.inc(sync.dma_start(out=part[t][o * P:(o + 1) * P, :],
                                           in_=s32(2 + o % 2)), 16)
                        sync.wait_ge(s_st.h, s_st.at(16 * DC))

        def bitnet_proj(name, w_ap, n_o, scol_off, extra_scale, dest,
                        kv_store=None, v_transpose=False, pre_gpsimd=None):
            with nc.Block(name) as blk:
                s_wd, s_ab, s_sg = S(f"{name}wd"), S(f"{name}ab"), S(f"{name}sg")
                s_sm, s_cl, s_tp = S(f"{name}sm"), S(f"{name}cl"), S(f"{name}tp")
                s_se, s_pm, s_ev = S(f"{name}se"), S(f"{name}pm"), S(f"{name}ev")
                s_st, s_vt, s_vc = S(f"{name}st"), S(f"{name}vt"), S(f"{name}vc")

                @blk.gpsimd
                def _(gpsimd):
                    if pre_gpsimd is not None:
                        pre_gpsimd(gpsimd)
                    for o in range(n_o):
                        if o >= 2:
                            gpsimd.wait_ge(s_pm.h, s_pm.at(o - 1))
                        s_wd.inc(gpsimd.dma_start(
                            out=wslot(o % 2, DC * P).rearrange("p (c j) -> p c j", j=P),
                            in_=w_ap[o]), 16)

                @blk.scalar
                def _(scalar):
                    for o in range(n_o):
                        scalar.wait_ge(s_wd.h, s_wd.at(16 * (o + 1)))
                        if o >= 1:
                            scalar.wait_ge(s_sm.h, s_sm.at(o))
                        w = wslot(o % 2, DC * P)
                        s_ab.inc(scalar.activation(out=absb, in_=w, func=AF.Abs), 1)
                        s_sg.inc(scalar.activation(out=w, in_=w, func=AF.Sign), 1)
                        scalar.wait_ge(s_pm.h, s_pm.at(o + 1))
                        scalar.wait_ge(s_se.h, s_se.at(o + 1))
                        if kv_store is not None and o >= 2:
                            scalar.wait_ge(s_st.h, s_st.at(16 * (o - 1)))
                        sc = seff[:, o % 2:o % 2 + 1]
                        for th in range(2):
                            if kv_store is None:
                                dst = dest(o, th)
                            elif v_transpose:
                                dst = vst[:, th * TH:(th + 1) * TH]
                            else:
                                dst = kst[:, th * TH:(th + 1) * TH]
                            a = scalar.activation(out=dst, in_=pb[th][:, :], func=AF.Copy, scale=sc)
                        s_ev.inc(a, 1)
                        if v_transpose:
                            scalar.wait_ge(s_vt.h, s_vt.at(o + 1))
                            if o >= 1:
                                scalar.wait_ge(s_st.h, s_st.at(16 * o))
                            s_vc.inc(scalar.activation(out=vst2, in_=pbf[:, 0:T], func=AF.Copy), 1)

                @blk.tensor
                def _(tensor):
                    for o in range(n_o):
                        tensor.wait_ge(s_ab.h, s_ab.at(o + 1))
                        if o >= 1:
                            tensor.wait_ge(s_cl.h, s_cl.at(o))
                        for c in range(DC):
                            m = tensor.matmul(pb[6][0:1, 0:P], o_c_bf[:, :], absb[:, c * P:(c + 1) * P],
                                              start=(c == 0), stop=(c == DC - 1))
                        s_sm.inc(m, 1)
                        tensor.wait_ge(s_cl.h, s_cl.at(o + 1))
                        if o >= 1:
                            tensor.wait_ge(s_se.h, s_se.at(o))
                        s_tp.inc(tensor.transpose(pb[6][:, 4:5], rows_sc[0:1, 0:P], idt32[0:1, 0:1]), 1)
                        tensor.wait_ge(s_sg.h, s_sg.at(o + 1))
                        if o >= 1:
                            tensor.wait_ge(s_ev.h, s_ev.at(o))
                        w = wslot(o % 2, DC * P)
                        for c in range(DC):
                            for th in range(2):
                                m = tensor.matmul(pb[th][:, :], w[:, c * P:(c + 1) * P], xlnc(c, th),
                                                  start=(c == 0), stop=(c == DC - 1))
                        s_pm.inc(m, 1)
                        if v_transpose:
                            tensor.wait_ge(s_ev.h, s_ev.at(o + 1))
                            if o >= 1:
                                tensor.wait_ge(s_vc.h, s_vc.at(o))
                            for tt in range(TN):
                                m = tensor.transpose(pbf[:, tt * P:(tt + 1) * P],
                                                     vst[:, tt * P:(tt + 1) * P],
                                                     idtbf[:, :])
                            s_vt.inc(m, 1)

                @blk.vector
                def _(vector):
                    for o in range(n_o):
                        vector.wait_ge(s_sm.h, s_sm.at(o + 1))
                        s_cl.inc(vector.tensor_scalar(out=rows_sc[0:1, 0:P], in0=pb[6][0:1, 0:P],
                                             scalar1=1.0 / D, scalar2=1e-8,
                                             op0=ALU.mult, op1=ALU.max), 1)
                        vector.wait_ge(s_tp.h, s_tp.at(o + 1))
                        if o >= 1:
                            vector.wait_ge(s_ev.h, s_ev.at(o))
                        s_se.inc(vector.scalar_tensor_tensor(out=seff[:, o % 2:o % 2 + 1], in0=pb[6][:, 4:5],
                                                    scalar=extra_scale,
                                                    in1=scv[:, scol_off + o:scol_off + o + 1],
                                                    op0=ALU.mult, op1=ALU.mult), 1)

                if kv_store is not None:
                    @blk.sync
                    def _(sync):
                        for o in range(n_o):
                            if not v_transpose:
                                sync.wait_ge(s_ev.h, s_ev.at(o + 1))
                                s_st.inc(sync.dma_start(out=cc_in.ap()[o * P:(o + 1) * P, :],
                                               in_=kst), 16)
                            else:
                                sync.wait_ge(s_vc.h, s_vc.at(o + 1))
                                s_st.inc(sync.dma_start(
                                    out=cc_in.ap()[T:2 * T, o * P:(o + 1) * P]
                                    .rearrange("(tt p) j -> p tt j", p=P),
                                    in_=vst2.rearrange("p (tt j) -> p tt j", j=P)), 16)
                        sync.wait_ge(s_st.h, s_st.at(16 * n_o))

        def attn_phase(name, s_cc_handle):
            with nc.Block(name) as blk:
                s_kv, s_rk, s_mr = S(f"{name}kv"), S(f"{name}rk"), S(f"{name}mr")
                s_nk, s_sq = S(f"{name}nk"), S(f"{name}sq")
                s_sb, s_ex, s_av = S(f"{name}sb"), S(f"{name}ex"), S(f"{name}av")
                s_tn, s_rc, s_ib = S(f"{name}tn"), S(f"{name}rc"), S(f"{name}ib")
                s_iv, s_ao = S(f"{name}iv"), S(f"{name}ao")
                UPH = GR * 2
                NU = NKV * UPH
                NKB = KEYS // TH
                NPH = NKB + UPH

                def unit_of(h, uh):
                    u = h * UPH + uh
                    return u, h * GR + uh // 2, uh % 2

                def mrow(u):
                    return rowbf[0:1, (u % 8) * TH:(u % 8) * TH + TH]

                def nrow_bank(i):
                    return (pb[5] if i % 2 == 0 else pb[6])[0:1, :]

                psB = [pb[2], pb[3], pb[6]]
                sqK = amisc[:, 0:KEYS]
                sqQ = amisc[:, AMB + T:AMB + 2 * T]
                rkp = a32[0:1, 0:8]
                rk2 = a32[0:1, 8:9]
                lnt = a32[0:1, T:T + TH]

                def invb(u):
                    return s32(2, u % 2)

                def mrow_inv(u):
                    return rows_mu[0:1, (u % 2) * TH:(u % 2) * TH + TH]

                @blk.sync
                def _(sync):
                    sync.wait_ge(s_cc_handle.h, s_cc_handle.at(1))
                    for h in range(NKV):
                        if h >= 2:
                            sync.wait_ge(s_tn.h, s_tn.at(UPH * (h - 1)))
                        kslot = wslot(h % 2, KEYS)
                        vslot = wslot(2 + h % 2, KEYS)
                        for r in range(R):
                            s_kv.inc(sync.dma_start(
                                out=kslot[:, r * T:(r + 1) * T],
                                in_=cc_out.ap()[r * 2 * T + h * P: r * 2 * T + h * P + P, :]), 16)
                            s_kv.inc(sync.dma_start(
                                out=vslot.rearrange("p (kt j) -> p kt j", j=P)[:, r * TN:(r + 1) * TN, :],
                                in_=cc_out.ap()[r * 2 * T + T: r * 2 * T + 2 * T, h * P:(h + 1) * P]
                                .rearrange("(kt p) j -> p kt j", p=P)), 16)

                @blk.scalar
                def _(scalar):
                    for h in range(NKV):
                        scalar.wait_ge(s_kv.h, s_kv.at(16 * 2 * R * (h + 1)))
                        if h >= 1:
                            scalar.wait_ge(s_nk.h, s_nk.at(NPH * h - UPH))
                        s_sq.inc(scalar.activation(out=sqK, in_=wslot(h % 2, KEYS), func=AF.Square), 1)
                        for uh in range(UPH):
                            u, g, qt = unit_of(h, uh)
                            if uh % 2 == 0:
                                if u >= 1:
                                    scalar.wait_ge(s_nk.h, s_nk.at(NKB * (h + 1) + UPH * h + uh))
                                s_sq.inc(scalar.activation(out=sqQ[:, 0:T], in_=bigc(g), func=AF.Square), 1)
                            scalar.wait_ge(s_nk.h, s_nk.at(NKB * (h + 1) + UPH * h + uh + 1))
                            scalar.wait_ge(s_rk.h, s_rk.at((NKB + 1) * (h + 1)))
                            scalar.activation(out=lnt, in_=nrow_bank(NKB + uh), func=AF.Ln,
                                              scale=rk2)
                            s_mr.inc(scalar.activation(out=mrow(u), in_=lnt, func=AF.Exp,
                                                       scale=0.5), 1)
                        for uh in range(UPH):
                            u, g, qt = unit_of(h, uh)
                            for kt in range(KTN):
                                gi = u * KTN + kt
                                scalar.wait_ge(s_sb.h, s_sb.at(gi + 1))
                                if gi >= 3:
                                    scalar.wait_ge(s_av.h, s_av.at(gi - 2))
                                s_ex.inc(scalar.activation(out=psb_tile(kt % 3),
                                                           in_=psB[kt % 3][:, :], func=AF.Exp), 1)
                            scalar.wait_ge(s_ib.h, s_ib.at(u + 1))
                            if u >= 2:
                                scalar.wait_ge(s_ao.h, s_ao.at(u - 1))
                            s_iv.inc(scalar.activation(out=invb(u), in_=pb[5][:, :],
                                                       func=AF.Copy), 1)

                @blk.tensor
                def _(tensor):
                    for h in range(NKV):
                        kslot = wslot(h % 2, KEYS)
                        vslot = wslot(2 + h % 2, KEYS)
                        tensor.wait_ge(s_sq.h, s_sq.at((1 + GR) * h + 1))
                        if h >= 1:
                            tensor.wait_ge(s_iv.h, s_iv.at(UPH * h))
                        for kb in range(NKB):
                            i = NPH * h + kb
                            if kb >= 2:
                                tensor.wait_ge(s_rk.h, s_rk.at((NKB + 1) * h + kb - 1))
                            s_nk.inc(tensor.matmul(nrow_bank(kb), o_c_bf[:, :],
                                                   sqK[:, kb * TH:(kb + 1) * TH],
                                                   start=True, stop=True), 1)
                        for uh in range(UPH):
                            u, g, qt = unit_of(h, uh)
                            if uh % 2 == 0:
                                tensor.wait_ge(s_sq.h, s_sq.at((1 + GR) * h + 2 + uh // 2))
                            if uh < 2:
                                tensor.wait_ge(s_rk.h, s_rk.at((NKB + 1) * h + NKB - 1 + uh))
                            else:
                                tensor.wait_ge(s_mr.h, s_mr.at(u - 1))
                            s_nk.inc(tensor.matmul(nrow_bank(NKB + uh), o_c_bf[:, :],
                                                   sqQ[:, qt * TH:(qt + 1) * TH],
                                                   start=True, stop=True), 1)
                        for uh in range(UPH):
                            u, g, qt = unit_of(h, uh)
                            rhs_q = bigc(g, qt)
                            for kt in range(KTN):
                                gi = u * KTN + kt
                                if gi >= 3:
                                    tensor.wait_ge(s_ex.h, s_ex.at(gi - 2))
                                if kt == 0:
                                    tensor.wait_ge(s_mr.h, s_mr.at(UPH * (h + 1)))
                                tensor.matmul(psB[kt % 3][:, :], n1_r_bf[:, :], mrow(u),
                                              start=True, stop=False)
                                s_sb.inc(tensor.matmul(psB[kt % 3][:, :], kslot[:, kt * P:(kt + 1) * P],
                                                       rhs_q, start=False, stop=True), 1)
                                if kt >= 2:
                                    kk = kt - 2
                                    tensor.wait_ge(s_ex.h, s_ex.at(gi - 1))
                                    if kt == 2:
                                        tensor.wait_ge(s_ao.h, s_ao.at(u))
                                        tensor.wait_ge(s_iv.h, s_iv.at(u))
                                    s_av.inc(tensor.matmul(pb[4][:, :], vslot[:, kk * P:(kk + 1) * P],
                                                           psb_tile(kk % 3), start=(kk == 0),
                                                           stop=False), 1)
                                    tensor.matmul(pb[5][0:1, :], o_c_bf[:, :], psb_tile(kk % 3),
                                                  start=(kk == 0), stop=False)
                            for kk in (KTN - 2, KTN - 1):
                                tensor.wait_ge(s_ex.h, s_ex.at(u * KTN + kk + 1))
                                s_av.inc(tensor.matmul(pb[4][:, :], vslot[:, kk * P:(kk + 1) * P],
                                                       psb_tile(kk % 3), start=False,
                                                       stop=(kk == KTN - 1)), 1)
                                m = tensor.matmul(pb[5][0:1, :], o_c_bf[:, :], psb_tile(kk % 3),
                                                  start=False, stop=(kk == KTN - 1))
                            s_tn.inc(m, 1)
                            tensor.wait_ge(s_rc.h, s_rc.at(u + 1))
                            s_ib.inc(tensor.matmul(pb[5][:, :], o_r_32[:, :],
                                                   mrow_inv(u), start=True, stop=True), 1)

                @blk.vector
                def _(vector):
                    for h in range(NKV):
                        for kb in range(NKB):
                            vector.wait_ge(s_nk.h, s_nk.at(NPH * h + kb + 1))
                            s_rk.inc(vector.reduce_max(out=rkp[0:1, kb:kb + 1],
                                                       in_=nrow_bank(kb), axis=AX.X), 1)
                        s_rk.inc(vector.reduce_max(out=rk2, in_=rkp, axis=AX.X), 1)
                        for uh in range(UPH):
                            u, g, qt = unit_of(h, uh)
                            vector.wait_ge(s_tn.h, s_tn.at(u + 1))
                            if u >= 1:
                                vector.wait_ge(s_ib.h, s_ib.at(u - 1))
                            s_rc.inc(vector.reciprocal(out=mrow_inv(u), in_=pb[5][0:1, :]), 1)
                            vector.wait_ge(s_iv.h, s_iv.at(u + 1))
                            s_ao.inc(vector.tensor_mul(out=bigc(g, qt), in0=pb[4][:, :],
                                                       in1=invb(u)), 1)

        def oproj_phase(name):
            with nc.Block(name) as blk:
                s_wd, s_ab, s_sg = S(f"{name}wd"), S(f"{name}ab"), S(f"{name}sg")
                s_sm, s_cl, s_tp = S(f"{name}sm"), S(f"{name}cl"), S(f"{name}tp")
                s_se, s_pm, s_ev = S(f"{name}se"), S(f"{name}pm"), S(f"{name}ev")
                s_ld, s_ve, s_st = S(f"{name}ld"), S(f"{name}ve"), S(f"{name}st")
                s_ac2, s_mm2 = S(f"{name}ac2"), S(f"{name}mm2")

                @blk.gpsimd
                def _(gpsimd):
                    for o in range(DC):
                        if o >= 2:
                            gpsimd.wait_ge(s_pm.h, s_pm.at(o - 1))
                        s_wd.inc(gpsimd.dma_start(
                            out=wslot(o % 2, DC * P).rearrange("p (c j) -> p c j", j=P),
                            in_=wo_t[o]), 16)

                @blk.scalar
                def _(scalar):
                    for o in range(DC):
                        scalar.wait_ge(s_wd.h, s_wd.at(16 * (o + 1)))
                        if o >= 1:
                            scalar.wait_ge(s_sm.h, s_sm.at(o))
                        w = wslot(o % 2, DC * P)
                        s_ab.inc(scalar.activation(out=absb, in_=w, func=AF.Abs), 1)
                        s_sg.inc(scalar.activation(out=w, in_=w, func=AF.Sign), 1)
                        scalar.wait_ge(s_pm.h, s_pm.at(o + 1))
                        scalar.wait_ge(s_se.h, s_se.at(o + 1))
                        if o >= 1:
                            scalar.wait_ge(s_ve.h, s_ve.at(2 * o))
                        sc = seff[:, o % 2:o % 2 + 1]
                        for th in range(2):
                            a = scalar.activation(out=s32(2, th), in_=pb[th][:, :], func=AF.Copy, scale=sc)
                        s_ev.inc(a, 1)
                        scalar.wait_ge(s_ve.h, s_ve.at(2 * (o + 1)))
                        if o >= 2:
                            scalar.wait_ge(s_mm2.h, s_mm2.at(4 * (o - 1)))
                        s_ac2.inc(scalar.activation(out=xlnc(o), in_=s32(4), func=AF.Copy), 1)
                        s_ac2.inc(scalar.activation(out=(kst if o % 2 == 0 else vst),
                                                    in_=s32(4), func=AF.Square), 1)

                @blk.tensor
                def _(tensor):
                    for o in range(DC):
                        tensor.wait_ge(s_ab.h, s_ab.at(o + 1))
                        if o >= 1:
                            tensor.wait_ge(s_cl.h, s_cl.at(o))
                        for c in range(DC):
                            m = tensor.matmul(pb[2][0:1, 0:P], o_c_bf[:, :], absb[:, c * P:(c + 1) * P],
                                              start=(c == 0), stop=(c == DC - 1))
                        s_sm.inc(m, 1)
                        tensor.wait_ge(s_cl.h, s_cl.at(o + 1))
                        if o >= 1:
                            tensor.wait_ge(s_se.h, s_se.at(o))
                        s_tp.inc(tensor.transpose(pb[2][:, 4:5], rows_sc[0:1, 0:P], idt32[0:1, 0:1]), 1)
                        tensor.wait_ge(s_sg.h, s_sg.at(o + 1))
                        if o >= 1:
                            tensor.wait_ge(s_ev.h, s_ev.at(o))
                        w = wslot(o % 2, DC * P)
                        for c in range(DC):
                            for th in range(2):
                                m = tensor.matmul(pb[th][:, :], w[:, c * P:(c + 1) * P], bigc(c, th),
                                                  start=(c == 0), stop=(c == DC - 1))
                        s_pm.inc(m, 1)
                        tensor.wait_ge(s_ac2.h, s_ac2.at(2 * (o + 1)))
                        sq_t = kst if o % 2 == 0 else vst
                        for th in range(2):
                            s_mm2.inc(tensor.matmul(sum_row(th, 0), o_c_bf[:, :], xlnc(o, th),
                                          start=(o == 0), stop=(o == DC - 1)), 1)
                            s_mm2.inc(tensor.matmul(sum_row(th, 1), o_c_bf[:, :],
                                          sq_t[:, th * TH:th * TH + TH],
                                          start=(o == 0), stop=(o == DC - 1)), 1)

                @blk.vector
                def _(vector):
                    for o in range(DC):
                        vector.wait_ge(s_sm.h, s_sm.at(o + 1))
                        s_cl.inc(vector.tensor_scalar(out=rows_sc[0:1, 0:P], in0=pb[2][0:1, 0:P],
                                             scalar1=1.0 / D, scalar2=1e-8,
                                             op0=ALU.mult, op1=ALU.max), 1)
                        vector.wait_ge(s_tp.h, s_tp.at(o + 1))
                        if o >= 1:
                            vector.wait_ge(s_ev.h, s_ev.at(o))
                        s_se.inc(vector.scalar_tensor_tensor(out=seff[:, o % 2:o % 2 + 1], in0=pb[2][:, 4:5],
                                                    scalar=1.0,
                                                    in1=scv[:, NH + 2 * NKV + o:NH + 2 * NKV + o + 1],
                                                    op0=ALU.mult, op1=ALU.mult), 1)
                        vector.wait_ge(s_ev.h, s_ev.at(o + 1))
                        vector.wait_ge(s_ld.h, s_ld.at(16 * (o + 1)))
                        if o >= 1:
                            vector.wait_ge(s_st.h, s_st.at(16 * o))
                            vector.wait_ge(s_ac2.h, s_ac2.at(2 * o))
                        s_ve.inc(vector.tensor_add(out=s32(4), in0=s32(2), in1=s32(3)), 2)

                @blk.sync
                def _(sync):
                    for o in range(DC):
                        if o >= 1:
                            sync.wait_ge(s_ve.h, s_ve.at(2 * o))
                        s_ld.inc(sync.dma_start(out=s32(3), in_=xc2[o * P:(o + 1) * P, :]), 16)
                        sync.wait_ge(s_ve.h, s_ve.at(2 * (o + 1)))
                        s_st.inc(sync.dma_start(out=xc3[o * P:(o + 1) * P, :], in_=s32(4)), 16)
                    sync.wait_ge(s_st.h, s_st.at(16 * DC))

            ln_fin(name + "f")

        ln_phase("ln1", [xT], None)
        ffn_phase("f1")
        ln_phase("ln2", [xT, part[0], part[1], part[2]], xc2)

        s_cc = PSem(es.enter_context(nc.semaphore("cc")), 0)

        bitnet_proj("kp", wk_t, NKV, NH, 1.0, None, kv_store="k")
        bitnet_proj("vp", wv_t, NKV, NH + NKV, 1.0, None, kv_store="v", v_transpose=True)

        def q_pre(gpsimd):
            s_cc.inc(gpsimd.collective_compute(
                "AllGather", ALU.bypass,
                replica_groups=[[0, 1, 2, 3], [4, 5, 6, 7]],
                ins=[cc_in.ap().opt()],
                outs=[cc_out.ap().opt()],
            ), 1)

        bitnet_proj("qp", wq_t, NH, 0, 1.0 / math.sqrt(P), lambda o, th: bigc(o, th),
                    pre_gpsimd=q_pre)
        attn_phase("at", s_cc)
        oproj_phase("op")
        ffn_phase("f2")

        dbg = {}
        dbg_xln = None
        if debug:
            for nm, src, shp, dt in [
                ("dbg_xc2", xc2, [D, T], F32), ("dbg_xc3", xc3, [D, T], F32),
                ("dbg_p0", part[0], [D, T], F32), ("dbg_p1", part[1], [D, T], F32),
                ("dbg_p2", part[2], [D, T], F32),
                ("dbg_ccin", cc_in.ap(), [2 * T, KV], BF16),
                ("dbg_ccout", cc_out.ap(), [2 * R * T, KV], BF16),
            ]:
                dbg[nm] = (nc.dram_tensor(nm, shp, dt, kind="ExternalOutput").ap(), src)
            dbg_xln = nc.dram_tensor("dbg_xln", [P, DC * T], BF16, kind="ExternalOutput").ap()
            with nc.Block("dbg") as blk:
                s_dst = S("dbgst")

                @blk.sync
                def _(sync):
                    n = 0
                    for nm, (dst, src) in dbg.items():
                        s_dst.inc(sync.dma_start(out=dst, in_=src), 16)
                        n += 1
                    s_dst.inc(sync.dma_start(out=dbg_xln, in_=xln[:, :]), 16)
                    sync.wait_ge(s_dst.h, s_dst.at(16 * (n + 1)))

        with nc.Block("fin") as blk:
            s_ld, s_ve, s_st = S("finld"), S("finve"), S("finst")

            @blk.sync
            def _(sync):
                for c in range(DC):
                    if c >= 1:
                        sync.wait_ge(s_ve.h, s_ve.at(3 * c))
                    for j, src in enumerate([xc3, part[0], part[1], part[2]]):
                        s_ld.inc(sync.dma_start(out=s32(j), in_=src[c * P:(c + 1) * P, :]), 16)
                    sync.wait_ge(s_ve.h, s_ve.at(3 * (c + 1)))
                    s_st.inc(sync.dma_start(out=yT[c * P:(c + 1) * P, :], in_=s32(4)), 16)
                sync.wait_ge(s_st.h, s_st.at(16 * DC))

            @blk.vector
            def _(vector):
                for c in range(DC):
                    vector.wait_ge(s_ld.h, s_ld.at(16 * 4 * (c + 1)))
                    if c >= 1:
                        vector.wait_ge(s_st.h, s_st.at(16 * c))
                    s_ve.inc(vector.tensor_add(out=s32(4), in0=s32(0), in1=s32(1)), 1)
                    s_ve.inc(vector.tensor_add(out=s32(4), in0=s32(4), in1=s32(2)), 1)
                    s_ve.inc(vector.tensor_add(out=s32(4), in0=s32(4), in1=s32(3)), 1)

    return nc


def _tile_T(w):
    O, IC_ = w.shape
    return np.ascontiguousarray(w.T.reshape(IC_ // P, P, O // P, P).transpose(2, 1, 0, 3))


def prep_inputs(cfg: Cfg, inp):
    T, I, R = cfg.T, cfg.I, cfg.R
    x = inp["L0_attn_hidden_states"]
    wgu = inp["L0_ffn_w_gate_up"]

    shared = {
        "wg_t": _tile_T(wgu[:I]),
        "wu_t": _tile_T(wgu[I:]),
        "wd_t": _tile_T(inp["L0_ffn_w_down"]),
        "wq_t": _tile_T(inp["L0_attn_q_weight"]),
        "wk_t": _tile_T(inp["L0_attn_k_weight"]),
        "wv_t": _tile_T(inp["L0_attn_v_weight"]),
        "wo_t": _tile_T(inp["L0_attn_o_weight"]),
        "qsc": np.ascontiguousarray(inp["L0_attn_q_scale"].reshape(-1, P).T),
        "ksc": np.ascontiguousarray(inp["L0_attn_k_scale"].reshape(-1, P).T),
        "vsc": np.ascontiguousarray(inp["L0_attn_v_scale"].reshape(-1, P).T),
        "osc": np.ascontiguousarray(inp["L0_attn_o_scale"].reshape(-1, P).T),
    }
    in_maps = []
    for c in range(8):
        b, q = c // R, c % R
        m = dict(shared)
        m["xT"] = np.ascontiguousarray(x[b, q * T:(q + 1) * T, :].T)
        in_maps.append(m)
    return in_maps


def kernel(**inputs):
    cfg = CFG
    in_maps = prep_inputs(cfg, inputs)
    nc = build_program(cfg)
    res = run_bass_kernel_spmd(nc, in_maps, core_ids=list(range(8)))
    B = 2
    y = np.empty((B, cfg.R * cfg.T, cfg.D), np.float32)
    for c in range(8):
        b, q = c // cfg.R, c % cfg.R
        y[b, q * cfg.T:(q + 1) * cfg.T, :] = res.results[c]["yT"].T
    return y


# version 11
# speedup vs baseline: 1.2983x; 1.1140x over previous
import math
from contextlib import ExitStack
from dataclasses import dataclass

import numpy as np
import ml_dtypes

import concourse.bass as bass
import concourse.mybir as mybir
from concourse.bass_utils import run_bass_kernel_spmd

F32 = mybir.dt.float32
BF16 = mybir.dt.bfloat16
AF = mybir.ActivationFunctionType
ALU = mybir.AluOpType
AX = mybir.AxisListType

P = 128
EPS = 1e-5


@dataclass(frozen=True)
class Cfg:
    D: int = 4096
    T: int = 1024
    I: int = 11008
    KV: int = 1024
    R: int = 4

    @property
    def DC(self):
        return self.D // P

    @property
    def IC(self):
        return self.I // P

    @property
    def NH(self):
        return self.D // P

    @property
    def NKV(self):
        return self.KV // P

    @property
    def TH(self):
        return self.T // 2

    @property
    def KEYS(self):
        return self.R * self.T

    @property
    def KTN(self):
        return self.KEYS // P

    @property
    def GR(self):
        return self.NH // self.NKV

    @property
    def isplits(self):
        n = self.IC
        a = (n + 2) // 3
        b = (n - a + 1) // 2
        return [(0, a), (a, a + b), (a + b, n)]


CFG = Cfg()


class PSem:

    def __init__(self, h, start):
        self.h = h
        self.n = start
        self.base = start

    def inc(self, instr, k=1):
        instr.then_inc(self.h, k)
        self.n += k
        return instr

    def at(self, v):
        return self.base + v


def build_program(cfg: Cfg, debug=False):
    nc = bass.Bass("TRN2", target_bir_lowering=False, detect_race_conditions=False)
    D, T, I, KV = cfg.D, cfg.T, cfg.I, cfg.KV
    DC, IC, NH, NKV, TH, R = cfg.DC, cfg.IC, cfg.NH, cfg.NKV, cfg.TH, cfg.R
    KTN, GR, KEYS = cfg.KTN, cfg.GR, cfg.KEYS
    TB = TH // P
    TN = T // P
    assert T == KV, "kernel assumes tokens-per-core == kv width"

    xT = nc.dram_tensor("xT", [D, T], F32, kind="ExternalInput").ap()
    wg_t = nc.dram_tensor("wg_t", [IC, P, DC, P], F32, kind="ExternalInput").ap()
    wu_t = nc.dram_tensor("wu_t", [IC, P, DC, P], F32, kind="ExternalInput").ap()
    wd_t = nc.dram_tensor("wd_t", [DC, P, IC, P], F32, kind="ExternalInput").ap()
    wq_t = nc.dram_tensor("wq_t", [NH, P, DC, P], F32, kind="ExternalInput").ap()
    wk_t = nc.dram_tensor("wk_t", [NKV, P, DC, P], F32, kind="ExternalInput").ap()
    wv_t = nc.dram_tensor("wv_t", [NKV, P, DC, P], F32, kind="ExternalInput").ap()
    wo_t = nc.dram_tensor("wo_t", [NH, P, DC, P], F32, kind="ExternalInput").ap()
    qsc = nc.dram_tensor("qsc", [P, NH], F32, kind="ExternalInput").ap()
    ksc = nc.dram_tensor("ksc", [P, NKV], F32, kind="ExternalInput").ap()
    vsc = nc.dram_tensor("vsc", [P, NKV], F32, kind="ExternalInput").ap()
    osc = nc.dram_tensor("osc", [P, NH], F32, kind="ExternalInput").ap()
    yT = nc.dram_tensor("yT", [D, T], F32, kind="ExternalOutput").ap()

    xc2 = nc.dram_tensor("xc2", [D, T], F32, kind="Internal").ap()
    xc3 = nc.dram_tensor("xc3", [D, T], F32, kind="Internal").ap()
    part = [nc.dram_tensor(f"part{i}", [D, T], F32, kind="Internal").ap() for i in range(3)]
    cc_in = nc.dram_tensor("cc_in", [2 * T, KV], BF16, kind="Internal")
    cc_out = nc.dram_tensor("cc_out", [2 * R * T, KV], BF16, kind="Internal")

    ident32 = nc.inline_tensor(np.eye(P, dtype=np.float32), name="ident32")
    identbf = nc.inline_tensor(np.eye(P, dtype=ml_dtypes.bfloat16), name="identbf")
    ones_r_bf = nc.inline_tensor(np.ones((1, P), dtype=ml_dtypes.bfloat16), name="ones_r_bf")
    ones_c_bf = nc.inline_tensor(np.ones((P, 1), dtype=ml_dtypes.bfloat16), name="ones_c_bf")
    ones_r_32 = nc.inline_tensor(np.ones((1, P), dtype=np.float32), name="ones_r_32")
    neg1_r_bf = nc.inline_tensor(np.full((1, P), -1.0, dtype=ml_dtypes.bfloat16), name="neg1_r_bf")
    neg1_r_32 = nc.inline_tensor(np.full((1, P), -1.0, dtype=np.float32), name="neg1_r_32")

    WSL = max(DC * P, KEYS)
    HMX = max(NH, max(b - a for a, b in cfg.isplits))

    es = ExitStack()
    with es:
        xln = es.enter_context(nc.sbuf_tensor("xln", [P, DC * T], BF16))
        big = es.enter_context(nc.sbuf_tensor("big", [P, HMX * T], BF16))
        aw = es.enter_context(nc.sbuf_tensor("aw", [P, 4 * WSL], BF16))
        a32 = es.enter_context(nc.sbuf_tensor("a32", [P, 5 * T], F32))
        AMB = max(DC * P, 8 * TH)
        amisc = es.enter_context(nc.sbuf_tensor("amisc", [P, AMB + 2 * T], BF16))
        rows_mu = es.enter_context(nc.sbuf_tensor("rows_mu", [1, T], F32))
        rowbf = es.enter_context(nc.sbuf_tensor("rowbf", [1, 8 * TH], BF16))
        m4sb = es.enter_context(nc.sbuf_tensor("m4sb", [P, 8], F32))
        seff = es.enter_context(nc.sbuf_tensor("seff", [P, 2], F32))
        scv = es.enter_context(nc.sbuf_tensor("scv", [P, 2 * NH + 2 * NKV], F32))
        idt32 = es.enter_context(nc.sbuf_tensor("idt32", [P, P], F32))
        idtbf = es.enter_context(nc.sbuf_tensor("idtbf", [P, P], BF16))
        o_r_bf = es.enter_context(nc.sbuf_tensor("o_r_bf", [1, P], BF16))
        o_c_bf = es.enter_context(nc.sbuf_tensor("o_c_bf", [P, 1], BF16))
        o_r_32 = es.enter_context(nc.sbuf_tensor("o_r_32", [1, P], F32))
        n1_r_bf = es.enter_context(nc.sbuf_tensor("n1_r_bf", [1, P], BF16))
        n1_r_32 = es.enter_context(nc.sbuf_tensor("n1_r_32", [1, P], F32))
        nbias = es.enter_context(nc.sbuf_tensor("nbias", [P, 2], F32))
        cnst = es.enter_context(nc.sbuf_tensor("cnst", [1, 2], F32))

        pb = [es.enter_context(nc.psum_tensor(f"pb{i}", [P, TH], F32)) for i in range(7)]
        pbf = es.enter_context(nc.psum_tensor("pbf", [P, 2 * TH], BF16))

        _pools = {
            "gdma": [es.enter_context(nc.semaphore(f"spg{i}")) for i in range(3)],
            "sdma": [es.enter_context(nc.semaphore(f"sps{i}")) for i in range(6)],
            "eng": [es.enter_context(nc.semaphore(f"spe{i}")) for i in range(16)],
        }
        _pool_ps = {}
        _pool_i = {"gdma": 0, "sdma": 0, "eng": 0}

        def S(name):
            if name.endswith("wd"):
                kind = "gdma"
            elif name.endswith(("ld", "st", "kv")):
                kind = "sdma"
            else:
                kind = "eng"
            pool = _pools[kind]
            i = _pool_i[kind] % len(pool)
            _pool_i[kind] += 1
            h = pool[i]
            prev = _pool_ps.get((kind, i))
            start = prev.n if prev is not None else 0
            ps = PSem(h, start)
            _pool_ps[(kind, i)] = ps
            return ps

        def wslot(i, width=None):
            width = WSL if width is None else width
            return aw[:, i * WSL:i * WSL + width]

        def s32(i, th=None):
            if th is None:
                return a32[:, i * T:(i + 1) * T]
            return a32[:, i * T + th * TH: i * T + (th + 1) * TH]

        rows_sc = a32[0:1, 1 * T:2 * T]
        rows_rs = a32[0:1, 2 * T:3 * T]

        def xlnc(c, th=None):
            if th is None:
                return xln[:, c * T:(c + 1) * T]
            return xln[:, c * T + th * TH: c * T + th * TH + TH]

        def bigc(c, th=None):
            if th is None:
                return big[:, c * T:(c + 1) * T]
            return big[:, c * T + th * TH: c * T + th * TH + TH]

        absb = amisc[:, 0:DC * P]
        kst = amisc[:, AMB:AMB + T]
        vst = amisc[:, AMB + T:AMB + 2 * T]
        vst2 = kst

        def sqv(i):
            return amisc[:, i * T:(i + 1) * T]

        def psb_tile(i):
            return amisc[:, AMB + i * TH:AMB + (i + 1) * TH]

        def ring(u):
            i = u % 8
            return amisc[:, i * TH:(i + 1) * TH]

        def sum_row(th, which):
            bank = {(0, 0): pb[4], (0, 1): pb[5], (1, 0): pb[6], (1, 1): pb[3]}[(th, which)]
            return bank[0:1, :]

        with nc.Block("init") as blk:
            s_ld = S("i_ld")

            @blk.sync
            def _(sync):
                for dst, src in [
                    (idt32[:, :], ident32.ap()), (idtbf[:, :], identbf.ap()),
                    (o_r_bf[:, :], ones_r_bf.ap()), (o_c_bf[:, :], ones_c_bf.ap()),
                    (o_r_32[:, :], ones_r_32.ap()),
                    (n1_r_bf[:, :], neg1_r_bf.ap()),
                    (n1_r_32[:, :], neg1_r_32.ap()),
                    (scv[:, 0:NH], qsc), (scv[:, NH:NH + NKV], ksc),
                    (scv[:, NH + NKV:NH + 2 * NKV], vsc),
                    (scv[:, NH + 2 * NKV:2 * NH + 2 * NKV], osc),
                ]:
                    s_ld.inc(sync.dma_start(out=dst, in_=src), 16)
                sync.wait_ge(s_ld.h, s_ld.at(16 * 9))

            @blk.vector
            def _(vector):
                vector.memset(cnst[0:1, 0:1], EPS)
                vector.memset(cnst[0:1, 1:2], -0.5)

        def ln_phase(name, srcs, write_to):
            ns = len(srcs)
            with nc.Block(name) as blk:
                s_ld, s_st, s_ve, s_ac, s_mm = (S(f"{name}ld"), S(f"{name}st"), S(f"{name}ve"),
                                                S(f"{name}ac"), S(f"{name}mm"))
                vp = max(ns - 1, 0)
                xs = 4 if ns > 1 else 0

                @blk.sync
                def _(sync):
                    for c in range(DC):
                        if c >= 1:
                            sync.wait_ge(s_ac.h, s_ac.at(2 * c))
                            if vp:
                                sync.wait_ge(s_ve.h, s_ve.at(vp * c))
                        for j, src in enumerate(srcs):
                            s_ld.inc(sync.dma_start(out=s32(j), in_=src[c * P:(c + 1) * P, :]), 16)
                        if write_to is not None:
                            sync.wait_ge(s_ve.h, s_ve.at(vp * (c + 1)))
                            s_st.inc(sync.dma_start(out=write_to[c * P:(c + 1) * P, :], in_=s32(xs)), 16)
                    if write_to is not None:
                        sync.wait_ge(s_st.h, s_st.at(16 * DC))

                if vp:
                    @blk.vector
                    def _(vector):
                        for c in range(DC):
                            vector.wait_ge(s_ld.h, s_ld.at(16 * ns * (c + 1)))
                            if c >= 1:
                                vector.wait_ge(s_ac.h, s_ac.at(2 * c))
                                if write_to is not None:
                                    vector.wait_ge(s_st.h, s_st.at(16 * c))
                            s_ve.inc(vector.tensor_add(out=s32(xs), in0=s32(0), in1=s32(1)), 1)
                            for j in range(2, ns):
                                s_ve.inc(vector.tensor_add(out=s32(xs), in0=s32(xs), in1=s32(j)), 1)

                @blk.scalar
                def _(scalar):
                    for c in range(DC):
                        if vp:
                            scalar.wait_ge(s_ve.h, s_ve.at(vp * (c + 1)))
                        else:
                            scalar.wait_ge(s_ld.h, s_ld.at(16 * ns * (c + 1)))
                        if c >= 2:
                            scalar.wait_ge(s_mm.h, s_mm.at(4 * (c - 1)))
                        s_ac.inc(scalar.activation(out=xlnc(c), in_=s32(xs), func=AF.Copy), 1)
                        s_ac.inc(scalar.activation(out=sqv(c % 2), in_=s32(xs), func=AF.Square), 1)

                @blk.tensor
                def _(tensor):
                    for c in range(DC):
                        tensor.wait_ge(s_ac.h, s_ac.at(2 * (c + 1)))
                        for th in range(2):
                            s_mm.inc(tensor.matmul(sum_row(th, 0), o_c_bf[:, :], xlnc(c, th),
                                          start=(c == 0), stop=(c == DC - 1)), 1)
                            s_mm.inc(tensor.matmul(sum_row(th, 1), o_c_bf[:, :],
                                          sqv(c % 2)[:, th * TH:th * TH + TH],
                                          start=(c == 0), stop=(c == DC - 1)), 1)

            ln_fin(name + "f")

        def ln_fin(name):
            with nc.Block(name) as blk:
                s_ve, s_ac, s_mm = S(f"{name}ve"), S(f"{name}ac"), S(f"{name}mm")

                @blk.vector
                def _(vector):
                    for th in range(2):
                        mu = rows_mu[0:1, th * TH:(th + 1) * TH]
                        sc = rows_sc[0:1, th * TH:(th + 1) * TH]
                        rs = rows_rs[0:1, th * TH:(th + 1) * TH]
                        s_ve.inc(vector.tensor_scalar(out=mu, in0=sum_row(th, 0), scalar1=1.0 / D,
                                             scalar2=None, op0=ALU.mult), 1)
                        s_ve.inc(vector.tensor_scalar(out=sc, in0=sum_row(th, 1), scalar1=1.0 / D,
                                             scalar2=None, op0=ALU.mult), 1)
                        s_ve.inc(vector.tensor_mul(out=rs, in0=mu, in1=mu), 1)
                        s_ve.inc(vector.tensor_sub(out=sc, in0=sc, in1=rs), 1)
                    vector.wait_ge(s_mm.h, s_mm.at(4))
                    for c in range(DC):
                        for th in range(2):
                            mb = pb[th]
                            rb = pb[2 + th]
                            t1 = s32(0, th)
                            s_ve.inc(vector.tensor_sub(out=t1, in0=xlnc(c, th), in1=mb[:, :]), 1)
                            s_ve.inc(vector.tensor_mul(out=xlnc(c, th), in0=t1, in1=rb[:, :]), 1)

                @blk.scalar
                def _(scalar):
                    for th in range(2):
                        scalar.wait_ge(s_ve.h, s_ve.at(4 * (th + 1)))
                        sc = rows_sc[0:1, th * TH:(th + 1) * TH]
                        rs = rows_rs[0:1, th * TH:(th + 1) * TH]
                        s_ac.inc(scalar.activation(out=sc, in_=sc, func=AF.Ln, bias=cnst[0:1, 0:1]), 1)
                        s_ac.inc(scalar.activation(out=rs, in_=sc, func=AF.Exp, scale=cnst[0:1, 1:2]), 1)

                @blk.tensor
                def _(tensor):
                    for th in range(2):
                        tensor.wait_ge(s_ac.h, s_ac.at(2 * (th + 1)))
                        mu = rows_mu[0:1, th * TH:(th + 1) * TH]
                        rs = rows_rs[0:1, th * TH:(th + 1) * TH]
                        s_mm.inc(tensor.matmul(pb[th][:, :], o_r_32[:, :], mu, start=True, stop=True), 1)
                        s_mm.inc(tensor.matmul(pb[2 + th][:, :], o_r_32[:, :], rs, start=True, stop=True), 1)

        def ffn_phase(name):
            for t, (c0, c1) in enumerate(cfg.isplits):
                ncnk = c1 - c0
                with nc.Block(f"{name}g{t}") as blk:
                    s_wd, s_mm, s_ac, s_ve = (S(f"{name}g{t}wd"), S(f"{name}g{t}mm"),
                                              S(f"{name}g{t}ac"), S(f"{name}g{t}ve"))

                    @blk.gpsimd
                    def _(gpsimd):
                        for s in range(ncnk):
                            if s >= 2:
                                gpsimd.wait_ge(s_mm.h, s_mm.at(s - 1))
                            gs = c0 + s
                            s_wd.inc(gpsimd.dma_start(
                                out=wslot(s % 2, DC * P).rearrange("p (c j) -> p c j", j=P),
                                in_=wg_t[gs]), 16)
                            s_wd.inc(gpsimd.dma_start(
                                out=wslot(2 + s % 2, DC * P).rearrange("p (c j) -> p c j", j=P),
                                in_=wu_t[gs]), 16)

                    @blk.tensor
                    def _(tensor):
                        for s in range(ncnk):
                            tensor.wait_ge(s_wd.h, s_wd.at(32 * (s + 1)))
                            if s >= 1:
                                tensor.wait_ge(s_ac.h, s_ac.at(s))
                                tensor.wait_ge(s_ve.h, s_ve.at(s))
                            g = wslot(s % 2, DC * P)
                            u = wslot(2 + s % 2, DC * P)
                            for c in range(DC):
                                for th in range(2):
                                    tensor.matmul(pb[th][:, :], g[:, c * P:(c + 1) * P], xlnc(c, th),
                                                  start=(c == 0), stop=(c == DC - 1))
                                    m = tensor.matmul(pb[2 + th][:, :], u[:, c * P:(c + 1) * P], xlnc(c, th),
                                                      start=(c == 0), stop=(c == DC - 1))
                            s_mm.inc(m, 1)

                    @blk.scalar
                    def _(scalar):
                        for s in range(ncnk):
                            scalar.wait_ge(s_mm.h, s_mm.at(s + 1))
                            if s >= 1:
                                scalar.wait_ge(s_ve.h, s_ve.at(s))
                            for th in range(2):
                                a = scalar.activation(out=s32(0, th), in_=pb[th][:, :], func=AF.Silu)
                            s_ac.inc(a, 1)

                    @blk.vector
                    def _(vector):
                        for s in range(ncnk):
                            vector.wait_ge(s_ac.h, s_ac.at(s + 1))
                            for th in range(2):
                                v = vector.tensor_mul(out=bigc(s, th), in0=s32(0, th), in1=pb[2 + th][:, :])
                            s_ve.inc(v, 1)

                with nc.Block(f"{name}d{t}") as blk:
                    s_wd, s_mm, s_ac, s_st = (S(f"{name}d{t}wd"), S(f"{name}d{t}mm"),
                                              S(f"{name}d{t}ac"), S(f"{name}d{t}st"))

                    @blk.gpsimd
                    def _(gpsimd):
                        for o in range(DC):
                            if o >= 2:
                                gpsimd.wait_ge(s_mm.h, s_mm.at(o - 1))
                            s_wd.inc(gpsimd.dma_start(
                                out=wslot(o % 2, ncnk * P).rearrange("p (c j) -> p c j", j=P),
                                in_=wd_t[o][:, c0:c1, :]), 16)

                    @blk.tensor
                    def _(tensor):
                        for o in range(DC):
                            tensor.wait_ge(s_wd.h, s_wd.at(16 * (o + 1)))
                            if o >= 1:
                                tensor.wait_ge(s_ac.h, s_ac.at(o))
                            w = wslot(o % 2, ncnk * P)
                            for c in range(ncnk):
                                for th in range(2):
                                    m = tensor.matmul(pb[4 + th][:, :], w[:, c * P:(c + 1) * P], bigc(c, th),
                                                      start=(c == 0), stop=(c == ncnk - 1))
                            s_mm.inc(m, 1)

                    @blk.scalar
                    def _(scalar):
                        for o in range(DC):
                            scalar.wait_ge(s_mm.h, s_mm.at(o + 1))
                            if o >= 2:
                                scalar.wait_ge(s_st.h, s_st.at(16 * (o - 1)))
                            for th in range(2):
                                a = scalar.activation(out=s32(2 + o % 2, th), in_=pb[4 + th][:, :],
                                                      func=AF.Copy, scale=0.5)
                            s_ac.inc(a, 1)

                    @blk.sync
                    def _(sync):
                        for o in range(DC):
                            sync.wait_ge(s_ac.h, s_ac.at(o + 1))
                            s_st.inc(sync.dma_start(out=part[t][o * P:(o + 1) * P, :],
                                           in_=s32(2 + o % 2)), 16)
                        sync.wait_ge(s_st.h, s_st.at(16 * DC))

        def bitnet_proj(name, w_ap, n_o, scol_off, extra_scale, dest,
                        kv_store=None, v_transpose=False, pre_gpsimd=None):
            with nc.Block(name) as blk:
                s_wd, s_ab, s_sg = S(f"{name}wd"), S(f"{name}ab"), S(f"{name}sg")
                s_sm, s_cl, s_tp = S(f"{name}sm"), S(f"{name}cl"), S(f"{name}tp")
                s_se, s_pm, s_ev = S(f"{name}se"), S(f"{name}pm"), S(f"{name}ev")
                s_st, s_vt, s_vc = S(f"{name}st"), S(f"{name}vt"), S(f"{name}vc")

                @blk.gpsimd
                def _(gpsimd):
                    if pre_gpsimd is not None:
                        pre_gpsimd(gpsimd)
                    for o in range(n_o):
                        if o >= 2:
                            gpsimd.wait_ge(s_pm.h, s_pm.at(o - 1))
                        s_wd.inc(gpsimd.dma_start(
                            out=wslot(o % 2, DC * P).rearrange("p (c j) -> p c j", j=P),
                            in_=w_ap[o]), 16)

                @blk.scalar
                def _(scalar):
                    for o in range(n_o):
                        scalar.wait_ge(s_wd.h, s_wd.at(16 * (o + 1)))
                        if o >= 1:
                            scalar.wait_ge(s_sm.h, s_sm.at(o))
                        w = wslot(o % 2, DC * P)
                        s_ab.inc(scalar.activation(out=absb, in_=w, func=AF.Abs), 1)
                        s_sg.inc(scalar.activation(out=w, in_=w, func=AF.Sign), 1)
                        scalar.wait_ge(s_pm.h, s_pm.at(o + 1))
                        scalar.wait_ge(s_se.h, s_se.at(o + 1))
                        if kv_store is not None and o >= 2:
                            scalar.wait_ge(s_st.h, s_st.at(16 * (o - 1)))
                        sc = seff[:, o % 2:o % 2 + 1]
                        for th in range(2):
                            if kv_store is None:
                                dst = dest(o, th)
                            elif v_transpose:
                                dst = vst[:, th * TH:(th + 1) * TH]
                            else:
                                dst = kst[:, th * TH:(th + 1) * TH]
                            a = scalar.activation(out=dst, in_=pb[th][:, :], func=AF.Copy, scale=sc)
                        s_ev.inc(a, 1)
                        if v_transpose:
                            scalar.wait_ge(s_vt.h, s_vt.at(o + 1))
                            if o >= 1:
                                scalar.wait_ge(s_st.h, s_st.at(16 * o))
                            s_vc.inc(scalar.activation(out=vst2, in_=pbf[:, 0:T], func=AF.Copy), 1)

                @blk.tensor
                def _(tensor):
                    for o in range(n_o):
                        tensor.wait_ge(s_ab.h, s_ab.at(o + 1))
                        if o >= 1:
                            tensor.wait_ge(s_cl.h, s_cl.at(o))
                        for c in range(DC):
                            m = tensor.matmul(pb[6][0:1, 0:P], o_c_bf[:, :], absb[:, c * P:(c + 1) * P],
                                              start=(c == 0), stop=(c == DC - 1))
                        s_sm.inc(m, 1)
                        tensor.wait_ge(s_cl.h, s_cl.at(o + 1))
                        if o >= 1:
                            tensor.wait_ge(s_se.h, s_se.at(o))
                        s_tp.inc(tensor.transpose(pb[6][:, 4:5], rows_sc[0:1, 0:P], idt32[0:1, 0:1]), 1)
                        tensor.wait_ge(s_sg.h, s_sg.at(o + 1))
                        if o >= 1:
                            tensor.wait_ge(s_ev.h, s_ev.at(o))
                        w = wslot(o % 2, DC * P)
                        for c in range(DC):
                            for th in range(2):
                                m = tensor.matmul(pb[th][:, :], w[:, c * P:(c + 1) * P], xlnc(c, th),
                                                  start=(c == 0), stop=(c == DC - 1))
                        s_pm.inc(m, 1)
                        if v_transpose:
                            tensor.wait_ge(s_ev.h, s_ev.at(o + 1))
                            if o >= 1:
                                tensor.wait_ge(s_vc.h, s_vc.at(o))
                            for tt in range(TN):
                                m = tensor.transpose(pbf[:, tt * P:(tt + 1) * P],
                                                     vst[:, tt * P:(tt + 1) * P],
                                                     idtbf[:, :])
                            s_vt.inc(m, 1)

                @blk.vector
                def _(vector):
                    for o in range(n_o):
                        vector.wait_ge(s_sm.h, s_sm.at(o + 1))
                        s_cl.inc(vector.tensor_scalar(out=rows_sc[0:1, 0:P], in0=pb[6][0:1, 0:P],
                                             scalar1=1.0 / D, scalar2=1e-8,
                                             op0=ALU.mult, op1=ALU.max), 1)
                        vector.wait_ge(s_tp.h, s_tp.at(o + 1))
                        if o >= 1:
                            vector.wait_ge(s_ev.h, s_ev.at(o))
                        s_se.inc(vector.scalar_tensor_tensor(out=seff[:, o % 2:o % 2 + 1], in0=pb[6][:, 4:5],
                                                    scalar=extra_scale,
                                                    in1=scv[:, scol_off + o:scol_off + o + 1],
                                                    op0=ALU.mult, op1=ALU.mult), 1)

                if kv_store is not None:
                    @blk.sync
                    def _(sync):
                        for o in range(n_o):
                            if not v_transpose:
                                sync.wait_ge(s_ev.h, s_ev.at(o + 1))
                                s_st.inc(sync.dma_start(out=cc_in.ap()[o * P:(o + 1) * P, :],
                                               in_=kst), 16)
                            else:
                                sync.wait_ge(s_vc.h, s_vc.at(o + 1))
                                s_st.inc(sync.dma_start(
                                    out=cc_in.ap()[T:2 * T, o * P:(o + 1) * P]
                                    .rearrange("(tt p) j -> p tt j", p=P),
                                    in_=vst2.rearrange("p (tt j) -> p tt j", j=P)), 16)
                        sync.wait_ge(s_st.h, s_st.at(16 * n_o))

        def attn_phase(name, s_cc_handle):
            with nc.Block(name) as blk:
                s_kv, s_rk, s_nb = S(f"{name}kv"), S(f"{name}rk"), S(f"{name}nb")
                s_nc = S(f"{name}nc")
                s_nk, s_sq = S(f"{name}nk"), S(f"{name}sq")
                s_sb, s_ex, s_av = S(f"{name}sb"), S(f"{name}ex"), S(f"{name}av")
                s_tn, s_rc, s_ib = S(f"{name}tn"), S(f"{name}rc"), S(f"{name}ib")
                s_iv, s_ao = S(f"{name}iv"), S(f"{name}ao")
                UPH = GR * 2
                NU = NKV * UPH
                NKB = KEYS // TH
                NP = KTN // 2
                NPH = NKB + UPH

                def unit_of(h, uh):
                    u = h * UPH + uh
                    return u, h * GR + uh // 2, uh % 2

                def nrow_bank(i):
                    return (pb[5] if i % 2 == 0 else pb[6])[0:1, :]

                pbf32 = pbf[:, :].bitcast(F32)
                sqK = amisc[:, 0:KEYS]
                sqQ = amisc[:, AMB + T:AMB + 2 * T]
                rkp = a32[0:1, 0:8]
                rk2 = a32[0:1, 8:9]
                qm2 = a32[0:1, 9:10]
                lnr = a32[0:1, 10:11]

                def invb(u):
                    return s32(2, u % 2)

                def mrow_inv(u):
                    return rows_mu[0:1, (u % 2) * TH:(u % 2) * TH + TH]

                @blk.sync
                def _(sync):
                    sync.wait_ge(s_cc_handle.h, s_cc_handle.at(1))
                    for h in range(NKV):
                        if h >= 2:
                            sync.wait_ge(s_tn.h, s_tn.at(UPH * (h - 1)))
                        kslot = wslot(h % 2, KEYS)
                        vslot = wslot(2 + h % 2, KEYS)
                        for r in range(R):
                            s_kv.inc(sync.dma_start(
                                out=kslot[:, r * T:(r + 1) * T],
                                in_=cc_out.ap()[r * 2 * T + h * P: r * 2 * T + h * P + P, :]), 16)
                            s_kv.inc(sync.dma_start(
                                out=vslot.rearrange("p (kt j) -> p kt j", j=P)[:, r * TN:(r + 1) * TN, :],
                                in_=cc_out.ap()[r * 2 * T + T: r * 2 * T + 2 * T, h * P:(h + 1) * P]
                                .rearrange("(kt p) j -> p kt j", p=P)), 16)

                @blk.scalar
                def _(scalar):
                    for h in range(NKV):
                        scalar.wait_ge(s_kv.h, s_kv.at(16 * 2 * R * (h + 1)))
                        if h >= 1:
                            scalar.wait_ge(s_nk.h, s_nk.at(NPH * h - UPH))
                        s_sq.inc(scalar.activation(out=sqK, in_=wslot(h % 2, KEYS), func=AF.Square), 1)
                        for uh in range(UPH):
                            u, g, qt = unit_of(h, uh)
                            if uh % 2 == 0:
                                if u >= 1:
                                    scalar.wait_ge(s_nk.h, s_nk.at(NKB * (h + 1) + UPH * h + uh))
                                s_sq.inc(scalar.activation(out=sqQ[:, 0:T], in_=bigc(g), func=AF.Square), 1)
                            scalar.wait_ge(s_rk.h, s_rk.at((NKB + 1 + 2 * uh + 1) + (NKB + 1 + 2 * UPH) * h))
                            if u >= 1:
                                scalar.wait_ge(s_nb.h, s_nb.at(2 * u))
                            scalar.activation(out=lnr, in_=qm2, func=AF.Ln, scale=rk2)
                            s_nb.inc(scalar.activation(out=lnr, in_=lnr, func=AF.Exp, scale=0.5), 1)
                            scalar.wait_ge(s_nb.h, s_nb.at(2 * u + 2))
                            s_nc.inc(scalar.activation(out=nbias[:, u % 2:u % 2 + 1],
                                                       in_=pbf32[:, 0:1], func=AF.Copy), 1)
                        psB = [pb[2], pb[3], pb[6]]
                        for uh in range(UPH):
                            u, g, qt = unit_of(h, uh)
                            for kt in range(KTN):
                                gi = u * KTN + kt
                                scalar.wait_ge(s_sb.h, s_sb.at(gi + 1))
                                scalar.wait_ge(s_nc.h, s_nc.at(u + 1))
                                if gi >= 3:
                                    scalar.wait_ge(s_av.h, s_av.at(gi - 2))
                                s_ex.inc(scalar.activation(out=psb_tile(kt % 3), in_=psB[kt % 3][:, :],
                                                           func=AF.Exp,
                                                           bias=nbias[:, u % 2:u % 2 + 1]), 1)
                            scalar.wait_ge(s_ib.h, s_ib.at(u + 1))
                            if u >= 2:
                                scalar.wait_ge(s_ao.h, s_ao.at(u - 1))
                            s_iv.inc(scalar.activation(out=invb(u), in_=pb[5][:, :],
                                                       func=AF.Copy), 1)

                @blk.tensor
                def _(tensor):
                    for h in range(NKV):
                        kslot = wslot(h % 2, KEYS)
                        vslot = wslot(2 + h % 2, KEYS)
                        tensor.wait_ge(s_sq.h, s_sq.at((1 + GR) * h + 1))
                        if h >= 1:
                            tensor.wait_ge(s_iv.h, s_iv.at(UPH * h))
                        for kb in range(NKB):
                            if kb >= 2:
                                tensor.wait_ge(s_rk.h, s_rk.at((NKB + 1 + 2 * UPH) * h + kb - 1))
                            s_nk.inc(tensor.matmul(nrow_bank(kb), o_c_bf[:, :],
                                                   sqK[:, kb * TH:(kb + 1) * TH],
                                                   start=True, stop=True), 1)
                        for uh in range(UPH):
                            u, g, qt = unit_of(h, uh)
                            if uh % 2 == 0:
                                tensor.wait_ge(s_sq.h, s_sq.at((1 + GR) * h + 2 + uh // 2))
                            if uh < 2:
                                tensor.wait_ge(s_rk.h, s_rk.at((NKB + 1 + 2 * UPH) * h + NKB - 1 + uh))
                            else:
                                tensor.wait_ge(s_rk.h, s_rk.at((NKB + 1 + 2 * UPH) * h + NKB + 1 + 2 * (uh - 2) + 2))
                            s_nk.inc(tensor.matmul(nrow_bank(NKB + uh), o_c_bf[:, :],
                                                   sqQ[:, qt * TH:(qt + 1) * TH],
                                                   start=True, stop=True), 1)
                            tensor.wait_ge(s_nb.h, s_nb.at(2 * u + 1))
                            if u >= 1:
                                tensor.wait_ge(s_nc.h, s_nc.at(u))
                            m = tensor.matmul(pbf32[:, 0:1], n1_r_32[:, :], lnr,
                                              start=True, stop=True)
                            s_nb.inc(m, 1)
                        psB = [pb[2], pb[3], pb[6]]
                        for uh in range(UPH):
                            u, g, qt = unit_of(h, uh)
                            rhs_q = bigc(g, qt)
                            for kt in range(KTN):
                                gi = u * KTN + kt
                                if gi >= 3:
                                    tensor.wait_ge(s_ex.h, s_ex.at(gi - 2))
                                s_sb.inc(tensor.matmul(psB[kt % 3][:, :], kslot[:, kt * P:(kt + 1) * P],
                                                       rhs_q, start=True, stop=True), 1)
                                if kt >= 2:
                                    kk = kt - 2
                                    tensor.wait_ge(s_ex.h, s_ex.at(gi - 1))
                                    if kt == 2:
                                        tensor.wait_ge(s_ao.h, s_ao.at(u))
                                        tensor.wait_ge(s_iv.h, s_iv.at(u))
                                    s_av.inc(tensor.matmul(pb[4][:, :], vslot[:, kk * P:(kk + 1) * P],
                                                           psb_tile(kk % 3), start=(kk == 0),
                                                           stop=False), 1)
                                    tensor.matmul(pb[5][0:1, :], o_c_bf[:, :], psb_tile(kk % 3),
                                                  start=(kk == 0), stop=False)
                            for kk in (KTN - 2, KTN - 1):
                                tensor.wait_ge(s_ex.h, s_ex.at(u * KTN + kk + 1))
                                s_av.inc(tensor.matmul(pb[4][:, :], vslot[:, kk * P:(kk + 1) * P],
                                                       psb_tile(kk % 3), start=False,
                                                       stop=(kk == KTN - 1)), 1)
                                m = tensor.matmul(pb[5][0:1, :], o_c_bf[:, :], psb_tile(kk % 3),
                                                  start=False, stop=(kk == KTN - 1))
                            s_tn.inc(m, 1)
                            tensor.wait_ge(s_rc.h, s_rc.at(u + 1))
                            s_ib.inc(tensor.matmul(pb[5][:, :], o_r_32[:, :],
                                                   mrow_inv(u), start=True, stop=True), 1)

                @blk.vector
                def _(vector):
                    for h in range(NKV):
                        for kb in range(NKB):
                            vector.wait_ge(s_nk.h, s_nk.at(NPH * h + kb + 1))
                            s_rk.inc(vector.reduce_max(out=rkp[0:1, kb:kb + 1],
                                                       in_=nrow_bank(kb), axis=AX.X), 1)
                        s_rk.inc(vector.reduce_max(out=rk2, in_=rkp, axis=AX.X), 1)
                        for uh in range(UPH):
                            u, g, qt = unit_of(h, uh)
                            vector.wait_ge(s_nk.h, s_nk.at(NPH * h + NKB + uh + 1))
                            s_rk.inc(vector.reduce_max(out=qm2, in_=nrow_bank(NKB + uh),
                                                       axis=AX.X), 1)
                            vector.wait_ge(s_nb.h, s_nb.at(2 * u + 1))
                            s_rk.inc(vector.tensor_copy(out=rkp[0:1, 7:8], in_=qm2), 1)
                        for uh in range(UPH):
                            u, g, qt = unit_of(h, uh)
                            vector.wait_ge(s_tn.h, s_tn.at(u + 1))
                            if u >= 1:
                                vector.wait_ge(s_ib.h, s_ib.at(u - 1))
                            s_rc.inc(vector.reciprocal(out=mrow_inv(u), in_=pb[5][0:1, :]), 1)
                            vector.wait_ge(s_iv.h, s_iv.at(u + 1))
                            s_ao.inc(vector.tensor_mul(out=bigc(g, qt), in0=pb[4][:, :],
                                                       in1=invb(u)), 1)

        def oproj_phase(name):
            with nc.Block(name) as blk:
                s_wd, s_ab, s_sg = S(f"{name}wd"), S(f"{name}ab"), S(f"{name}sg")
                s_sm, s_cl, s_tp = S(f"{name}sm"), S(f"{name}cl"), S(f"{name}tp")
                s_se, s_pm, s_ev = S(f"{name}se"), S(f"{name}pm"), S(f"{name}ev")
                s_ld, s_ve, s_st = S(f"{name}ld"), S(f"{name}ve"), S(f"{name}st")
                s_ac2, s_mm2 = S(f"{name}ac2"), S(f"{name}mm2")

                @blk.gpsimd
                def _(gpsimd):
                    for o in range(DC):
                        if o >= 2:
                            gpsimd.wait_ge(s_pm.h, s_pm.at(o - 1))
                        s_wd.inc(gpsimd.dma_start(
                            out=wslot(o % 2, DC * P).rearrange("p (c j) -> p c j", j=P),
                            in_=wo_t[o]), 16)

                @blk.scalar
                def _(scalar):
                    for o in range(DC):
                        scalar.wait_ge(s_wd.h, s_wd.at(16 * (o + 1)))
                        if o >= 1:
                            scalar.wait_ge(s_sm.h, s_sm.at(o))
                        w = wslot(o % 2, DC * P)
                        s_ab.inc(scalar.activation(out=absb, in_=w, func=AF.Abs), 1)
                        s_sg.inc(scalar.activation(out=w, in_=w, func=AF.Sign), 1)
                        scalar.wait_ge(s_pm.h, s_pm.at(o + 1))
                        scalar.wait_ge(s_se.h, s_se.at(o + 1))
                        if o >= 1:
                            scalar.wait_ge(s_ve.h, s_ve.at(2 * o))
                        sc = seff[:, o % 2:o % 2 + 1]
                        for th in range(2):
                            a = scalar.activation(out=s32(2, th), in_=pb[th][:, :], func=AF.Copy, scale=sc)
                        s_ev.inc(a, 1)
                        scalar.wait_ge(s_ve.h, s_ve.at(2 * (o + 1)))
                        if o >= 2:
                            scalar.wait_ge(s_mm2.h, s_mm2.at(4 * (o - 1)))
                        s_ac2.inc(scalar.activation(out=xlnc(o), in_=s32(4), func=AF.Copy), 1)
                        s_ac2.inc(scalar.activation(out=(kst if o % 2 == 0 else vst),
                                                    in_=s32(4), func=AF.Square), 1)

                @blk.tensor
                def _(tensor):
                    for o in range(DC):
                        tensor.wait_ge(s_ab.h, s_ab.at(o + 1))
                        if o >= 1:
                            tensor.wait_ge(s_cl.h, s_cl.at(o))
                        for c in range(DC):
                            m = tensor.matmul(pb[2][0:1, 0:P], o_c_bf[:, :], absb[:, c * P:(c + 1) * P],
                                              start=(c == 0), stop=(c == DC - 1))
                        s_sm.inc(m, 1)
                        tensor.wait_ge(s_cl.h, s_cl.at(o + 1))
                        if o >= 1:
                            tensor.wait_ge(s_se.h, s_se.at(o))
                        s_tp.inc(tensor.transpose(pb[2][:, 4:5], rows_sc[0:1, 0:P], idt32[0:1, 0:1]), 1)
                        tensor.wait_ge(s_sg.h, s_sg.at(o + 1))
                        if o >= 1:
                            tensor.wait_ge(s_ev.h, s_ev.at(o))
                        w = wslot(o % 2, DC * P)
                        for c in range(DC):
                            for th in range(2):
                                m = tensor.matmul(pb[th][:, :], w[:, c * P:(c + 1) * P], bigc(c, th),
                                                  start=(c == 0), stop=(c == DC - 1))
                        s_pm.inc(m, 1)
                        tensor.wait_ge(s_ac2.h, s_ac2.at(2 * (o + 1)))
                        sq_t = kst if o % 2 == 0 else vst
                        for th in range(2):
                            s_mm2.inc(tensor.matmul(sum_row(th, 0), o_c_bf[:, :], xlnc(o, th),
                                          start=(o == 0), stop=(o == DC - 1)), 1)
                            s_mm2.inc(tensor.matmul(sum_row(th, 1), o_c_bf[:, :],
                                          sq_t[:, th * TH:th * TH + TH],
                                          start=(o == 0), stop=(o == DC - 1)), 1)

                @blk.vector
                def _(vector):
                    for o in range(DC):
                        vector.wait_ge(s_sm.h, s_sm.at(o + 1))
                        s_cl.inc(vector.tensor_scalar(out=rows_sc[0:1, 0:P], in0=pb[2][0:1, 0:P],
                                             scalar1=1.0 / D, scalar2=1e-8,
                                             op0=ALU.mult, op1=ALU.max), 1)
                        vector.wait_ge(s_tp.h, s_tp.at(o + 1))
                        if o >= 1:
                            vector.wait_ge(s_ev.h, s_ev.at(o))
                        s_se.inc(vector.scalar_tensor_tensor(out=seff[:, o % 2:o % 2 + 1], in0=pb[2][:, 4:5],
                                                    scalar=1.0,
                                                    in1=scv[:, NH + 2 * NKV + o:NH + 2 * NKV + o + 1],
                                                    op0=ALU.mult, op1=ALU.mult), 1)
                        vector.wait_ge(s_ev.h, s_ev.at(o + 1))
                        vector.wait_ge(s_ld.h, s_ld.at(16 * (o + 1)))
                        if o >= 1:
                            vector.wait_ge(s_st.h, s_st.at(16 * o))
                            vector.wait_ge(s_ac2.h, s_ac2.at(2 * o))
                        s_ve.inc(vector.tensor_add(out=s32(4), in0=s32(2), in1=s32(3)), 2)

                @blk.sync
                def _(sync):
                    for o in range(DC):
                        if o >= 1:
                            sync.wait_ge(s_ve.h, s_ve.at(2 * o))
                        s_ld.inc(sync.dma_start(out=s32(3), in_=xc2[o * P:(o + 1) * P, :]), 16)
                        sync.wait_ge(s_ve.h, s_ve.at(2 * (o + 1)))
                        s_st.inc(sync.dma_start(out=xc3[o * P:(o + 1) * P, :], in_=s32(4)), 16)
                    sync.wait_ge(s_st.h, s_st.at(16 * DC))

            ln_fin(name + "f")

        ln_phase("ln1", [xT], None)
        ffn_phase("f1")
        ln_phase("ln2", [xT, part[0], part[1], part[2]], xc2)

        s_cc = PSem(es.enter_context(nc.semaphore("cc")), 0)

        bitnet_proj("kp", wk_t, NKV, NH, 1.0, None, kv_store="k")
        bitnet_proj("vp", wv_t, NKV, NH + NKV, 1.0, None, kv_store="v", v_transpose=True)

        def q_pre(gpsimd):
            s_cc.inc(gpsimd.collective_compute(
                "AllGather", ALU.bypass,
                replica_groups=[[0, 1, 2, 3], [4, 5, 6, 7]],
                ins=[cc_in.ap().opt()],
                outs=[cc_out.ap().opt()],
            ), 1)

        bitnet_proj("qp", wq_t, NH, 0, 1.0 / math.sqrt(P), lambda o, th: bigc(o, th),
                    pre_gpsimd=q_pre)
        attn_phase("at", s_cc)
        oproj_phase("op")
        ffn_phase("f2")

        dbg = {}
        dbg_xln = None
        if debug:
            for nm, src, shp, dt in [
                ("dbg_xc2", xc2, [D, T], F32), ("dbg_xc3", xc3, [D, T], F32),
                ("dbg_p0", part[0], [D, T], F32), ("dbg_p1", part[1], [D, T], F32),
                ("dbg_p2", part[2], [D, T], F32),
                ("dbg_ccin", cc_in.ap(), [2 * T, KV], BF16),
                ("dbg_ccout", cc_out.ap(), [2 * R * T, KV], BF16),
            ]:
                dbg[nm] = (nc.dram_tensor(nm, shp, dt, kind="ExternalOutput").ap(), src)
            dbg_xln = nc.dram_tensor("dbg_xln", [P, DC * T], BF16, kind="ExternalOutput").ap()
            with nc.Block("dbg") as blk:
                s_dst = S("dbgst")

                @blk.sync
                def _(sync):
                    n = 0
                    for nm, (dst, src) in dbg.items():
                        s_dst.inc(sync.dma_start(out=dst, in_=src), 16)
                        n += 1
                    s_dst.inc(sync.dma_start(out=dbg_xln, in_=xln[:, :]), 16)
                    sync.wait_ge(s_dst.h, s_dst.at(16 * (n + 1)))

        with nc.Block("fin") as blk:
            s_ld, s_ve, s_st = S("finld"), S("finve"), S("finst")

            @blk.sync
            def _(sync):
                for c in range(DC):
                    if c >= 1:
                        sync.wait_ge(s_ve.h, s_ve.at(3 * c))
                    for j, src in enumerate([xc3, part[0], part[1], part[2]]):
                        s_ld.inc(sync.dma_start(out=s32(j), in_=src[c * P:(c + 1) * P, :]), 16)
                    sync.wait_ge(s_ve.h, s_ve.at(3 * (c + 1)))
                    s_st.inc(sync.dma_start(out=yT[c * P:(c + 1) * P, :], in_=s32(4)), 16)
                sync.wait_ge(s_st.h, s_st.at(16 * DC))

            @blk.vector
            def _(vector):
                for c in range(DC):
                    vector.wait_ge(s_ld.h, s_ld.at(16 * 4 * (c + 1)))
                    if c >= 1:
                        vector.wait_ge(s_st.h, s_st.at(16 * c))
                    s_ve.inc(vector.tensor_add(out=s32(4), in0=s32(0), in1=s32(1)), 1)
                    s_ve.inc(vector.tensor_add(out=s32(4), in0=s32(4), in1=s32(2)), 1)
                    s_ve.inc(vector.tensor_add(out=s32(4), in0=s32(4), in1=s32(3)), 1)

    return nc


def _tile_T(w):
    O, IC_ = w.shape
    return np.ascontiguousarray(w.T.reshape(IC_ // P, P, O // P, P).transpose(2, 1, 0, 3))


def prep_inputs(cfg: Cfg, inp):
    T, I, R = cfg.T, cfg.I, cfg.R
    x = inp["L0_attn_hidden_states"]
    wgu = inp["L0_ffn_w_gate_up"]

    shared = {
        "wg_t": _tile_T(wgu[:I]),
        "wu_t": _tile_T(wgu[I:]),
        "wd_t": _tile_T(inp["L0_ffn_w_down"]),
        "wq_t": _tile_T(inp["L0_attn_q_weight"]),
        "wk_t": _tile_T(inp["L0_attn_k_weight"]),
        "wv_t": _tile_T(inp["L0_attn_v_weight"]),
        "wo_t": _tile_T(inp["L0_attn_o_weight"]),
        "qsc": np.ascontiguousarray(inp["L0_attn_q_scale"].reshape(-1, P).T),
        "ksc": np.ascontiguousarray(inp["L0_attn_k_scale"].reshape(-1, P).T),
        "vsc": np.ascontiguousarray(inp["L0_attn_v_scale"].reshape(-1, P).T),
        "osc": np.ascontiguousarray(inp["L0_attn_o_scale"].reshape(-1, P).T),
    }
    in_maps = []
    for c in range(8):
        b, q = c // R, c % R
        m = dict(shared)
        m["xT"] = np.ascontiguousarray(x[b, q * T:(q + 1) * T, :].T)
        in_maps.append(m)
    return in_maps


def kernel(**inputs):
    cfg = CFG
    in_maps = prep_inputs(cfg, inputs)
    nc = build_program(cfg)
    res = run_bass_kernel_spmd(nc, in_maps, core_ids=list(range(8)))
    B = 2
    y = np.empty((B, cfg.R * cfg.T, cfg.D), np.float32)
    for c in range(8):
        b, q = c // cfg.R, c % cfg.R
        y[b, q * cfg.T:(q + 1) * cfg.T, :] = res.results[c]["yT"].T
    return y


# version 12
# speedup vs baseline: 1.3718x; 1.0566x over previous
import math
from contextlib import ExitStack
from dataclasses import dataclass

import numpy as np
import ml_dtypes

import concourse.bass as bass
import concourse.mybir as mybir
from concourse.bass_utils import run_bass_kernel_spmd

F32 = mybir.dt.float32
BF16 = mybir.dt.bfloat16
AF = mybir.ActivationFunctionType
ALU = mybir.AluOpType
AX = mybir.AxisListType

P = 128
EPS = 1e-5


@dataclass(frozen=True)
class Cfg:
    D: int = 4096
    T: int = 1024
    I: int = 11008
    KV: int = 1024
    R: int = 4

    @property
    def DC(self):
        return self.D // P

    @property
    def IC(self):
        return self.I // P

    @property
    def NH(self):
        return self.D // P

    @property
    def NKV(self):
        return self.KV // P

    @property
    def TH(self):
        return self.T // 2

    @property
    def KEYS(self):
        return self.R * self.T

    @property
    def KTN(self):
        return self.KEYS // P

    @property
    def GR(self):
        return self.NH // self.NKV

    @property
    def isplits(self):
        n = self.IC
        a = (n + 2) // 3
        b = (n - a + 1) // 2
        return [(0, a), (a, a + b), (a + b, n)]


CFG = Cfg()


class PSem:

    def __init__(self, h, start):
        self.h = h
        self.n = start
        self.base = start

    def inc(self, instr, k=1):
        instr.then_inc(self.h, k)
        self.n += k
        return instr

    def at(self, v):
        return self.base + v


def build_program(cfg: Cfg, debug=False):
    nc = bass.Bass("TRN2", target_bir_lowering=False, detect_race_conditions=False)
    D, T, I, KV = cfg.D, cfg.T, cfg.I, cfg.KV
    DC, IC, NH, NKV, TH, R = cfg.DC, cfg.IC, cfg.NH, cfg.NKV, cfg.TH, cfg.R
    KTN, GR, KEYS = cfg.KTN, cfg.GR, cfg.KEYS
    TB = TH // P
    TN = T // P
    assert T == KV, "kernel assumes tokens-per-core == kv width"

    xT = nc.dram_tensor("xT", [D, T], F32, kind="ExternalInput").ap()
    wg_t = nc.dram_tensor("wg_t", [IC, P, DC, P], F32, kind="ExternalInput").ap()
    wu_t = nc.dram_tensor("wu_t", [IC, P, DC, P], F32, kind="ExternalInput").ap()
    wd_t = nc.dram_tensor("wd_t", [DC, P, IC, P], F32, kind="ExternalInput").ap()
    wq_t = nc.dram_tensor("wq_t", [NH, P, DC, P], F32, kind="ExternalInput").ap()
    wk_t = nc.dram_tensor("wk_t", [NKV, P, DC, P], F32, kind="ExternalInput").ap()
    wv_t = nc.dram_tensor("wv_t", [NKV, P, DC, P], F32, kind="ExternalInput").ap()
    wo_t = nc.dram_tensor("wo_t", [NH, P, DC, P], F32, kind="ExternalInput").ap()
    qsc = nc.dram_tensor("qsc", [P, NH], F32, kind="ExternalInput").ap()
    ksc = nc.dram_tensor("ksc", [P, NKV], F32, kind="ExternalInput").ap()
    vsc = nc.dram_tensor("vsc", [P, NKV], F32, kind="ExternalInput").ap()
    osc = nc.dram_tensor("osc", [P, NH], F32, kind="ExternalInput").ap()
    yT = nc.dram_tensor("yT", [D, T], F32, kind="ExternalOutput").ap()

    xc2 = nc.dram_tensor("xc2", [D, T], F32, kind="Internal").ap()
    xc3 = nc.dram_tensor("xc3", [D, T], F32, kind="Internal").ap()
    part = [nc.dram_tensor(f"part{i}", [D, T], F32, kind="Internal").ap() for i in range(3)]
    cc_in = nc.dram_tensor("cc_in", [2 * T, KV], BF16, kind="Internal")
    cc_out = nc.dram_tensor("cc_out", [2 * R * T, KV], BF16, kind="Internal")

    ident32 = nc.inline_tensor(np.eye(P, dtype=np.float32), name="ident32")
    identbf = nc.inline_tensor(np.eye(P, dtype=ml_dtypes.bfloat16), name="identbf")
    ones_r_bf = nc.inline_tensor(np.ones((1, P), dtype=ml_dtypes.bfloat16), name="ones_r_bf")
    ones_c_bf = nc.inline_tensor(np.ones((P, 1), dtype=ml_dtypes.bfloat16), name="ones_c_bf")
    ones_r_32 = nc.inline_tensor(np.ones((1, P), dtype=np.float32), name="ones_r_32")
    neg1_r_bf = nc.inline_tensor(np.full((1, P), -1.0, dtype=ml_dtypes.bfloat16), name="neg1_r_bf")
    neg1_r_32 = nc.inline_tensor(np.full((1, P), -1.0, dtype=np.float32), name="neg1_r_32")

    WSL = max(DC * P, KEYS)
    HMX = max(NH, max(b - a for a, b in cfg.isplits))

    es = ExitStack()
    with es:
        xln = es.enter_context(nc.sbuf_tensor("xln", [P, DC * T], BF16))
        big = es.enter_context(nc.sbuf_tensor("big", [P, HMX * T], BF16))
        aw = es.enter_context(nc.sbuf_tensor("aw", [P, 4 * WSL], BF16))
        a32 = es.enter_context(nc.sbuf_tensor("a32", [P, 5 * T], F32))
        AMB = max(DC * P, 8 * TH)
        amisc = es.enter_context(nc.sbuf_tensor("amisc", [P, AMB + 2 * T], BF16))
        rows_mu = es.enter_context(nc.sbuf_tensor("rows_mu", [1, T], F32))
        rowbf = es.enter_context(nc.sbuf_tensor("rowbf", [1, 8 * TH], BF16))
        m4sb = es.enter_context(nc.sbuf_tensor("m4sb", [P, 8], F32))
        seff = es.enter_context(nc.sbuf_tensor("seff", [P, 2], F32))
        scv = es.enter_context(nc.sbuf_tensor("scv", [P, 2 * NH + 2 * NKV], F32))
        idt32 = es.enter_context(nc.sbuf_tensor("idt32", [P, P], F32))
        idtbf = es.enter_context(nc.sbuf_tensor("idtbf", [P, P], BF16))
        o_r_bf = es.enter_context(nc.sbuf_tensor("o_r_bf", [1, P], BF16))
        o_c_bf = es.enter_context(nc.sbuf_tensor("o_c_bf", [P, 1], BF16))
        o_r_32 = es.enter_context(nc.sbuf_tensor("o_r_32", [1, P], F32))
        n1_r_bf = es.enter_context(nc.sbuf_tensor("n1_r_bf", [1, P], BF16))
        n1_r_32 = es.enter_context(nc.sbuf_tensor("n1_r_32", [1, P], F32))
        nbias = es.enter_context(nc.sbuf_tensor("nbias", [P, 2], F32))
        cnst = es.enter_context(nc.sbuf_tensor("cnst", [1, 2], F32))

        pb = [es.enter_context(nc.psum_tensor(f"pb{i}", [P, TH], F32)) for i in range(7)]
        pbf = es.enter_context(nc.psum_tensor("pbf", [P, 2 * TH], BF16))

        _pools = {
            "gdma": [es.enter_context(nc.semaphore(f"spg{i}")) for i in range(3)],
            "sdma": [es.enter_context(nc.semaphore(f"sps{i}")) for i in range(6)],
            "eng": [es.enter_context(nc.semaphore(f"spe{i}")) for i in range(16)],
        }
        _pool_ps = {}
        _pool_i = {"gdma": 0, "sdma": 0, "eng": 0}

        def S(name):
            if name.endswith("wd"):
                kind = "gdma"
            elif name.endswith(("ld", "st", "kv")):
                kind = "sdma"
            else:
                kind = "eng"
            pool = _pools[kind]
            i = _pool_i[kind] % len(pool)
            _pool_i[kind] += 1
            h = pool[i]
            prev = _pool_ps.get((kind, i))
            start = prev.n if prev is not None else 0
            ps = PSem(h, start)
            _pool_ps[(kind, i)] = ps
            return ps

        def wslot(i, width=None):
            width = WSL if width is None else width
            return aw[:, i * WSL:i * WSL + width]

        def s32(i, th=None):
            if th is None:
                return a32[:, i * T:(i + 1) * T]
            return a32[:, i * T + th * TH: i * T + (th + 1) * TH]

        rows_sc = a32[0:1, 1 * T:2 * T]
        rows_rs = a32[0:1, 2 * T:3 * T]

        def xlnc(c, th=None):
            if th is None:
                return xln[:, c * T:(c + 1) * T]
            return xln[:, c * T + th * TH: c * T + th * TH + TH]

        def bigc(c, th=None):
            if th is None:
                return big[:, c * T:(c + 1) * T]
            return big[:, c * T + th * TH: c * T + th * TH + TH]

        absb = amisc[:, 0:DC * P]
        kst = amisc[:, AMB:AMB + T]
        vst = amisc[:, AMB + T:AMB + 2 * T]
        vst2 = kst

        def sqv(i):
            return amisc[:, i * T:(i + 1) * T]

        def psb_tile(i):
            return amisc[:, AMB + i * TH:AMB + (i + 1) * TH]

        def ring(u):
            i = u % 8
            return amisc[:, i * TH:(i + 1) * TH]

        def sum_row(th, which, banks=None):
            banks = banks if banks is not None else (4, 5, 6, 3)
            bank = pb[banks[{(0, 0): 0, (0, 1): 1, (1, 0): 2, (1, 1): 3}[(th, which)]]]
            return bank[0:1, :]

        with nc.Block("init") as blk:
            s_ld = S("i_ld")

            @blk.sync
            def _(sync):
                for dst, src in [
                    (idt32[:, :], ident32.ap()), (idtbf[:, :], identbf.ap()),
                    (o_r_bf[:, :], ones_r_bf.ap()), (o_c_bf[:, :], ones_c_bf.ap()),
                    (o_r_32[:, :], ones_r_32.ap()),
                    (n1_r_bf[:, :], neg1_r_bf.ap()),
                    (n1_r_32[:, :], neg1_r_32.ap()),
                    (scv[:, 0:NH], qsc), (scv[:, NH:NH + NKV], ksc),
                    (scv[:, NH + NKV:NH + 2 * NKV], vsc),
                    (scv[:, NH + 2 * NKV:2 * NH + 2 * NKV], osc),
                ]:
                    s_ld.inc(sync.dma_start(out=dst, in_=src), 16)
                sync.wait_ge(s_ld.h, s_ld.at(16 * 9))

            @blk.vector
            def _(vector):
                vector.memset(cnst[0:1, 0:1], EPS)
                vector.memset(cnst[0:1, 1:2], -0.5)

        def ln_phase(name, srcs, write_to):
            ns = len(srcs)
            with nc.Block(name) as blk:
                s_ld, s_st, s_ve, s_ac, s_mm = (S(f"{name}ld"), S(f"{name}st"), S(f"{name}ve"),
                                                S(f"{name}ac"), S(f"{name}mm"))
                vp = max(ns - 1, 0)
                xs = 4 if ns > 1 else 0

                @blk.sync
                def _(sync):
                    for c in range(DC):
                        if c >= 1:
                            sync.wait_ge(s_ac.h, s_ac.at(2 * c))
                            if vp:
                                sync.wait_ge(s_ve.h, s_ve.at(vp * c))
                        for j, src in enumerate(srcs):
                            s_ld.inc(sync.dma_start(out=s32(j), in_=src[c * P:(c + 1) * P, :]), 16)
                        if write_to is not None:
                            sync.wait_ge(s_ve.h, s_ve.at(vp * (c + 1)))
                            s_st.inc(sync.dma_start(out=write_to[c * P:(c + 1) * P, :], in_=s32(xs)), 16)
                    if write_to is not None:
                        sync.wait_ge(s_st.h, s_st.at(16 * DC))

                if vp:
                    @blk.vector
                    def _(vector):
                        for c in range(DC):
                            vector.wait_ge(s_ld.h, s_ld.at(16 * ns * (c + 1)))
                            if c >= 1:
                                vector.wait_ge(s_ac.h, s_ac.at(2 * c))
                                if write_to is not None:
                                    vector.wait_ge(s_st.h, s_st.at(16 * c))
                            s_ve.inc(vector.tensor_add(out=s32(xs), in0=s32(0), in1=s32(1)), 1)
                            for j in range(2, ns):
                                s_ve.inc(vector.tensor_add(out=s32(xs), in0=s32(xs), in1=s32(j)), 1)

                @blk.scalar
                def _(scalar):
                    for c in range(DC):
                        if vp:
                            scalar.wait_ge(s_ve.h, s_ve.at(vp * (c + 1)))
                        else:
                            scalar.wait_ge(s_ld.h, s_ld.at(16 * ns * (c + 1)))
                        if c >= 2:
                            scalar.wait_ge(s_mm.h, s_mm.at(4 * (c - 1)))
                        s_ac.inc(scalar.activation(out=xlnc(c), in_=s32(xs), func=AF.Copy), 1)
                        s_ac.inc(scalar.activation(out=sqv(c % 2), in_=s32(xs), func=AF.Square), 1)

                @blk.tensor
                def _(tensor):
                    for c in range(DC):
                        tensor.wait_ge(s_ac.h, s_ac.at(2 * (c + 1)))
                        for th in range(2):
                            s_mm.inc(tensor.matmul(sum_row(th, 0), o_c_bf[:, :], xlnc(c, th),
                                          start=(c == 0), stop=(c == DC - 1)), 1)
                            s_mm.inc(tensor.matmul(sum_row(th, 1), o_c_bf[:, :],
                                          sqv(c % 2)[:, th * TH:th * TH + TH],
                                          start=(c == 0), stop=(c == DC - 1)), 1)

            ln_fin(name + "f")

        def ln_fin(name, banks=None):
            with nc.Block(name) as blk:
                s_ve, s_ac, s_mm = S(f"{name}ve"), S(f"{name}ac"), S(f"{name}mm")

                @blk.vector
                def _(vector):
                    for th in range(2):
                        mu = rows_mu[0:1, th * TH:(th + 1) * TH]
                        sc = rows_sc[0:1, th * TH:(th + 1) * TH]
                        rs = rows_rs[0:1, th * TH:(th + 1) * TH]
                        s_ve.inc(vector.tensor_scalar(out=mu, in0=sum_row(th, 0, banks), scalar1=1.0 / D,
                                             scalar2=None, op0=ALU.mult), 1)
                        s_ve.inc(vector.tensor_scalar(out=sc, in0=sum_row(th, 1, banks), scalar1=1.0 / D,
                                             scalar2=None, op0=ALU.mult), 1)
                        s_ve.inc(vector.tensor_mul(out=rs, in0=mu, in1=mu), 1)
                        s_ve.inc(vector.tensor_sub(out=sc, in0=sc, in1=rs), 1)
                    vector.wait_ge(s_mm.h, s_mm.at(4))
                    for c in range(DC):
                        for th in range(2):
                            mb = pb[th]
                            rb = pb[2 + th]
                            t1 = s32(0, th)
                            s_ve.inc(vector.tensor_sub(out=t1, in0=xlnc(c, th), in1=mb[:, :]), 1)
                            s_ve.inc(vector.tensor_mul(out=xlnc(c, th), in0=t1, in1=rb[:, :]), 1)

                @blk.scalar
                def _(scalar):
                    for th in range(2):
                        scalar.wait_ge(s_ve.h, s_ve.at(4 * (th + 1)))
                        sc = rows_sc[0:1, th * TH:(th + 1) * TH]
                        rs = rows_rs[0:1, th * TH:(th + 1) * TH]
                        s_ac.inc(scalar.activation(out=sc, in_=sc, func=AF.Ln, bias=cnst[0:1, 0:1]), 1)
                        s_ac.inc(scalar.activation(out=rs, in_=sc, func=AF.Exp, scale=cnst[0:1, 1:2]), 1)

                @blk.tensor
                def _(tensor):
                    for th in range(2):
                        tensor.wait_ge(s_ac.h, s_ac.at(2 * (th + 1)))
                        mu = rows_mu[0:1, th * TH:(th + 1) * TH]
                        rs = rows_rs[0:1, th * TH:(th + 1) * TH]
                        s_mm.inc(tensor.matmul(pb[th][:, :], o_r_32[:, :], mu, start=True, stop=True), 1)
                        s_mm.inc(tensor.matmul(pb[2 + th][:, :], o_r_32[:, :], rs, start=True, stop=True), 1)

        def ffn_phase(name, tail=None, tail_srcs=None, tail_out=None):
            for t, (c0, c1) in enumerate(cfg.isplits):
                ncnk = c1 - c0
                with nc.Block(f"{name}g{t}") as blk:
                    s_wd, s_mm, s_ac, s_ve = (S(f"{name}g{t}wd"), S(f"{name}g{t}mm"),
                                              S(f"{name}g{t}ac"), S(f"{name}g{t}ve"))

                    @blk.gpsimd
                    def _(gpsimd):
                        for sx in range(ncnk):
                            if sx >= 2:
                                gpsimd.wait_ge(s_mm.h, s_mm.at(sx - 1))
                            gs = c0 + sx
                            s_wd.inc(gpsimd.dma_start(
                                out=wslot(sx % 2, DC * P).rearrange("p (c j) -> p c j", j=P),
                                in_=wg_t[gs]), 16)
                            s_wd.inc(gpsimd.dma_start(
                                out=wslot(2 + sx % 2, DC * P).rearrange("p (c j) -> p c j", j=P),
                                in_=wu_t[gs]), 16)

                    @blk.tensor
                    def _(tensor):
                        for sx in range(ncnk):
                            tensor.wait_ge(s_wd.h, s_wd.at(32 * (sx + 1)))
                            if sx >= 1:
                                tensor.wait_ge(s_ac.h, s_ac.at(sx))
                                tensor.wait_ge(s_ve.h, s_ve.at(sx))
                            g = wslot(sx % 2, DC * P)
                            u = wslot(2 + sx % 2, DC * P)
                            for c in range(DC):
                                for th in range(2):
                                    tensor.matmul(pb[th][:, :], g[:, c * P:(c + 1) * P], xlnc(c, th),
                                                  start=(c == 0), stop=(c == DC - 1))
                                    m = tensor.matmul(pb[2 + th][:, :], u[:, c * P:(c + 1) * P], xlnc(c, th),
                                                      start=(c == 0), stop=(c == DC - 1))
                            s_mm.inc(m, 1)

                    @blk.scalar
                    def _(scalar):
                        for sx in range(ncnk):
                            scalar.wait_ge(s_mm.h, s_mm.at(sx + 1))
                            if sx >= 1:
                                scalar.wait_ge(s_ve.h, s_ve.at(sx))
                            for th in range(2):
                                a = scalar.activation(out=s32(0, th), in_=pb[th][:, :], func=AF.Silu)
                            s_ac.inc(a, 1)

                    @blk.vector
                    def _(vector):
                        for sx in range(ncnk):
                            vector.wait_ge(s_ac.h, s_ac.at(sx + 1))
                            for th in range(2):
                                v = vector.tensor_mul(out=bigc(sx, th), in0=s32(0, th), in1=pb[2 + th][:, :])
                            s_ve.inc(v, 1)

                last = (t == 2) and tail is not None
                with nc.Block(f"{name}d{t}") as blk:
                    s_wd, s_mm, s_ac, s_st = (S(f"{name}d{t}wd"), S(f"{name}d{t}mm"),
                                              S(f"{name}d{t}ac"), S(f"{name}d{t}st"))
                    if last:
                        s_ld2, s_st2, s_ve2 = S(f"{name}dtld"), S(f"{name}dtst"), S(f"{name}dtve")
                        s_ac3, s_mm3 = S(f"{name}dtac"), S(f"{name}dtmm")

                    @blk.gpsimd
                    def _(gpsimd):
                        for o in range(DC):
                            if o >= 2:
                                gpsimd.wait_ge(s_mm.h, s_mm.at(o - 1))
                            s_wd.inc(gpsimd.dma_start(
                                out=wslot(o % 2, ncnk * P).rearrange("p (c j) -> p c j", j=P),
                                in_=wd_t[o][:, c0:c1, :]), 16)

                    def _tail_sums(tensor, c):
                        tensor.wait_ge(s_ac3.h, s_ac3.at(2 * (c + 1)))
                        for th in range(2):
                            s_mm3.inc(tensor.matmul(sum_row(th, 0, (0, 1, 2, 3)), o_c_bf[:, :],
                                                    xlnc(c, th), start=(c == 0), stop=(c == DC - 1)), 1)
                            s_mm3.inc(tensor.matmul(sum_row(th, 1, (0, 1, 2, 3)), o_c_bf[:, :],
                                                    sqv(c % 2)[:, th * TH:th * TH + TH],
                                                    start=(c == 0), stop=(c == DC - 1)), 1)

                    @blk.tensor
                    def _(tensor):
                        for o in range(DC):
                            tensor.wait_ge(s_wd.h, s_wd.at(16 * (o + 1)))
                            if o >= 1:
                                tensor.wait_ge(s_ac.h, s_ac.at(o))
                            w = wslot(o % 2, ncnk * P)
                            for c in range(ncnk):
                                for th in range(2):
                                    m = tensor.matmul(pb[4 + th][:, :], w[:, c * P:(c + 1) * P], bigc(c, th),
                                                      start=(c == 0), stop=(c == ncnk - 1))
                            s_mm.inc(m, 1)
                            if last and tail == "ln" and o >= 2:
                                _tail_sums(tensor, o - 2)
                        if last and tail == "ln":
                            _tail_sums(tensor, DC - 2)
                            _tail_sums(tensor, DC - 1)

                    @blk.scalar
                    def _(scalar):
                        for o in range(DC):
                            scalar.wait_ge(s_mm.h, s_mm.at(o + 1))
                            if not last:
                                if o >= 2:
                                    scalar.wait_ge(s_st.h, s_st.at(16 * (o - 1)))
                            else:
                                if o >= 1:
                                    scalar.wait_ge(s_ve2.h, s_ve2.at(3 * o))
                            for th in range(2):
                                a = scalar.activation(out=s32(2 + o % 2, th), in_=pb[4 + th][:, :],
                                                      func=AF.Copy, scale=0.5)
                            s_ac.inc(a, 1)
                            if last and tail == "ln":
                                scalar.wait_ge(s_ve2.h, s_ve2.at(3 * (o + 1)))
                                if o >= 2:
                                    scalar.wait_ge(s_mm3.h, s_mm3.at(4 * (o - 1)))
                                s_ac3.inc(scalar.activation(out=xlnc(o), in_=s32(0), func=AF.Copy), 1)
                                s_ac3.inc(scalar.activation(out=sqv(o % 2), in_=s32(0), func=AF.Square), 1)

                    if last:
                        @blk.vector
                        def _(vector):
                            for o in range(DC):
                                vector.wait_ge(s_ld2.h, s_ld2.at(16 * 3 * (o + 1)))
                                vector.wait_ge(s_ac.h, s_ac.at(o + 1))
                                if o >= 1:
                                    vector.wait_ge(s_st2.h, s_st2.at(16 * o))
                                    if tail == "ln":
                                        vector.wait_ge(s_ac3.h, s_ac3.at(2 * o))
                                s_ve2.inc(vector.tensor_add(out=s32(0), in0=s32(0), in1=s32(1)), 1)
                                s_ve2.inc(vector.tensor_add(out=s32(0), in0=s32(0), in1=s32(4)), 1)
                                s_ve2.inc(vector.tensor_add(out=s32(0), in0=s32(0), in1=s32(2 + o % 2)), 1)

                    @blk.sync
                    def _(sync):
                        if not last:
                            for o in range(DC):
                                sync.wait_ge(s_ac.h, s_ac.at(o + 1))
                                s_st.inc(sync.dma_start(out=part[t][o * P:(o + 1) * P, :],
                                                        in_=s32(2 + o % 2)), 16)
                            sync.wait_ge(s_st.h, s_st.at(16 * DC))
                        else:
                            for o in range(DC):
                                if o >= 1:
                                    sync.wait_ge(s_ve2.h, s_ve2.at(3 * o))
                                for j, src in enumerate(tail_srcs):
                                    slot = (0, 1, 4)[j]
                                    s_ld2.inc(sync.dma_start(out=s32(slot),
                                                             in_=src[o * P:(o + 1) * P, :]), 16)
                                sync.wait_ge(s_ve2.h, s_ve2.at(3 * (o + 1)))
                                s_st2.inc(sync.dma_start(out=tail_out[o * P:(o + 1) * P, :],
                                                         in_=s32(0)), 16)
                            sync.wait_ge(s_st2.h, s_st2.at(16 * DC))

        def bitnet_proj(name, w_ap, n_o, scol_off, extra_scale, dest,
                        kv_store=None, v_transpose=False, pre_gpsimd=None):
            with nc.Block(name) as blk:
                s_wd, s_ab, s_sg = S(f"{name}wd"), S(f"{name}ab"), S(f"{name}sg")
                s_sm, s_cl, s_tp = S(f"{name}sm"), S(f"{name}cl"), S(f"{name}tp")
                s_se, s_pm, s_ev = S(f"{name}se"), S(f"{name}pm"), S(f"{name}ev")
                s_st, s_vt, s_vc = S(f"{name}st"), S(f"{name}vt"), S(f"{name}vc")

                @blk.gpsimd
                def _(gpsimd):
                    if pre_gpsimd is not None:
                        pre_gpsimd(gpsimd)
                    for o in range(n_o):
                        if o >= 2:
                            gpsimd.wait_ge(s_pm.h, s_pm.at(o - 1))
                        s_wd.inc(gpsimd.dma_start(
                            out=wslot(o % 2, DC * P).rearrange("p (c j) -> p c j", j=P),
                            in_=w_ap[o]), 16)

                @blk.scalar
                def _(scalar):
                    for o in range(n_o):
                        scalar.wait_ge(s_wd.h, s_wd.at(16 * (o + 1)))
                        if o >= 1:
                            scalar.wait_ge(s_sm.h, s_sm.at(o))
                        w = wslot(o % 2, DC * P)
                        s_ab.inc(scalar.activation(out=absb, in_=w, func=AF.Abs), 1)
                        s_sg.inc(scalar.activation(out=w, in_=w, func=AF.Sign), 1)
                        scalar.wait_ge(s_pm.h, s_pm.at(o + 1))
                        scalar.wait_ge(s_se.h, s_se.at(o + 1))
                        if kv_store is not None and o >= 2:
                            scalar.wait_ge(s_st.h, s_st.at(16 * (o - 1)))
                        sc = seff[:, o % 2:o % 2 + 1]
                        for th in range(2):
                            if kv_store is None:
                                dst = dest(o, th)
                            elif v_transpose:
                                dst = vst[:, th * TH:(th + 1) * TH]
                            else:
                                dst = kst[:, th * TH:(th + 1) * TH]
                            a = scalar.activation(out=dst, in_=pb[th][:, :], func=AF.Copy, scale=sc)
                        s_ev.inc(a, 1)
                        if v_transpose:
                            scalar.wait_ge(s_vt.h, s_vt.at(o + 1))
                            if o >= 1:
                                scalar.wait_ge(s_st.h, s_st.at(16 * o))
                            s_vc.inc(scalar.activation(out=vst2, in_=pbf[:, 0:T], func=AF.Copy), 1)

                @blk.tensor
                def _(tensor):
                    for o in range(n_o):
                        tensor.wait_ge(s_ab.h, s_ab.at(o + 1))
                        if o >= 1:
                            tensor.wait_ge(s_cl.h, s_cl.at(o))
                        for c in range(DC):
                            m = tensor.matmul(pb[6][0:1, 0:P], o_c_bf[:, :], absb[:, c * P:(c + 1) * P],
                                              start=(c == 0), stop=(c == DC - 1))
                        s_sm.inc(m, 1)
                        tensor.wait_ge(s_cl.h, s_cl.at(o + 1))
                        if o >= 1:
                            tensor.wait_ge(s_se.h, s_se.at(o))
                        s_tp.inc(tensor.transpose(pb[6][:, 4:5], rows_sc[0:1, 0:P], idt32[0:1, 0:1]), 1)
                        tensor.wait_ge(s_sg.h, s_sg.at(o + 1))
                        if o >= 1:
                            tensor.wait_ge(s_ev.h, s_ev.at(o))
                        w = wslot(o % 2, DC * P)
                        for c in range(DC):
                            for th in range(2):
                                m = tensor.matmul(pb[th][:, :], w[:, c * P:(c + 1) * P], xlnc(c, th),
                                                  start=(c == 0), stop=(c == DC - 1))
                        s_pm.inc(m, 1)
                        if v_transpose:
                            tensor.wait_ge(s_ev.h, s_ev.at(o + 1))
                            if o >= 1:
                                tensor.wait_ge(s_vc.h, s_vc.at(o))
                            for tt in range(TN):
                                m = tensor.transpose(pbf[:, tt * P:(tt + 1) * P],
                                                     vst[:, tt * P:(tt + 1) * P],
                                                     idtbf[:, :])
                            s_vt.inc(m, 1)

                @blk.vector
                def _(vector):
                    for o in range(n_o):
                        vector.wait_ge(s_sm.h, s_sm.at(o + 1))
                        s_cl.inc(vector.tensor_scalar(out=rows_sc[0:1, 0:P], in0=pb[6][0:1, 0:P],
                                             scalar1=1.0 / D, scalar2=1e-8,
                                             op0=ALU.mult, op1=ALU.max), 1)
                        vector.wait_ge(s_tp.h, s_tp.at(o + 1))
                        if o >= 1:
                            vector.wait_ge(s_ev.h, s_ev.at(o))
                        s_se.inc(vector.scalar_tensor_tensor(out=seff[:, o % 2:o % 2 + 1], in0=pb[6][:, 4:5],
                                                    scalar=extra_scale,
                                                    in1=scv[:, scol_off + o:scol_off + o + 1],
                                                    op0=ALU.mult, op1=ALU.mult), 1)

                if kv_store is not None:
                    @blk.sync
                    def _(sync):
                        for o in range(n_o):
                            if not v_transpose:
                                sync.wait_ge(s_ev.h, s_ev.at(o + 1))
                                s_st.inc(sync.dma_start(out=cc_in.ap()[o * P:(o + 1) * P, :],
                                               in_=kst), 16)
                            else:
                                sync.wait_ge(s_vc.h, s_vc.at(o + 1))
                                s_st.inc(sync.dma_start(
                                    out=cc_in.ap()[T:2 * T, o * P:(o + 1) * P]
                                    .rearrange("(tt p) j -> p tt j", p=P),
                                    in_=vst2.rearrange("p (tt j) -> p tt j", j=P)), 16)
                        sync.wait_ge(s_st.h, s_st.at(16 * n_o))

        def attn_phase(name, s_cc_handle):
            with nc.Block(name) as blk:
                s_kv, s_rk, s_nb = S(f"{name}kv"), S(f"{name}rk"), S(f"{name}nb")
                s_nc = S(f"{name}nc")
                s_nk, s_sq = S(f"{name}nk"), S(f"{name}sq")
                s_sb, s_ex, s_av = S(f"{name}sb"), S(f"{name}ex"), S(f"{name}av")
                s_tn, s_rc, s_ib = S(f"{name}tn"), S(f"{name}rc"), S(f"{name}ib")
                s_iv, s_ao = S(f"{name}iv"), S(f"{name}ao")
                UPH = GR * 2
                NU = NKV * UPH
                NKB = KEYS // TH
                NP = KTN // 2
                NPH = NKB + UPH

                def unit_of(h, uh):
                    u = h * UPH + uh
                    return u, h * GR + uh // 2, uh % 2

                def nrow_bank(i):
                    return (pb[5] if i % 2 == 0 else pb[6])[0:1, :]

                pbf32 = pbf[:, :].bitcast(F32)
                sqK = amisc[:, 0:KEYS]
                sqQ = amisc[:, AMB + T:AMB + 2 * T]
                rkp = a32[0:1, 0:8]
                rk2 = a32[0:1, 8:9]
                qm2 = a32[0:1, 9:10]
                lnr = a32[0:1, 10:11]

                def invb(u):
                    return s32(2, u % 2)

                def mrow_inv(u):
                    return rows_mu[0:1, (u % 2) * TH:(u % 2) * TH + TH]

                @blk.sync
                def _(sync):
                    sync.wait_ge(s_cc_handle.h, s_cc_handle.at(1))
                    for h in range(NKV):
                        if h >= 2:
                            sync.wait_ge(s_tn.h, s_tn.at(UPH * (h - 1)))
                        kslot = wslot(h % 2, KEYS)
                        vslot = wslot(2 + h % 2, KEYS)
                        for r in range(R):
                            s_kv.inc(sync.dma_start(
                                out=kslot[:, r * T:(r + 1) * T],
                                in_=cc_out.ap()[r * 2 * T + h * P: r * 2 * T + h * P + P, :]), 16)
                            s_kv.inc(sync.dma_start(
                                out=vslot.rearrange("p (kt j) -> p kt j", j=P)[:, r * TN:(r + 1) * TN, :],
                                in_=cc_out.ap()[r * 2 * T + T: r * 2 * T + 2 * T, h * P:(h + 1) * P]
                                .rearrange("(kt p) j -> p kt j", p=P)), 16)

                @blk.scalar
                def _(scalar):
                    for h in range(NKV):
                        scalar.wait_ge(s_kv.h, s_kv.at(16 * 2 * R * (h + 1)))
                        if h >= 1:
                            scalar.wait_ge(s_nk.h, s_nk.at(NPH * h - UPH))
                        s_sq.inc(scalar.activation(out=sqK, in_=wslot(h % 2, KEYS), func=AF.Square), 1)
                        for uh in range(UPH):
                            u, g, qt = unit_of(h, uh)
                            if uh % 2 == 0:
                                if u >= 1:
                                    scalar.wait_ge(s_nk.h, s_nk.at(NKB * (h + 1) + UPH * h + uh))
                                s_sq.inc(scalar.activation(out=sqQ[:, 0:T], in_=bigc(g), func=AF.Square), 1)
                            scalar.wait_ge(s_rk.h, s_rk.at((NKB + 1 + 2 * uh + 1) + (NKB + 1 + 2 * UPH) * h))
                            if u >= 1:
                                scalar.wait_ge(s_nb.h, s_nb.at(2 * u))
                            scalar.activation(out=lnr, in_=qm2, func=AF.Ln, scale=rk2)
                            s_nb.inc(scalar.activation(out=lnr, in_=lnr, func=AF.Exp, scale=0.5), 1)
                            scalar.wait_ge(s_nb.h, s_nb.at(2 * u + 2))
                            s_nc.inc(scalar.activation(out=nbias[:, u % 2:u % 2 + 1],
                                                       in_=pbf32[:, 0:1], func=AF.Copy), 1)
                        psB = [pb[2], pb[3], pb[6]]
                        for uh in range(UPH):
                            u, g, qt = unit_of(h, uh)
                            for kt in range(KTN):
                                gi = u * KTN + kt
                                scalar.wait_ge(s_sb.h, s_sb.at(gi + 1))
                                scalar.wait_ge(s_nc.h, s_nc.at(u + 1))
                                if gi >= 3:
                                    scalar.wait_ge(s_av.h, s_av.at(gi - 2))
                                s_ex.inc(scalar.activation(out=psb_tile(kt % 3), in_=psB[kt % 3][:, :],
                                                           func=AF.Exp,
                                                           bias=nbias[:, u % 2:u % 2 + 1]), 1)
                            scalar.wait_ge(s_ib.h, s_ib.at(u + 1))
                            if u >= 2:
                                scalar.wait_ge(s_ao.h, s_ao.at(u - 1))
                            s_iv.inc(scalar.activation(out=invb(u), in_=pb[5][:, :],
                                                       func=AF.Copy), 1)

                @blk.tensor
                def _(tensor):
                    for h in range(NKV):
                        kslot = wslot(h % 2, KEYS)
                        vslot = wslot(2 + h % 2, KEYS)
                        tensor.wait_ge(s_sq.h, s_sq.at((1 + GR) * h + 1))
                        if h >= 1:
                            tensor.wait_ge(s_iv.h, s_iv.at(UPH * h))
                        for kb in range(NKB):
                            if kb >= 2:
                                tensor.wait_ge(s_rk.h, s_rk.at((NKB + 1 + 2 * UPH) * h + kb - 1))
                            s_nk.inc(tensor.matmul(nrow_bank(kb), o_c_bf[:, :],
                                                   sqK[:, kb * TH:(kb + 1) * TH],
                                                   start=True, stop=True), 1)
                        for uh in range(UPH):
                            u, g, qt = unit_of(h, uh)
                            if uh % 2 == 0:
                                tensor.wait_ge(s_sq.h, s_sq.at((1 + GR) * h + 2 + uh // 2))
                            if uh < 2:
                                tensor.wait_ge(s_rk.h, s_rk.at((NKB + 1 + 2 * UPH) * h + NKB - 1 + uh))
                            else:
                                tensor.wait_ge(s_rk.h, s_rk.at((NKB + 1 + 2 * UPH) * h + NKB + 1 + 2 * (uh - 2) + 2))
                            s_nk.inc(tensor.matmul(nrow_bank(NKB + uh), o_c_bf[:, :],
                                                   sqQ[:, qt * TH:(qt + 1) * TH],
                                                   start=True, stop=True), 1)
                            tensor.wait_ge(s_nb.h, s_nb.at(2 * u + 1))
                            if u >= 1:
                                tensor.wait_ge(s_nc.h, s_nc.at(u))
                            m = tensor.matmul(pbf32[:, 0:1], n1_r_32[:, :], lnr,
                                              start=True, stop=True)
                            s_nb.inc(m, 1)
                        psB = [pb[2], pb[3], pb[6]]
                        for uh in range(UPH):
                            u, g, qt = unit_of(h, uh)
                            rhs_q = bigc(g, qt)
                            for kt in range(KTN):
                                gi = u * KTN + kt
                                if gi >= 3:
                                    tensor.wait_ge(s_ex.h, s_ex.at(gi - 2))
                                s_sb.inc(tensor.matmul(psB[kt % 3][:, :], kslot[:, kt * P:(kt + 1) * P],
                                                       rhs_q, start=True, stop=True), 1)
                                if kt >= 2:
                                    kk = kt - 2
                                    tensor.wait_ge(s_ex.h, s_ex.at(gi - 1))
                                    if kt == 2:
                                        tensor.wait_ge(s_ao.h, s_ao.at(u))
                                        tensor.wait_ge(s_iv.h, s_iv.at(u))
                                    s_av.inc(tensor.matmul(pb[4][:, :], vslot[:, kk * P:(kk + 1) * P],
                                                           psb_tile(kk % 3), start=(kk == 0),
                                                           stop=False), 1)
                                    tensor.matmul(pb[5][0:1, :], o_c_bf[:, :], psb_tile(kk % 3),
                                                  start=(kk == 0), stop=False)
                            for kk in (KTN - 2, KTN - 1):
                                tensor.wait_ge(s_ex.h, s_ex.at(u * KTN + kk + 1))
                                s_av.inc(tensor.matmul(pb[4][:, :], vslot[:, kk * P:(kk + 1) * P],
                                                       psb_tile(kk % 3), start=False,
                                                       stop=(kk == KTN - 1)), 1)
                                m = tensor.matmul(pb[5][0:1, :], o_c_bf[:, :], psb_tile(kk % 3),
                                                  start=False, stop=(kk == KTN - 1))
                            s_tn.inc(m, 1)
                            tensor.wait_ge(s_rc.h, s_rc.at(u + 1))
                            s_ib.inc(tensor.matmul(pb[5][:, :], o_r_32[:, :],
                                                   mrow_inv(u), start=True, stop=True), 1)

                @blk.vector
                def _(vector):
                    for h in range(NKV):
                        for kb in range(NKB):
                            vector.wait_ge(s_nk.h, s_nk.at(NPH * h + kb + 1))
                            s_rk.inc(vector.reduce_max(out=rkp[0:1, kb:kb + 1],
                                                       in_=nrow_bank(kb), axis=AX.X), 1)
                        s_rk.inc(vector.reduce_max(out=rk2, in_=rkp, axis=AX.X), 1)
                        for uh in range(UPH):
                            u, g, qt = unit_of(h, uh)
                            vector.wait_ge(s_nk.h, s_nk.at(NPH * h + NKB + uh + 1))
                            s_rk.inc(vector.reduce_max(out=qm2, in_=nrow_bank(NKB + uh),
                                                       axis=AX.X), 1)
                            vector.wait_ge(s_nb.h, s_nb.at(2 * u + 1))
                            s_rk.inc(vector.tensor_copy(out=rkp[0:1, 7:8], in_=qm2), 1)
                        for uh in range(UPH):
                            u, g, qt = unit_of(h, uh)
                            vector.wait_ge(s_tn.h, s_tn.at(u + 1))
                            if u >= 1:
                                vector.wait_ge(s_ib.h, s_ib.at(u - 1))
                            s_rc.inc(vector.reciprocal(out=mrow_inv(u), in_=pb[5][0:1, :]), 1)
                            vector.wait_ge(s_iv.h, s_iv.at(u + 1))
                            s_ao.inc(vector.tensor_mul(out=bigc(g, qt), in0=pb[4][:, :],
                                                       in1=invb(u)), 1)

        def oproj_phase(name):
            with nc.Block(name) as blk:
                s_wd, s_ab, s_sg = S(f"{name}wd"), S(f"{name}ab"), S(f"{name}sg")
                s_sm, s_cl, s_tp = S(f"{name}sm"), S(f"{name}cl"), S(f"{name}tp")
                s_se, s_pm, s_ev = S(f"{name}se"), S(f"{name}pm"), S(f"{name}ev")
                s_ld, s_ve, s_st = S(f"{name}ld"), S(f"{name}ve"), S(f"{name}st")
                s_ac2, s_mm2 = S(f"{name}ac2"), S(f"{name}mm2")

                @blk.gpsimd
                def _(gpsimd):
                    for o in range(DC):
                        if o >= 2:
                            gpsimd.wait_ge(s_pm.h, s_pm.at(o - 1))
                        s_wd.inc(gpsimd.dma_start(
                            out=wslot(o % 2, DC * P).rearrange("p (c j) -> p c j", j=P),
                            in_=wo_t[o]), 16)

                @blk.scalar
                def _(scalar):
                    for o in range(DC):
                        scalar.wait_ge(s_wd.h, s_wd.at(16 * (o + 1)))
                        if o >= 1:
                            scalar.wait_ge(s_sm.h, s_sm.at(o))
                        w = wslot(o % 2, DC * P)
                        s_ab.inc(scalar.activation(out=absb, in_=w, func=AF.Abs), 1)
                        s_sg.inc(scalar.activation(out=w, in_=w, func=AF.Sign), 1)
                        scalar.wait_ge(s_pm.h, s_pm.at(o + 1))
                        scalar.wait_ge(s_se.h, s_se.at(o + 1))
                        if o >= 1:
                            scalar.wait_ge(s_ve.h, s_ve.at(2 * o))
                        sc = seff[:, o % 2:o % 2 + 1]
                        for th in range(2):
                            a = scalar.activation(out=s32(2, th), in_=pb[th][:, :], func=AF.Copy, scale=sc)
                        s_ev.inc(a, 1)
                        scalar.wait_ge(s_ve.h, s_ve.at(2 * (o + 1)))
                        if o >= 2:
                            scalar.wait_ge(s_mm2.h, s_mm2.at(4 * (o - 1)))
                        s_ac2.inc(scalar.activation(out=xlnc(o), in_=s32(4), func=AF.Copy), 1)
                        s_ac2.inc(scalar.activation(out=(kst if o % 2 == 0 else vst),
                                                    in_=s32(4), func=AF.Square), 1)

                @blk.tensor
                def _(tensor):
                    for o in range(DC):
                        tensor.wait_ge(s_ab.h, s_ab.at(o + 1))
                        if o >= 1:
                            tensor.wait_ge(s_cl.h, s_cl.at(o))
                        for c in range(DC):
                            m = tensor.matmul(pb[2][0:1, 0:P], o_c_bf[:, :], absb[:, c * P:(c + 1) * P],
                                              start=(c == 0), stop=(c == DC - 1))
                        s_sm.inc(m, 1)
                        tensor.wait_ge(s_cl.h, s_cl.at(o + 1))
                        if o >= 1:
                            tensor.wait_ge(s_se.h, s_se.at(o))
                        s_tp.inc(tensor.transpose(pb[2][:, 4:5], rows_sc[0:1, 0:P], idt32[0:1, 0:1]), 1)
                        tensor.wait_ge(s_sg.h, s_sg.at(o + 1))
                        if o >= 1:
                            tensor.wait_ge(s_ev.h, s_ev.at(o))
                        w = wslot(o % 2, DC * P)
                        for c in range(DC):
                            for th in range(2):
                                m = tensor.matmul(pb[th][:, :], w[:, c * P:(c + 1) * P], bigc(c, th),
                                                  start=(c == 0), stop=(c == DC - 1))
                        s_pm.inc(m, 1)
                        tensor.wait_ge(s_ac2.h, s_ac2.at(2 * (o + 1)))
                        sq_t = kst if o % 2 == 0 else vst
                        for th in range(2):
                            s_mm2.inc(tensor.matmul(sum_row(th, 0), o_c_bf[:, :], xlnc(o, th),
                                          start=(o == 0), stop=(o == DC - 1)), 1)
                            s_mm2.inc(tensor.matmul(sum_row(th, 1), o_c_bf[:, :],
                                          sq_t[:, th * TH:th * TH + TH],
                                          start=(o == 0), stop=(o == DC - 1)), 1)

                @blk.vector
                def _(vector):
                    for o in range(DC):
                        vector.wait_ge(s_sm.h, s_sm.at(o + 1))
                        s_cl.inc(vector.tensor_scalar(out=rows_sc[0:1, 0:P], in0=pb[2][0:1, 0:P],
                                             scalar1=1.0 / D, scalar2=1e-8,
                                             op0=ALU.mult, op1=ALU.max), 1)
                        vector.wait_ge(s_tp.h, s_tp.at(o + 1))
                        if o >= 1:
                            vector.wait_ge(s_ev.h, s_ev.at(o))
                        s_se.inc(vector.scalar_tensor_tensor(out=seff[:, o % 2:o % 2 + 1], in0=pb[2][:, 4:5],
                                                    scalar=1.0,
                                                    in1=scv[:, NH + 2 * NKV + o:NH + 2 * NKV + o + 1],
                                                    op0=ALU.mult, op1=ALU.mult), 1)
                        vector.wait_ge(s_ev.h, s_ev.at(o + 1))
                        vector.wait_ge(s_ld.h, s_ld.at(16 * (o + 1)))
                        if o >= 1:
                            vector.wait_ge(s_st.h, s_st.at(16 * o))
                            vector.wait_ge(s_ac2.h, s_ac2.at(2 * o))
                        s_ve.inc(vector.tensor_add(out=s32(4), in0=s32(2), in1=s32(3)), 2)

                @blk.sync
                def _(sync):
                    for o in range(DC):
                        if o >= 1:
                            sync.wait_ge(s_ve.h, s_ve.at(2 * o))
                        s_ld.inc(sync.dma_start(out=s32(3), in_=xc2[o * P:(o + 1) * P, :]), 16)
                        sync.wait_ge(s_ve.h, s_ve.at(2 * (o + 1)))
                        s_st.inc(sync.dma_start(out=xc3[o * P:(o + 1) * P, :], in_=s32(4)), 16)
                    sync.wait_ge(s_st.h, s_st.at(16 * DC))

            ln_fin(name + "f")

        ln_phase("ln1", [xT], None)
        ffn_phase("f1", tail="ln", tail_srcs=[xT, part[0], part[1]], tail_out=xc2)
        ln_fin("ln2f", banks=(0, 1, 2, 3))

        s_cc = PSem(es.enter_context(nc.semaphore("cc")), 0)

        bitnet_proj("kp", wk_t, NKV, NH, 1.0, None, kv_store="k")
        bitnet_proj("vp", wv_t, NKV, NH + NKV, 1.0, None, kv_store="v", v_transpose=True)

        def q_pre(gpsimd):
            s_cc.inc(gpsimd.collective_compute(
                "AllGather", ALU.bypass,
                replica_groups=[[0, 1, 2, 3], [4, 5, 6, 7]],
                ins=[cc_in.ap().opt()],
                outs=[cc_out.ap().opt()],
            ), 1)

        bitnet_proj("qp", wq_t, NH, 0, 1.0 / math.sqrt(P), lambda o, th: bigc(o, th),
                    pre_gpsimd=q_pre)
        attn_phase("at", s_cc)
        oproj_phase("op")
        ffn_phase("f2", tail="fin", tail_srcs=[xc3, part[0], part[1]], tail_out=yT)

        dbg = {}
        dbg_xln = None
        if debug:
            for nm, src, shp, dt in [
                ("dbg_xc2", xc2, [D, T], F32), ("dbg_xc3", xc3, [D, T], F32),
                ("dbg_p0", part[0], [D, T], F32), ("dbg_p1", part[1], [D, T], F32),
                ("dbg_p2", part[2], [D, T], F32),
                ("dbg_ccin", cc_in.ap(), [2 * T, KV], BF16),
                ("dbg_ccout", cc_out.ap(), [2 * R * T, KV], BF16),
            ]:
                dbg[nm] = (nc.dram_tensor(nm, shp, dt, kind="ExternalOutput").ap(), src)
            dbg_xln = nc.dram_tensor("dbg_xln", [P, DC * T], BF16, kind="ExternalOutput").ap()
            with nc.Block("dbg") as blk:
                s_dst = S("dbgst")

                @blk.sync
                def _(sync):
                    n = 0
                    for nm, (dst, src) in dbg.items():
                        s_dst.inc(sync.dma_start(out=dst, in_=src), 16)
                        n += 1
                    s_dst.inc(sync.dma_start(out=dbg_xln, in_=xln[:, :]), 16)
                    sync.wait_ge(s_dst.h, s_dst.at(16 * (n + 1)))

    return nc


def _tile_T(w):
    O, IC_ = w.shape
    return np.ascontiguousarray(w.T.reshape(IC_ // P, P, O // P, P).transpose(2, 1, 0, 3))


def prep_inputs(cfg: Cfg, inp):
    T, I, R = cfg.T, cfg.I, cfg.R
    x = inp["L0_attn_hidden_states"]
    wgu = inp["L0_ffn_w_gate_up"]

    shared = {
        "wg_t": _tile_T(wgu[:I]),
        "wu_t": _tile_T(wgu[I:]),
        "wd_t": _tile_T(inp["L0_ffn_w_down"]),
        "wq_t": _tile_T(inp["L0_attn_q_weight"]),
        "wk_t": _tile_T(inp["L0_attn_k_weight"]),
        "wv_t": _tile_T(inp["L0_attn_v_weight"]),
        "wo_t": _tile_T(inp["L0_attn_o_weight"]),
        "qsc": np.ascontiguousarray(inp["L0_attn_q_scale"].reshape(-1, P).T),
        "ksc": np.ascontiguousarray(inp["L0_attn_k_scale"].reshape(-1, P).T),
        "vsc": np.ascontiguousarray(inp["L0_attn_v_scale"].reshape(-1, P).T),
        "osc": np.ascontiguousarray(inp["L0_attn_o_scale"].reshape(-1, P).T),
    }
    in_maps = []
    for c in range(8):
        b, q = c // R, c % R
        m = dict(shared)
        m["xT"] = np.ascontiguousarray(x[b, q * T:(q + 1) * T, :].T)
        in_maps.append(m)
    return in_maps


def kernel(**inputs):
    cfg = CFG
    in_maps = prep_inputs(cfg, inputs)
    nc = build_program(cfg)
    res = run_bass_kernel_spmd(nc, in_maps, core_ids=list(range(8)))
    B = 2
    y = np.empty((B, cfg.R * cfg.T, cfg.D), np.float32)
    for c in range(8):
        b, q = c // cfg.R, c % cfg.R
        y[b, q * cfg.T:(q + 1) * cfg.T, :] = res.results[c]["yT"].T
    return y


# version 13
# speedup vs baseline: 1.4654x; 1.0683x over previous
import math
from contextlib import ExitStack
from dataclasses import dataclass

import numpy as np
import ml_dtypes

import concourse.bass as bass
import concourse.mybir as mybir
from concourse.bass_utils import run_bass_kernel_spmd

F32 = mybir.dt.float32
BF16 = mybir.dt.bfloat16
AF = mybir.ActivationFunctionType
ALU = mybir.AluOpType
AX = mybir.AxisListType

P = 128
EPS = 1e-5


@dataclass(frozen=True)
class Cfg:
    D: int = 4096
    T: int = 1024
    I: int = 11008
    KV: int = 1024
    R: int = 4

    @property
    def DC(self):
        return self.D // P

    @property
    def IC(self):
        return self.I // P

    @property
    def NH(self):
        return self.D // P

    @property
    def NKV(self):
        return self.KV // P

    @property
    def TH(self):
        return self.T // 2

    @property
    def KEYS(self):
        return self.R * self.T

    @property
    def KTN(self):
        return self.KEYS // P

    @property
    def GR(self):
        return self.NH // self.NKV

    @property
    def isplits(self):
        n = self.IC
        a = (n + 2) // 3
        b = (n - a + 1) // 2
        return [(0, a), (a, a + b), (a + b, n)]


CFG = Cfg()


class PSem:

    def __init__(self, h, start):
        self.h = h
        self.n = start
        self.base = start

    def inc(self, instr, k=1):
        instr.then_inc(self.h, k)
        self.n += k
        return instr

    def at(self, v):
        return self.base + v


def build_program(cfg: Cfg, debug=False):
    nc = bass.Bass("TRN2", target_bir_lowering=False, detect_race_conditions=False)
    D, T, I, KV = cfg.D, cfg.T, cfg.I, cfg.KV
    DC, IC, NH, NKV, TH, R = cfg.DC, cfg.IC, cfg.NH, cfg.NKV, cfg.TH, cfg.R
    KTN, GR, KEYS = cfg.KTN, cfg.GR, cfg.KEYS
    TB = TH // P
    TN = T // P
    assert T == KV, "kernel assumes tokens-per-core == kv width"

    xT = nc.dram_tensor("xT", [D, T], F32, kind="ExternalInput").ap()
    wg_t = nc.dram_tensor("wg_t", [IC, P, DC, P], F32, kind="ExternalInput").ap()
    wu_t = nc.dram_tensor("wu_t", [IC, P, DC, P], F32, kind="ExternalInput").ap()
    wd_t = nc.dram_tensor("wd_t", [DC, P, IC, P], F32, kind="ExternalInput").ap()
    wq_t = nc.dram_tensor("wq_t", [NH, P, DC, P], F32, kind="ExternalInput").ap()
    wk_t = nc.dram_tensor("wk_t", [NKV, P, DC, P], F32, kind="ExternalInput").ap()
    wv_t = nc.dram_tensor("wv_t", [NKV, P, DC, P], F32, kind="ExternalInput").ap()
    wo_t = nc.dram_tensor("wo_t", [NH, P, DC, P], F32, kind="ExternalInput").ap()
    qsc = nc.dram_tensor("qsc", [P, NH], F32, kind="ExternalInput").ap()
    ksc = nc.dram_tensor("ksc", [P, NKV], F32, kind="ExternalInput").ap()
    vsc = nc.dram_tensor("vsc", [P, NKV], F32, kind="ExternalInput").ap()
    osc = nc.dram_tensor("osc", [P, NH], F32, kind="ExternalInput").ap()
    yT = nc.dram_tensor("yT", [D, T], F32, kind="ExternalOutput").ap()

    xc2 = nc.dram_tensor("xc2", [D, T], F32, kind="Internal").ap()
    xc3 = nc.dram_tensor("xc3", [D, T], F32, kind="Internal").ap()
    part = [nc.dram_tensor(f"part{i}", [D, T], F32, kind="Internal").ap() for i in range(3)]
    cc_in = nc.dram_tensor("cc_in", [2 * T, KV], BF16, kind="Internal")
    cc_out = nc.dram_tensor("cc_out", [2 * R * T, KV], BF16, kind="Internal")

    ident32 = nc.inline_tensor(np.eye(P, dtype=np.float32), name="ident32")
    identbf = nc.inline_tensor(np.eye(P, dtype=ml_dtypes.bfloat16), name="identbf")
    ones_r_bf = nc.inline_tensor(np.ones((1, P), dtype=ml_dtypes.bfloat16), name="ones_r_bf")
    ones_c_bf = nc.inline_tensor(np.ones((P, 1), dtype=ml_dtypes.bfloat16), name="ones_c_bf")
    ones_r_32 = nc.inline_tensor(np.ones((1, P), dtype=np.float32), name="ones_r_32")
    neg1_r_bf = nc.inline_tensor(np.full((1, P), -1.0, dtype=ml_dtypes.bfloat16), name="neg1_r_bf")
    neg1_r_32 = nc.inline_tensor(np.full((1, P), -1.0, dtype=np.float32), name="neg1_r_32")

    WSL = max(DC * P, KEYS)
    HMX = max(NH, max(b - a for a, b in cfg.isplits))

    es = ExitStack()
    with es:
        xln = es.enter_context(nc.sbuf_tensor("xln", [P, DC * T], BF16))
        big = es.enter_context(nc.sbuf_tensor("big", [P, HMX * T], BF16))
        aw = es.enter_context(nc.sbuf_tensor("aw", [P, 4 * WSL], BF16))
        a32 = es.enter_context(nc.sbuf_tensor("a32", [P, 5 * T], F32))
        AMB = max(DC * P, 8 * TH)
        amisc = es.enter_context(nc.sbuf_tensor("amisc", [P, AMB + 2 * T], BF16))
        rows_mu = es.enter_context(nc.sbuf_tensor("rows_mu", [1, T], F32))
        rowbf = es.enter_context(nc.sbuf_tensor("rowbf", [1, 8 * TH], BF16))
        m4sb = es.enter_context(nc.sbuf_tensor("m4sb", [P, 8], F32))
        seff = es.enter_context(nc.sbuf_tensor("seff", [P, 2], F32))
        scv = es.enter_context(nc.sbuf_tensor("scv", [P, 2 * NH + 2 * NKV], F32))
        idt32 = es.enter_context(nc.sbuf_tensor("idt32", [P, P], F32))
        idtbf = es.enter_context(nc.sbuf_tensor("idtbf", [P, P], BF16))
        o_r_bf = es.enter_context(nc.sbuf_tensor("o_r_bf", [1, P], BF16))
        o_c_bf = es.enter_context(nc.sbuf_tensor("o_c_bf", [P, 1], BF16))
        o_r_32 = es.enter_context(nc.sbuf_tensor("o_r_32", [1, P], F32))
        n1_r_bf = es.enter_context(nc.sbuf_tensor("n1_r_bf", [1, P], BF16))
        n1_r_32 = es.enter_context(nc.sbuf_tensor("n1_r_32", [1, P], F32))
        nbias = es.enter_context(nc.sbuf_tensor("nbias", [P, 2], F32))
        cnst = es.enter_context(nc.sbuf_tensor("cnst", [1, 2], F32))

        pb = [es.enter_context(nc.psum_tensor(f"pb{i}", [P, TH], F32)) for i in range(7)]
        pbf = es.enter_context(nc.psum_tensor("pbf", [P, 2 * TH], BF16))
        pbf32 = pbf[:, :].bitcast(F32)

        _pools = {
            "gdma": [es.enter_context(nc.semaphore(f"spg{i}")) for i in range(3)],
            "sdma": [es.enter_context(nc.semaphore(f"sps{i}")) for i in range(6)],
            "eng": [es.enter_context(nc.semaphore(f"spe{i}")) for i in range(16)],
        }
        _pool_ps = {}
        _pool_i = {"gdma": 0, "sdma": 0, "eng": 0}

        def S(name):
            if name.endswith("wd"):
                kind = "gdma"
            elif name.endswith(("ld", "st", "kv")):
                kind = "sdma"
            else:
                kind = "eng"
            pool = _pools[kind]
            i = _pool_i[kind] % len(pool)
            _pool_i[kind] += 1
            h = pool[i]
            prev = _pool_ps.get((kind, i))
            start = prev.n if prev is not None else 0
            ps = PSem(h, start)
            _pool_ps[(kind, i)] = ps
            return ps

        def wslot(i, width=None):
            width = WSL if width is None else width
            return aw[:, i * WSL:i * WSL + width]

        def s32(i, th=None):
            if th is None:
                return a32[:, i * T:(i + 1) * T]
            return a32[:, i * T + th * TH: i * T + (th + 1) * TH]

        rows_sc = a32[0:1, 1 * T:2 * T]
        rows_rs = a32[0:1, 2 * T:3 * T]

        def xlnc(c, th=None):
            if th is None:
                return xln[:, c * T:(c + 1) * T]
            return xln[:, c * T + th * TH: c * T + th * TH + TH]

        def bigc(c, th=None):
            if th is None:
                return big[:, c * T:(c + 1) * T]
            return big[:, c * T + th * TH: c * T + th * TH + TH]

        absb = amisc[:, 0:DC * P]
        kst = amisc[:, AMB:AMB + T]
        vst = amisc[:, AMB + T:AMB + 2 * T]
        vst2 = kst

        def sqv(i):
            return amisc[:, i * T:(i + 1) * T]

        def psb_tile(i):
            return amisc[:, AMB + i * TH:AMB + (i + 1) * TH]

        def ring(u):
            i = u % 8
            return amisc[:, i * TH:(i + 1) * TH]

        def sum_row(th, which, banks=None):
            banks = banks if banks is not None else (4, 5, 6, 3)
            bank = pb[banks[{(0, 0): 0, (0, 1): 1, (1, 0): 2, (1, 1): 3}[(th, which)]]]
            return bank[0:1, :]

        with nc.Block("init") as blk:
            s_ld = S("i_ld")

            @blk.sync
            def _(sync):
                for dst, src in [
                    (idt32[:, :], ident32.ap()), (idtbf[:, :], identbf.ap()),
                    (o_r_bf[:, :], ones_r_bf.ap()), (o_c_bf[:, :], ones_c_bf.ap()),
                    (o_r_32[:, :], ones_r_32.ap()),
                    (n1_r_bf[:, :], neg1_r_bf.ap()),
                    (n1_r_32[:, :], neg1_r_32.ap()),
                    (scv[:, 0:NH], qsc), (scv[:, NH:NH + NKV], ksc),
                    (scv[:, NH + NKV:NH + 2 * NKV], vsc),
                    (scv[:, NH + 2 * NKV:2 * NH + 2 * NKV], osc),
                ]:
                    s_ld.inc(sync.dma_start(out=dst, in_=src), 16)
                sync.wait_ge(s_ld.h, s_ld.at(16 * 9))

            @blk.vector
            def _(vector):
                vector.memset(cnst[0:1, 0:1], EPS)
                vector.memset(cnst[0:1, 1:2], -0.5)

        def ln_phase(name, srcs, write_to):
            ns = len(srcs)
            with nc.Block(name) as blk:
                s_ld, s_st, s_ve, s_ac, s_mm = (S(f"{name}ld"), S(f"{name}st"), S(f"{name}ve"),
                                                S(f"{name}ac"), S(f"{name}mm"))
                vp = max(ns - 1, 0)
                xs = 4 if ns > 1 else 0

                @blk.sync
                def _(sync):
                    for c in range(DC):
                        if c >= 1:
                            sync.wait_ge(s_ac.h, s_ac.at(2 * c))
                            if vp:
                                sync.wait_ge(s_ve.h, s_ve.at(vp * c))
                        for j, src in enumerate(srcs):
                            s_ld.inc(sync.dma_start(out=s32(j), in_=src[c * P:(c + 1) * P, :]), 16)
                        if write_to is not None:
                            sync.wait_ge(s_ve.h, s_ve.at(vp * (c + 1)))
                            s_st.inc(sync.dma_start(out=write_to[c * P:(c + 1) * P, :], in_=s32(xs)), 16)
                    if write_to is not None:
                        sync.wait_ge(s_st.h, s_st.at(16 * DC))

                if vp:
                    @blk.vector
                    def _(vector):
                        for c in range(DC):
                            vector.wait_ge(s_ld.h, s_ld.at(16 * ns * (c + 1)))
                            if c >= 1:
                                vector.wait_ge(s_ac.h, s_ac.at(2 * c))
                                if write_to is not None:
                                    vector.wait_ge(s_st.h, s_st.at(16 * c))
                            s_ve.inc(vector.tensor_add(out=s32(xs), in0=s32(0), in1=s32(1)), 1)
                            for j in range(2, ns):
                                s_ve.inc(vector.tensor_add(out=s32(xs), in0=s32(xs), in1=s32(j)), 1)

                @blk.scalar
                def _(scalar):
                    for c in range(DC):
                        if vp:
                            scalar.wait_ge(s_ve.h, s_ve.at(vp * (c + 1)))
                        else:
                            scalar.wait_ge(s_ld.h, s_ld.at(16 * ns * (c + 1)))
                        if c >= 2:
                            scalar.wait_ge(s_mm.h, s_mm.at(4 * (c - 1)))
                        s_ac.inc(scalar.activation(out=xlnc(c), in_=s32(xs), func=AF.Copy), 1)
                        s_ac.inc(scalar.activation(out=sqv(c % 2), in_=s32(xs), func=AF.Square), 1)

                @blk.tensor
                def _(tensor):
                    for c in range(DC):
                        tensor.wait_ge(s_ac.h, s_ac.at(2 * (c + 1)))
                        for th in range(2):
                            s_mm.inc(tensor.matmul(sum_row(th, 0), o_c_bf[:, :], xlnc(c, th),
                                          start=(c == 0), stop=(c == DC - 1)), 1)
                            s_mm.inc(tensor.matmul(sum_row(th, 1), o_c_bf[:, :],
                                          sqv(c % 2)[:, th * TH:th * TH + TH],
                                          start=(c == 0), stop=(c == DC - 1)), 1)

            ln_fin(name + "f")

        def ln_fin(name, banks=None):
            with nc.Block(name) as blk:
                s_ve, s_ac, s_mm = S(f"{name}ve"), S(f"{name}ac"), S(f"{name}mm")

                @blk.vector
                def _(vector):
                    for th in range(2):
                        mu = rows_mu[0:1, th * TH:(th + 1) * TH]
                        sc = rows_sc[0:1, th * TH:(th + 1) * TH]
                        rs = rows_rs[0:1, th * TH:(th + 1) * TH]
                        s_ve.inc(vector.tensor_scalar(out=mu, in0=sum_row(th, 0, banks), scalar1=1.0 / D,
                                             scalar2=None, op0=ALU.mult), 1)
                        s_ve.inc(vector.tensor_scalar(out=sc, in0=sum_row(th, 1, banks), scalar1=1.0 / D,
                                             scalar2=None, op0=ALU.mult), 1)
                        s_ve.inc(vector.tensor_mul(out=rs, in0=mu, in1=mu), 1)
                        s_ve.inc(vector.tensor_sub(out=sc, in0=sc, in1=rs), 1)
                    vector.wait_ge(s_mm.h, s_mm.at(4))
                    for c in range(DC):
                        for th in range(2):
                            mb = pb[th]
                            rb = pb[2 + th]
                            t1 = s32(0, th)
                            s_ve.inc(vector.tensor_sub(out=t1, in0=xlnc(c, th), in1=mb[:, :]), 1)
                            s_ve.inc(vector.tensor_mul(out=xlnc(c, th), in0=t1, in1=rb[:, :]), 1)

                @blk.scalar
                def _(scalar):
                    for th in range(2):
                        scalar.wait_ge(s_ve.h, s_ve.at(4 * (th + 1)))
                        sc = rows_sc[0:1, th * TH:(th + 1) * TH]
                        rs = rows_rs[0:1, th * TH:(th + 1) * TH]
                        s_ac.inc(scalar.activation(out=sc, in_=sc, func=AF.Ln, bias=cnst[0:1, 0:1]), 1)
                        s_ac.inc(scalar.activation(out=rs, in_=sc, func=AF.Exp, scale=cnst[0:1, 1:2]), 1)

                @blk.tensor
                def _(tensor):
                    for th in range(2):
                        tensor.wait_ge(s_ac.h, s_ac.at(2 * (th + 1)))
                        mu = rows_mu[0:1, th * TH:(th + 1) * TH]
                        rs = rows_rs[0:1, th * TH:(th + 1) * TH]
                        s_mm.inc(tensor.matmul(pb[th][:, :], o_r_32[:, :], mu, start=True, stop=True), 1)
                        s_mm.inc(tensor.matmul(pb[2 + th][:, :], o_r_32[:, :], rs, start=True, stop=True), 1)

        def ffn_phase(name, tail=None, tail_srcs=None, tail_out=None):
            for t, (c0, c1) in enumerate(cfg.isplits):
                ncnk = c1 - c0
                with nc.Block(f"{name}g{t}") as blk:
                    s_wd, s_mm, s_ac, s_ve = (S(f"{name}g{t}wd"), S(f"{name}g{t}mm"),
                                              S(f"{name}g{t}ac"), S(f"{name}g{t}ve"))

                    @blk.gpsimd
                    def _(gpsimd):
                        for sx in range(ncnk):
                            if sx >= 2:
                                gpsimd.wait_ge(s_mm.h, s_mm.at(sx - 1))
                            gs = c0 + sx
                            s_wd.inc(gpsimd.dma_start(
                                out=wslot(sx % 2, DC * P).rearrange("p (c j) -> p c j", j=P),
                                in_=wg_t[gs]), 16)
                            s_wd.inc(gpsimd.dma_start(
                                out=wslot(2 + sx % 2, DC * P).rearrange("p (c j) -> p c j", j=P),
                                in_=wu_t[gs]), 16)

                    def gq(sx, th):
                        if sx % 2 == 0:
                            return pb[th][:, :], pb[2 + th][:, :]
                        return (pb[4][:, :], pb[5][:, :]) if th == 0 else (pb[6][:, :], pbf32)

                    @blk.tensor
                    def _(tensor):
                        for sx in range(ncnk):
                            tensor.wait_ge(s_wd.h, s_wd.at(32 * (sx + 1)))
                            if sx >= 2:
                                tensor.wait_ge(s_ac.h, s_ac.at(sx - 1))
                                tensor.wait_ge(s_ve.h, s_ve.at(sx - 1))
                            g = wslot(sx % 2, DC * P)
                            u = wslot(2 + sx % 2, DC * P)
                            for c in range(DC):
                                for th in range(2):
                                    gp, up = gq(sx, th)
                                    tensor.matmul(gp, g[:, c * P:(c + 1) * P], xlnc(c, th),
                                                  start=(c == 0), stop=(c == DC - 1))
                                    m = tensor.matmul(up, u[:, c * P:(c + 1) * P], xlnc(c, th),
                                                      start=(c == 0), stop=(c == DC - 1))
                            s_mm.inc(m, 1)

                    @blk.scalar
                    def _(scalar):
                        for sx in range(ncnk):
                            scalar.wait_ge(s_mm.h, s_mm.at(sx + 1))
                            if sx >= 2:
                                scalar.wait_ge(s_ve.h, s_ve.at(sx - 1))
                            for th in range(2):
                                a = scalar.activation(out=s32(sx % 2, th), in_=gq(sx, th)[0], func=AF.Silu)
                            s_ac.inc(a, 1)

                    @blk.vector
                    def _(vector):
                        for sx in range(ncnk):
                            vector.wait_ge(s_ac.h, s_ac.at(sx + 1))
                            for th in range(2):
                                v = vector.tensor_mul(out=bigc(sx, th), in0=s32(sx % 2, th), in1=gq(sx, th)[1])
                            s_ve.inc(v, 1)

                last = (t == 2) and tail is not None
                with nc.Block(f"{name}d{t}") as blk:
                    s_wd, s_mm, s_ac, s_st = (S(f"{name}d{t}wd"), S(f"{name}d{t}mm"),
                                              S(f"{name}d{t}ac"), S(f"{name}d{t}st"))
                    if last:
                        s_ld2, s_st2, s_ve2 = S(f"{name}dtld"), S(f"{name}dtst"), S(f"{name}dtve")
                        s_ac3, s_mm3 = S(f"{name}dtac"), S(f"{name}dtmm")

                    @blk.gpsimd
                    def _(gpsimd):
                        for o in range(DC):
                            if o >= 2:
                                gpsimd.wait_ge(s_mm.h, s_mm.at(o - 1))
                            s_wd.inc(gpsimd.dma_start(
                                out=wslot(o % 2, ncnk * P).rearrange("p (c j) -> p c j", j=P),
                                in_=wd_t[o][:, c0:c1, :]), 16)

                    def _tail_sums(tensor, c):
                        tensor.wait_ge(s_ac3.h, s_ac3.at(2 * (c + 1)))
                        for th in range(2):
                            s_mm3.inc(tensor.matmul(sum_row(th, 0, (0, 1, 2, 3)), o_c_bf[:, :],
                                                    xlnc(c, th), start=(c == 0), stop=(c == DC - 1)), 1)
                            s_mm3.inc(tensor.matmul(sum_row(th, 1, (0, 1, 2, 3)), o_c_bf[:, :],
                                                    sqv(c % 2)[:, th * TH:th * TH + TH],
                                                    start=(c == 0), stop=(c == DC - 1)), 1)

                    def dq(o, th):
                        if o % 2 == 0:
                            return pb[4 + th][:, :]
                        if last:
                            return pb[6][:, :] if th == 0 else pbf32
                        return pb[th][:, :]

                    @blk.tensor
                    def _(tensor):
                        for o in range(DC):
                            tensor.wait_ge(s_wd.h, s_wd.at(16 * (o + 1)))
                            if o >= 2:
                                tensor.wait_ge(s_ac.h, s_ac.at(o - 1))
                            w = wslot(o % 2, ncnk * P)
                            for c in range(ncnk):
                                for th in range(2):
                                    m = tensor.matmul(dq(o, th), w[:, c * P:(c + 1) * P], bigc(c, th),
                                                      start=(c == 0), stop=(c == ncnk - 1))
                            s_mm.inc(m, 1)
                            if last and tail == "ln" and o >= 2:
                                _tail_sums(tensor, o - 2)
                        if last and tail == "ln":
                            _tail_sums(tensor, DC - 2)
                            _tail_sums(tensor, DC - 1)

                    @blk.scalar
                    def _(scalar):
                        for o in range(DC):
                            scalar.wait_ge(s_mm.h, s_mm.at(o + 1))
                            if not last:
                                if o >= 2:
                                    scalar.wait_ge(s_st.h, s_st.at(16 * (o - 1)))
                            else:
                                if o >= 1:
                                    scalar.wait_ge(s_ve2.h, s_ve2.at(3 * o))
                            for th in range(2):
                                a = scalar.activation(out=s32(2 + o % 2, th), in_=dq(o, th),
                                                      func=AF.Copy, scale=0.5)
                            s_ac.inc(a, 1)
                            if last and tail == "ln":
                                scalar.wait_ge(s_ve2.h, s_ve2.at(3 * (o + 1)))
                                if o >= 2:
                                    scalar.wait_ge(s_mm3.h, s_mm3.at(4 * (o - 1)))
                                s_ac3.inc(scalar.activation(out=xlnc(o), in_=s32(0), func=AF.Copy), 1)
                                s_ac3.inc(scalar.activation(out=sqv(o % 2), in_=s32(0), func=AF.Square), 1)

                    if last:
                        @blk.vector
                        def _(vector):
                            for o in range(DC):
                                vector.wait_ge(s_ld2.h, s_ld2.at(16 * 3 * (o + 1)))
                                vector.wait_ge(s_ac.h, s_ac.at(o + 1))
                                if o >= 1:
                                    vector.wait_ge(s_st2.h, s_st2.at(16 * o))
                                    if tail == "ln":
                                        vector.wait_ge(s_ac3.h, s_ac3.at(2 * o))
                                s_ve2.inc(vector.tensor_add(out=s32(0), in0=s32(0), in1=s32(1)), 1)
                                s_ve2.inc(vector.tensor_add(out=s32(0), in0=s32(0), in1=s32(4)), 1)
                                s_ve2.inc(vector.tensor_add(out=s32(0), in0=s32(0), in1=s32(2 + o % 2)), 1)

                    @blk.sync
                    def _(sync):
                        if not last:
                            for o in range(DC):
                                sync.wait_ge(s_ac.h, s_ac.at(o + 1))
                                s_st.inc(sync.dma_start(out=part[t][o * P:(o + 1) * P, :],
                                                        in_=s32(2 + o % 2)), 16)
                            sync.wait_ge(s_st.h, s_st.at(16 * DC))
                        else:
                            for o in range(DC):
                                if o >= 1:
                                    sync.wait_ge(s_ve2.h, s_ve2.at(3 * o))
                                for j, src in enumerate(tail_srcs):
                                    slot = (0, 1, 4)[j]
                                    s_ld2.inc(sync.dma_start(out=s32(slot),
                                                             in_=src[o * P:(o + 1) * P, :]), 16)
                                sync.wait_ge(s_ve2.h, s_ve2.at(3 * (o + 1)))
                                s_st2.inc(sync.dma_start(out=tail_out[o * P:(o + 1) * P, :],
                                                         in_=s32(0)), 16)
                            sync.wait_ge(s_st2.h, s_st2.at(16 * DC))

        def bitnet_proj(name, w_ap, n_o, scol_off, extra_scale, dest,
                        kv_store=None, v_transpose=False, pre_gpsimd=None):
            with nc.Block(name) as blk:
                s_wd, s_ab, s_sg = S(f"{name}wd"), S(f"{name}ab"), S(f"{name}sg")
                s_sm, s_cl, s_tp = S(f"{name}sm"), S(f"{name}cl"), S(f"{name}tp")
                s_se, s_pm, s_ev = S(f"{name}se"), S(f"{name}pm"), S(f"{name}ev")
                s_st, s_vt, s_vc = S(f"{name}st"), S(f"{name}vt"), S(f"{name}vc")

                @blk.gpsimd
                def _(gpsimd):
                    if pre_gpsimd is not None:
                        pre_gpsimd(gpsimd)
                    for o in range(n_o):
                        if o >= 2:
                            gpsimd.wait_ge(s_pm.h, s_pm.at(o - 1))
                        s_wd.inc(gpsimd.dma_start(
                            out=wslot(o % 2, DC * P).rearrange("p (c j) -> p c j", j=P),
                            in_=w_ap[o]), 16)

                @blk.scalar
                def _(scalar):
                    for o in range(n_o):
                        scalar.wait_ge(s_wd.h, s_wd.at(16 * (o + 1)))
                        if o >= 1:
                            scalar.wait_ge(s_sm.h, s_sm.at(o))
                        w = wslot(o % 2, DC * P)
                        s_ab.inc(scalar.activation(out=absb, in_=w, func=AF.Abs), 1)
                        s_sg.inc(scalar.activation(out=w, in_=w, func=AF.Sign), 1)
                        scalar.wait_ge(s_pm.h, s_pm.at(o + 1))
                        scalar.wait_ge(s_se.h, s_se.at(o + 1))
                        if kv_store is not None and o >= 2:
                            scalar.wait_ge(s_st.h, s_st.at(16 * (o - 1)))
                        sc = seff[:, o % 2:o % 2 + 1]
                        for th in range(2):
                            if kv_store is None:
                                dst = dest(o, th)
                            elif v_transpose:
                                dst = vst[:, th * TH:(th + 1) * TH]
                            else:
                                dst = kst[:, th * TH:(th + 1) * TH]
                            a = scalar.activation(out=dst, in_=pb[th][:, :], func=AF.Copy, scale=sc)
                        s_ev.inc(a, 1)
                        if v_transpose:
                            scalar.wait_ge(s_vt.h, s_vt.at(o + 1))
                            if o >= 1:
                                scalar.wait_ge(s_st.h, s_st.at(16 * o))
                            s_vc.inc(scalar.activation(out=vst2, in_=pbf[:, 0:T], func=AF.Copy), 1)

                @blk.tensor
                def _(tensor):
                    for o in range(n_o):
                        tensor.wait_ge(s_ab.h, s_ab.at(o + 1))
                        if o >= 1:
                            tensor.wait_ge(s_cl.h, s_cl.at(o))
                        for c in range(DC):
                            m = tensor.matmul(pb[6][0:1, 0:P], o_c_bf[:, :], absb[:, c * P:(c + 1) * P],
                                              start=(c == 0), stop=(c == DC - 1))
                        s_sm.inc(m, 1)
                        tensor.wait_ge(s_cl.h, s_cl.at(o + 1))
                        if o >= 1:
                            tensor.wait_ge(s_se.h, s_se.at(o))
                        s_tp.inc(tensor.transpose(pb[6][:, 4:5], rows_sc[0:1, 0:P], idt32[0:1, 0:1]), 1)
                        tensor.wait_ge(s_sg.h, s_sg.at(o + 1))
                        if o >= 1:
                            tensor.wait_ge(s_ev.h, s_ev.at(o))
                        w = wslot(o % 2, DC * P)
                        for c in range(DC):
                            for th in range(2):
                                m = tensor.matmul(pb[th][:, :], w[:, c * P:(c + 1) * P], xlnc(c, th),
                                                  start=(c == 0), stop=(c == DC - 1))
                        s_pm.inc(m, 1)
                        if v_transpose:
                            tensor.wait_ge(s_ev.h, s_ev.at(o + 1))
                            if o >= 1:
                                tensor.wait_ge(s_vc.h, s_vc.at(o))
                            for tt in range(TN):
                                m = tensor.transpose(pbf[:, tt * P:(tt + 1) * P],
                                                     vst[:, tt * P:(tt + 1) * P],
                                                     idtbf[:, :])
                            s_vt.inc(m, 1)

                @blk.vector
                def _(vector):
                    for o in range(n_o):
                        vector.wait_ge(s_sm.h, s_sm.at(o + 1))
                        s_cl.inc(vector.tensor_scalar(out=rows_sc[0:1, 0:P], in0=pb[6][0:1, 0:P],
                                             scalar1=1.0 / D, scalar2=1e-8,
                                             op0=ALU.mult, op1=ALU.max), 1)
                        vector.wait_ge(s_tp.h, s_tp.at(o + 1))
                        if o >= 1:
                            vector.wait_ge(s_ev.h, s_ev.at(o))
                        s_se.inc(vector.scalar_tensor_tensor(out=seff[:, o % 2:o % 2 + 1], in0=pb[6][:, 4:5],
                                                    scalar=extra_scale,
                                                    in1=scv[:, scol_off + o:scol_off + o + 1],
                                                    op0=ALU.mult, op1=ALU.mult), 1)

                if kv_store is not None:
                    @blk.sync
                    def _(sync):
                        for o in range(n_o):
                            if not v_transpose:
                                sync.wait_ge(s_ev.h, s_ev.at(o + 1))
                                s_st.inc(sync.dma_start(out=cc_in.ap()[o * P:(o + 1) * P, :],
                                               in_=kst), 16)
                            else:
                                sync.wait_ge(s_vc.h, s_vc.at(o + 1))
                                s_st.inc(sync.dma_start(
                                    out=cc_in.ap()[T:2 * T, o * P:(o + 1) * P]
                                    .rearrange("(tt p) j -> p tt j", p=P),
                                    in_=vst2.rearrange("p (tt j) -> p tt j", j=P)), 16)
                        sync.wait_ge(s_st.h, s_st.at(16 * n_o))

        def attn_phase(name, s_cc_handle):
            with nc.Block(name) as blk:
                s_kv, s_rk, s_nb = S(f"{name}kv"), S(f"{name}rk"), S(f"{name}nb")
                s_nc = S(f"{name}nc")
                s_nk, s_sq = S(f"{name}nk"), S(f"{name}sq")
                s_sb, s_ex, s_av = S(f"{name}sb"), S(f"{name}ex"), S(f"{name}av")
                s_tn, s_rc, s_ib = S(f"{name}tn"), S(f"{name}rc"), S(f"{name}ib")
                s_iv, s_ao = S(f"{name}iv"), S(f"{name}ao")
                UPH = GR * 2
                NU = NKV * UPH
                NKB = KEYS // TH
                NP = KTN // 2
                NPH = NKB + UPH

                def unit_of(h, uh):
                    u = h * UPH + uh
                    return u, h * GR + uh // 2, uh % 2

                def nrow_bank(i):
                    return (pb[5] if i % 2 == 0 else pb[6])[0:1, :]

                sqK = amisc[:, 0:KEYS]
                sqQ = amisc[:, AMB + T:AMB + 2 * T]
                rkp = a32[0:1, 0:8]
                rk2 = a32[0:1, 8:9]
                qm2 = a32[0:1, 9:10]
                lnr = a32[0:1, 10:11]

                def invb(u):
                    return s32(2, u % 2)

                def mrow_inv(u):
                    return rows_mu[0:1, (u % 2) * TH:(u % 2) * TH + TH]

                @blk.sync
                def _(sync):
                    sync.wait_ge(s_cc_handle.h, s_cc_handle.at(1))
                    for h in range(NKV):
                        if h >= 2:
                            sync.wait_ge(s_tn.h, s_tn.at(UPH * (h - 1)))
                        kslot = wslot(h % 2, KEYS)
                        vslot = wslot(2 + h % 2, KEYS)
                        for r in range(R):
                            s_kv.inc(sync.dma_start(
                                out=kslot[:, r * T:(r + 1) * T],
                                in_=cc_out.ap()[r * 2 * T + h * P: r * 2 * T + h * P + P, :]), 16)
                            s_kv.inc(sync.dma_start(
                                out=vslot.rearrange("p (kt j) -> p kt j", j=P)[:, r * TN:(r + 1) * TN, :],
                                in_=cc_out.ap()[r * 2 * T + T: r * 2 * T + 2 * T, h * P:(h + 1) * P]
                                .rearrange("(kt p) j -> p kt j", p=P)), 16)

                @blk.scalar
                def _(scalar):
                    for h in range(NKV):
                        scalar.wait_ge(s_kv.h, s_kv.at(16 * 2 * R * (h + 1)))
                        if h >= 1:
                            scalar.wait_ge(s_nk.h, s_nk.at(NPH * h - UPH))
                        s_sq.inc(scalar.activation(out=sqK, in_=wslot(h % 2, KEYS), func=AF.Square), 1)
                        for uh in range(UPH):
                            u, g, qt = unit_of(h, uh)
                            if uh % 2 == 0:
                                if u >= 1:
                                    scalar.wait_ge(s_nk.h, s_nk.at(NKB * (h + 1) + UPH * h + uh))
                                s_sq.inc(scalar.activation(out=sqQ[:, 0:T], in_=bigc(g), func=AF.Square), 1)
                            scalar.wait_ge(s_rk.h, s_rk.at((NKB + 1 + 2 * uh + 1) + (NKB + 1 + 2 * UPH) * h))
                            if u >= 1:
                                scalar.wait_ge(s_nb.h, s_nb.at(2 * u))
                            scalar.activation(out=lnr, in_=qm2, func=AF.Ln, scale=rk2)
                            s_nb.inc(scalar.activation(out=lnr, in_=lnr, func=AF.Exp, scale=0.5), 1)
                            scalar.wait_ge(s_nb.h, s_nb.at(2 * u + 2))
                            s_nc.inc(scalar.activation(out=nbias[:, u % 2:u % 2 + 1],
                                                       in_=pbf32[:, 0:1], func=AF.Copy), 1)
                        psB = [pb[2], pb[3], pb[6]]
                        for uh in range(UPH):
                            u, g, qt = unit_of(h, uh)
                            for kt in range(KTN):
                                gi = u * KTN + kt
                                scalar.wait_ge(s_sb.h, s_sb.at(gi + 1))
                                scalar.wait_ge(s_nc.h, s_nc.at(u + 1))
                                if gi >= 3:
                                    scalar.wait_ge(s_av.h, s_av.at(gi - 2))
                                s_ex.inc(scalar.activation(out=psb_tile(kt % 3), in_=psB[kt % 3][:, :],
                                                           func=AF.Exp,
                                                           bias=nbias[:, u % 2:u % 2 + 1]), 1)
                            scalar.wait_ge(s_ib.h, s_ib.at(u + 1))
                            if u >= 2:
                                scalar.wait_ge(s_ao.h, s_ao.at(u - 1))
                            s_iv.inc(scalar.activation(out=invb(u), in_=pb[5][:, :],
                                                       func=AF.Copy), 1)

                @blk.tensor
                def _(tensor):
                    for h in range(NKV):
                        kslot = wslot(h % 2, KEYS)
                        vslot = wslot(2 + h % 2, KEYS)
                        tensor.wait_ge(s_sq.h, s_sq.at((1 + GR) * h + 1))
                        if h >= 1:
                            tensor.wait_ge(s_iv.h, s_iv.at(UPH * h))
                        for kb in range(NKB):
                            if kb >= 2:
                                tensor.wait_ge(s_rk.h, s_rk.at((NKB + 1 + 2 * UPH) * h + kb - 1))
                            s_nk.inc(tensor.matmul(nrow_bank(kb), o_c_bf[:, :],
                                                   sqK[:, kb * TH:(kb + 1) * TH],
                                                   start=True, stop=True), 1)
                        for uh in range(UPH):
                            u, g, qt = unit_of(h, uh)
                            if uh % 2 == 0:
                                tensor.wait_ge(s_sq.h, s_sq.at((1 + GR) * h + 2 + uh // 2))
                            if uh < 2:
                                tensor.wait_ge(s_rk.h, s_rk.at((NKB + 1 + 2 * UPH) * h + NKB - 1 + uh))
                            else:
                                tensor.wait_ge(s_rk.h, s_rk.at((NKB + 1 + 2 * UPH) * h + NKB + 1 + 2 * (uh - 2) + 2))
                            s_nk.inc(tensor.matmul(nrow_bank(NKB + uh), o_c_bf[:, :],
                                                   sqQ[:, qt * TH:(qt + 1) * TH],
                                                   start=True, stop=True), 1)
                            tensor.wait_ge(s_nb.h, s_nb.at(2 * u + 1))
                            if u >= 1:
                                tensor.wait_ge(s_nc.h, s_nc.at(u))
                            m = tensor.matmul(pbf32[:, 0:1], n1_r_32[:, :], lnr,
                                              start=True, stop=True)
                            s_nb.inc(m, 1)
                        psB = [pb[2], pb[3], pb[6]]
                        for uh in range(UPH):
                            u, g, qt = unit_of(h, uh)
                            rhs_q = bigc(g, qt)
                            for kt in range(KTN):
                                gi = u * KTN + kt
                                if gi >= 3:
                                    tensor.wait_ge(s_ex.h, s_ex.at(gi - 2))
                                s_sb.inc(tensor.matmul(psB[kt % 3][:, :], kslot[:, kt * P:(kt + 1) * P],
                                                       rhs_q, start=True, stop=True), 1)
                                if kt >= 2:
                                    kk = kt - 2
                                    tensor.wait_ge(s_ex.h, s_ex.at(gi - 1))
                                    if kt == 2:
                                        tensor.wait_ge(s_ao.h, s_ao.at(u))
                                        tensor.wait_ge(s_iv.h, s_iv.at(u))
                                    s_av.inc(tensor.matmul(pb[4][:, :], vslot[:, kk * P:(kk + 1) * P],
                                                           psb_tile(kk % 3), start=(kk == 0),
                                                           stop=False), 1)
                                    tensor.matmul(pb[5][0:1, :], o_c_bf[:, :], psb_tile(kk % 3),
                                                  start=(kk == 0), stop=False)
                            for kk in (KTN - 2, KTN - 1):
                                tensor.wait_ge(s_ex.h, s_ex.at(u * KTN + kk + 1))
                                s_av.inc(tensor.matmul(pb[4][:, :], vslot[:, kk * P:(kk + 1) * P],
                                                       psb_tile(kk % 3), start=False,
                                                       stop=(kk == KTN - 1)), 1)
                                m = tensor.matmul(pb[5][0:1, :], o_c_bf[:, :], psb_tile(kk % 3),
                                                  start=False, stop=(kk == KTN - 1))
                            s_tn.inc(m, 1)
                            tensor.wait_ge(s_rc.h, s_rc.at(u + 1))
                            s_ib.inc(tensor.matmul(pb[5][:, :], o_r_32[:, :],
                                                   mrow_inv(u), start=True, stop=True), 1)

                @blk.vector
                def _(vector):
                    for h in range(NKV):
                        for kb in range(NKB):
                            vector.wait_ge(s_nk.h, s_nk.at(NPH * h + kb + 1))
                            s_rk.inc(vector.reduce_max(out=rkp[0:1, kb:kb + 1],
                                                       in_=nrow_bank(kb), axis=AX.X), 1)
                        s_rk.inc(vector.reduce_max(out=rk2, in_=rkp, axis=AX.X), 1)
                        for uh in range(UPH):
                            u, g, qt = unit_of(h, uh)
                            vector.wait_ge(s_nk.h, s_nk.at(NPH * h + NKB + uh + 1))
                            s_rk.inc(vector.reduce_max(out=qm2, in_=nrow_bank(NKB + uh),
                                                       axis=AX.X), 1)
                            vector.wait_ge(s_nb.h, s_nb.at(2 * u + 1))
                            s_rk.inc(vector.tensor_copy(out=rkp[0:1, 7:8], in_=qm2), 1)
                        for uh in range(UPH):
                            u, g, qt = unit_of(h, uh)
                            vector.wait_ge(s_tn.h, s_tn.at(u + 1))
                            if u >= 1:
                                vector.wait_ge(s_ib.h, s_ib.at(u - 1))
                            s_rc.inc(vector.reciprocal(out=mrow_inv(u), in_=pb[5][0:1, :]), 1)
                            vector.wait_ge(s_iv.h, s_iv.at(u + 1))
                            s_ao.inc(vector.tensor_mul(out=bigc(g, qt), in0=pb[4][:, :],
                                                       in1=invb(u)), 1)

        def oproj_phase(name):
            with nc.Block(name) as blk:
                s_wd, s_ab, s_sg = S(f"{name}wd"), S(f"{name}ab"), S(f"{name}sg")
                s_sm, s_cl, s_tp = S(f"{name}sm"), S(f"{name}cl"), S(f"{name}tp")
                s_se, s_pm, s_ev = S(f"{name}se"), S(f"{name}pm"), S(f"{name}ev")
                s_ld, s_ve, s_st = S(f"{name}ld"), S(f"{name}ve"), S(f"{name}st")
                s_ac2, s_mm2 = S(f"{name}ac2"), S(f"{name}mm2")

                @blk.gpsimd
                def _(gpsimd):
                    for o in range(DC):
                        if o >= 2:
                            gpsimd.wait_ge(s_pm.h, s_pm.at(o - 1))
                        s_wd.inc(gpsimd.dma_start(
                            out=wslot(o % 2, DC * P).rearrange("p (c j) -> p c j", j=P),
                            in_=wo_t[o]), 16)

                @blk.scalar
                def _(scalar):
                    for o in range(DC):
                        scalar.wait_ge(s_wd.h, s_wd.at(16 * (o + 1)))
                        if o >= 1:
                            scalar.wait_ge(s_sm.h, s_sm.at(o))
                        w = wslot(o % 2, DC * P)
                        s_ab.inc(scalar.activation(out=absb, in_=w, func=AF.Abs), 1)
                        s_sg.inc(scalar.activation(out=w, in_=w, func=AF.Sign), 1)
                        scalar.wait_ge(s_pm.h, s_pm.at(o + 1))
                        scalar.wait_ge(s_se.h, s_se.at(o + 1))
                        if o >= 1:
                            scalar.wait_ge(s_ve.h, s_ve.at(2 * o))
                        sc = seff[:, o % 2:o % 2 + 1]
                        for th in range(2):
                            a = scalar.activation(out=s32(2, th), in_=pb[th][:, :], func=AF.Copy, scale=sc)
                        s_ev.inc(a, 1)
                        scalar.wait_ge(s_ve.h, s_ve.at(2 * (o + 1)))
                        if o >= 2:
                            scalar.wait_ge(s_mm2.h, s_mm2.at(4 * (o - 1)))
                        s_ac2.inc(scalar.activation(out=xlnc(o), in_=s32(4), func=AF.Copy), 1)
                        s_ac2.inc(scalar.activation(out=(kst if o % 2 == 0 else vst),
                                                    in_=s32(4), func=AF.Square), 1)

                @blk.tensor
                def _(tensor):
                    for o in range(DC):
                        tensor.wait_ge(s_ab.h, s_ab.at(o + 1))
                        if o >= 1:
                            tensor.wait_ge(s_cl.h, s_cl.at(o))
                        for c in range(DC):
                            m = tensor.matmul(pb[2][0:1, 0:P], o_c_bf[:, :], absb[:, c * P:(c + 1) * P],
                                              start=(c == 0), stop=(c == DC - 1))
                        s_sm.inc(m, 1)
                        tensor.wait_ge(s_cl.h, s_cl.at(o + 1))
                        if o >= 1:
                            tensor.wait_ge(s_se.h, s_se.at(o))
                        s_tp.inc(tensor.transpose(pb[2][:, 4:5], rows_sc[0:1, 0:P], idt32[0:1, 0:1]), 1)
                        tensor.wait_ge(s_sg.h, s_sg.at(o + 1))
                        if o >= 1:
                            tensor.wait_ge(s_ev.h, s_ev.at(o))
                        w = wslot(o % 2, DC * P)
                        for c in range(DC):
                            for th in range(2):
                                m = tensor.matmul(pb[th][:, :], w[:, c * P:(c + 1) * P], bigc(c, th),
                                                  start=(c == 0), stop=(c == DC - 1))
                        s_pm.inc(m, 1)
                        tensor.wait_ge(s_ac2.h, s_ac2.at(2 * (o + 1)))
                        sq_t = kst if o % 2 == 0 else vst
                        for th in range(2):
                            s_mm2.inc(tensor.matmul(sum_row(th, 0), o_c_bf[:, :], xlnc(o, th),
                                          start=(o == 0), stop=(o == DC - 1)), 1)
                            s_mm2.inc(tensor.matmul(sum_row(th, 1), o_c_bf[:, :],
                                          sq_t[:, th * TH:th * TH + TH],
                                          start=(o == 0), stop=(o == DC - 1)), 1)

                @blk.vector
                def _(vector):
                    for o in range(DC):
                        vector.wait_ge(s_sm.h, s_sm.at(o + 1))
                        s_cl.inc(vector.tensor_scalar(out=rows_sc[0:1, 0:P], in0=pb[2][0:1, 0:P],
                                             scalar1=1.0 / D, scalar2=1e-8,
                                             op0=ALU.mult, op1=ALU.max), 1)
                        vector.wait_ge(s_tp.h, s_tp.at(o + 1))
                        if o >= 1:
                            vector.wait_ge(s_ev.h, s_ev.at(o))
                        s_se.inc(vector.scalar_tensor_tensor(out=seff[:, o % 2:o % 2 + 1], in0=pb[2][:, 4:5],
                                                    scalar=1.0,
                                                    in1=scv[:, NH + 2 * NKV + o:NH + 2 * NKV + o + 1],
                                                    op0=ALU.mult, op1=ALU.mult), 1)
                        vector.wait_ge(s_ev.h, s_ev.at(o + 1))
                        vector.wait_ge(s_ld.h, s_ld.at(16 * (o + 1)))
                        if o >= 1:
                            vector.wait_ge(s_st.h, s_st.at(16 * o))
                            vector.wait_ge(s_ac2.h, s_ac2.at(2 * o))
                        s_ve.inc(vector.tensor_add(out=s32(4), in0=s32(2), in1=s32(3)), 2)

                @blk.sync
                def _(sync):
                    for o in range(DC):
                        if o >= 1:
                            sync.wait_ge(s_ve.h, s_ve.at(2 * o))
                        s_ld.inc(sync.dma_start(out=s32(3), in_=xc2[o * P:(o + 1) * P, :]), 16)
                        sync.wait_ge(s_ve.h, s_ve.at(2 * (o + 1)))
                        s_st.inc(sync.dma_start(out=xc3[o * P:(o + 1) * P, :], in_=s32(4)), 16)
                    sync.wait_ge(s_st.h, s_st.at(16 * DC))

            ln_fin(name + "f")

        ln_phase("ln1", [xT], None)
        ffn_phase("f1", tail="ln", tail_srcs=[xT, part[0], part[1]], tail_out=xc2)
        ln_fin("ln2f", banks=(0, 1, 2, 3))

        s_cc = PSem(es.enter_context(nc.semaphore("cc")), 0)

        bitnet_proj("kp", wk_t, NKV, NH, 1.0, None, kv_store="k")
        bitnet_proj("vp", wv_t, NKV, NH + NKV, 1.0, None, kv_store="v", v_transpose=True)

        def q_pre(gpsimd):
            s_cc.inc(gpsimd.collective_compute(
                "AllGather", ALU.bypass,
                replica_groups=[[0, 1, 2, 3], [4, 5, 6, 7]],
                ins=[cc_in.ap().opt()],
                outs=[cc_out.ap().opt()],
            ), 1)

        bitnet_proj("qp", wq_t, NH, 0, 1.0 / math.sqrt(P), lambda o, th: bigc(o, th),
                    pre_gpsimd=q_pre)
        attn_phase("at", s_cc)
        oproj_phase("op")
        ffn_phase("f2", tail="fin", tail_srcs=[xc3, part[0], part[1]], tail_out=yT)

        dbg = {}
        dbg_xln = None
        if debug:
            for nm, src, shp, dt in [
                ("dbg_xc2", xc2, [D, T], F32), ("dbg_xc3", xc3, [D, T], F32),
                ("dbg_p0", part[0], [D, T], F32), ("dbg_p1", part[1], [D, T], F32),
                ("dbg_p2", part[2], [D, T], F32),
                ("dbg_ccin", cc_in.ap(), [2 * T, KV], BF16),
                ("dbg_ccout", cc_out.ap(), [2 * R * T, KV], BF16),
            ]:
                dbg[nm] = (nc.dram_tensor(nm, shp, dt, kind="ExternalOutput").ap(), src)
            dbg_xln = nc.dram_tensor("dbg_xln", [P, DC * T], BF16, kind="ExternalOutput").ap()
            with nc.Block("dbg") as blk:
                s_dst = S("dbgst")

                @blk.sync
                def _(sync):
                    n = 0
                    for nm, (dst, src) in dbg.items():
                        s_dst.inc(sync.dma_start(out=dst, in_=src), 16)
                        n += 1
                    s_dst.inc(sync.dma_start(out=dbg_xln, in_=xln[:, :]), 16)
                    sync.wait_ge(s_dst.h, s_dst.at(16 * (n + 1)))

    return nc


def _tile_T(w):
    O, IC_ = w.shape
    return np.ascontiguousarray(w.T.reshape(IC_ // P, P, O // P, P).transpose(2, 1, 0, 3))


def prep_inputs(cfg: Cfg, inp):
    T, I, R = cfg.T, cfg.I, cfg.R
    x = inp["L0_attn_hidden_states"]
    wgu = inp["L0_ffn_w_gate_up"]

    shared = {
        "wg_t": _tile_T(wgu[:I]),
        "wu_t": _tile_T(wgu[I:]),
        "wd_t": _tile_T(inp["L0_ffn_w_down"]),
        "wq_t": _tile_T(inp["L0_attn_q_weight"]),
        "wk_t": _tile_T(inp["L0_attn_k_weight"]),
        "wv_t": _tile_T(inp["L0_attn_v_weight"]),
        "wo_t": _tile_T(inp["L0_attn_o_weight"]),
        "qsc": np.ascontiguousarray(inp["L0_attn_q_scale"].reshape(-1, P).T),
        "ksc": np.ascontiguousarray(inp["L0_attn_k_scale"].reshape(-1, P).T),
        "vsc": np.ascontiguousarray(inp["L0_attn_v_scale"].reshape(-1, P).T),
        "osc": np.ascontiguousarray(inp["L0_attn_o_scale"].reshape(-1, P).T),
    }
    in_maps = []
    for c in range(8):
        b, q = c // R, c % R
        m = dict(shared)
        m["xT"] = np.ascontiguousarray(x[b, q * T:(q + 1) * T, :].T)
        in_maps.append(m)
    return in_maps


def kernel(**inputs):
    cfg = CFG
    in_maps = prep_inputs(cfg, inputs)
    nc = build_program(cfg)
    res = run_bass_kernel_spmd(nc, in_maps, core_ids=list(range(8)))
    B = 2
    y = np.empty((B, cfg.R * cfg.T, cfg.D), np.float32)
    for c in range(8):
        b, q = c // cfg.R, c % cfg.R
        y[b, q * cfg.T:(q + 1) * cfg.T, :] = res.results[c]["yT"].T
    return y
